# version 19
# baseline (speedup 1.0000x reference)
import math
import numpy as np
import ml_dtypes
from contextlib import ExitStack

import concourse.bass as bass
import concourse.mybir as mybir
from concourse.bass_utils import run_bass_kernel_spmd

F32 = mybir.dt.float32
BF16 = mybir.dt.bfloat16
AF = mybir.ActivationFunctionType
ALU = mybir.AluOpType
AX = mybir.AxisListType

STRICT = False

P = 128
S = 2048
D = 1024
NT = S // P
NSEQ = 2
NCORE = 8
LN_EPS = 1e-5

FUSE = [(0, 1, 2, 3)]
DEBUG_B0 = False
STOP_AFTER = None
OP1_OPS = 99

ENG_SEM = {"tensor": "pe", "scalar": "act", "vector": "dve", "gpsimd": "gp",
           "sync": None}


class Prog:
    def __init__(self, nc, cur, sems):
        self.nc = nc
        self.cur = cur
        self.sems = sems
        self.c = {k: 0 for k in sems}
        self.waited = {}
        self.ready = {}
        self.eng = {"tensor": nc.tensor, "scalar": nc.scalar, "vector": nc.vector,
                    "gpsimd": nc.gpsimd, "sync": nc.sync}

    def _wait(self, e, wl):
        for s, v in wl:
            if v <= 0:
                continue
            if s in ("dsp", "dgp"):
                v = self.c[s]
            if self.waited.get(s, -1) >= v:
                continue
            e.wait_ge(self.sems[s], v)
            self.waited[s] = v

    def need(self, *tags):
        out = []
        for t in tags:
            if t in self.ready:
                out.append(self.ready[t])
        return out

    def req(self, *tags):
        for t in tags:
            if t not in self.ready:
                raise KeyError(f"mark {t} not set")
        return [self.ready[t] for t in tags]

    def mark(self, tag, sem=None):
        sem = sem or ENG_SEM.get(self.cur)
        self.ready[tag] = (sem, self.c[sem])

    def do(self, engname, fn, wait=(), inc=None, n=1):
        own = ENG_SEM[engname]
        if engname in ("scalar", "vector", "gpsimd"):
            wait = list(wait) + [(own, self.c[own])]
            if inc is None:
                inc = own
        elif engname == "tensor":
            if inc is None:
                inc = own
            if STRICT:
                wait = list(wait) + [(own, self.c[own])]
        if engname == self.cur:
            e = self.eng[engname]
            self._wait(e, wait)
            ins = fn(e)
            if inc is not None and ins is not None:
                ins.then_inc(self.sems[inc], n)
        if inc is not None:
            self.c[inc] += n

    def bkr(self, b0, nb, sem):
        for b in range(b0, b0 + nb):
            self.mark(f"bkR{b}", sem)

    def bkw(self, b0, nb):
        out = []
        for b in range(b0, b0 + nb):
            out += self.need(f"bkR{b}")
        return out

    def mm(self, out, lhsT, rhs, start, stop, wait=(), inc=None, lowp=False):
        def f(t):
            if lowp:
                with self.nc.allow_low_precision(reason="exact small-int matmul"):
                    return t.matmul(out, lhsT, rhs, start=start, stop=stop)
            return t.matmul(out, lhsT, rhs, start=start, stop=stop)
        self.do("tensor", f, wait=wait, inc=inc)

    def transp(self, out, in_, ident, wait=(), inc=None):
        def f(t):
            with self.nc.allow_low_precision(reason="bf16 transpose"):
                return t.transpose(out, in_, ident)
        self.do("tensor", f, wait=wait, inc=inc)

    def act(self, out, in_, func, bias=0.0, scale=1.0, accum_out=None, wait=(), inc=None):
        self.do("scalar", lambda s: s.activation(out, in_, func, bias=bias, scale=scale,
                                                 accum_out=accum_out),
                wait=wait, inc=inc)

    def dma(self, q, out, in_, wait=(), inc=None):
        inc = inc or ("dsp" if q == "sync" else "dgp")
        chain = [(inc, self.c[inc])]
        self.do(q, lambda e: e.dma_start(out=out, in_=in_), wait=list(wait) + chain,
                inc=inc, n=16)

    def tt(self, eng, out, a, b, op, wait=(), inc=None):
        self.do(eng, lambda v: v.tensor_tensor(out, a, b, op), wait=wait, inc=inc)

    def ttb(self, eng, out, a, scol, op, wait=(), inc=None):
        s_b, _ = bass.broadcast_tensor_aps(scol, a)
        self.do(eng, lambda v: v.tensor_tensor(out, a, s_b, op),
                wait=wait, inc=inc)

    def ts(self, eng, out, in0, s1, s2, op0, op1=None, accum_out=None, wait=(), inc=None):
        if op1 is None:
            self.do(eng, lambda v: v.tensor_scalar(out, in0, s1, None, op0,
                                                   accum_out=accum_out), wait=wait, inc=inc)
        else:
            self.do(eng, lambda v: v.tensor_scalar(out, in0, s1, s2, op0, op1,
                                                   accum_out=accum_out), wait=wait, inc=inc)

    def stt(self, eng, out, in0, scalar, in1, op0, op1, accum_out=None, wait=(), inc=None):
        self.do(eng, lambda v: v.scalar_tensor_tensor(out, in0, scalar, in1, op0, op1,
                                                      accum_out=accum_out),
                wait=wait, inc=inc)

    def copy(self, eng, out, in_, wait=(), inc=None):
        if eng == "scalar":
            self.do(eng, lambda s: s.copy(out, in_), wait=wait, inc=inc)
        else:
            self.do(eng, lambda v: v.tensor_copy(out, in_), wait=wait, inc=inc)

    def red(self, out, in_, op, negate=False, wait=(), inc=None, eng="vector"):
        self.do(eng, lambda v: v.tensor_reduce(out, in_, AX.X, op, negate=negate),
                wait=wait, inc=inc)

    def recip(self, out, in_, wait=(), inc=None):
        def f(v):
            with self.nc.allow_low_precision(reason="recip"):
                return v.reciprocal(out, in_)
        self.do("vector", f, wait=wait, inc=inc)

    def memset(self, eng, ap, val, wait=(), inc=None):
        self.do(eng, lambda e: e.memset(ap, val), wait=wait, inc=inc)

    def nop(self, eng, wait=(), inc=None):
        self.do(eng, lambda e: e.nop(), wait=wait, inc=inc)


class Arena:

    def __init__(self, ap, cap):
        self.ap = ap
        self.cap = cap
        self.live = {}

    def _gaps(self):
        spans = sorted(self.live.values())
        gaps = []
        pos = 0
        for o, c in spans:
            if o > pos:
                gaps.append((pos, o - pos))
            pos = max(pos, o + c)
        if pos < self.cap:
            gaps.append((pos, self.cap - pos))
        return gaps

    def alloc(self, tag, cols, align=64, top=None):
        assert tag not in self.live, tag
        cols = (cols + align - 1) // align * align
        if top is None:
            top = cols >= 8192
        gaps = self._gaps()
        it = reversed(gaps) if top else iter(gaps)
        for off, ln in it:
            if top:
                o = (off + ln - cols) // align * align
                if o < off:
                    continue
            else:
                o = (off + align - 1) // align * align
                if o + cols > off + ln:
                    continue
            self.live[tag] = (o, cols)
            return self.ap[:, o:o + cols]
        raise MemoryError(f"arena OOM for {tag} ({cols}); gaps={gaps}")

    def alloc_at(self, tag, off, cols):
        assert tag not in self.live, tag
        assert off % 64 == 0 and off + cols <= self.cap, (tag, off, cols)
        for t, (o, c) in self.live.items():
            assert off + cols <= o or o + c <= off, \
                f"overlap {tag}@[{off},{off+cols}) with {t}@[{o},{o+c})"
        self.live[tag] = (off, cols)
        return self.ap[:, off:off + cols]

    def free_(self, *tags):
        for tag in tags:
            self.live.pop(tag)

    def reset(self):
        self.live = {}


def a3(ap, a, b):
    return ap.rearrange("p (a b) -> p a b", a=a)



def host_prep(inp):
    C = lambda x: np.ascontiguousarray(x, dtype=np.float32)
    d = {}
    d["l0_wq"] = C(inp["L0_attn_wq"])
    d["l0_bq"] = C(inp["L0_attn_bq"].reshape(2, 128).T)
    d["l0_wqc"] = C(inp["L0_attn_wqc"])
    d["l0_bqc"] = C(inp["L0_attn_bqc"].reshape(7, 128).T)
    d["l0_wqr"] = C(inp["L0_attn_wqr"])
    d["l0_bqr"] = C(inp["L0_attn_bqr"].reshape(1, 128))
    d["l0_wkv"] = C(inp["L0_attn_wkv"])
    d["l0_bkv"] = C(inp["L0_attn_bkv"].reshape(2, 128).T)
    d["l0_wkr"] = C(inp["L0_attn_wkr"])
    d["l0_bkr"] = C(inp["L0_attn_bkr"].reshape(1, 128))
    d["l0_wkc"] = C(inp["L0_attn_wkc"])
    d["l0_bkc"] = C(inp["L0_attn_bkc"].reshape(7, 128).T)
    d["l0_wv"] = C(inp["L0_attn_wv"])
    d["l0_bv"] = C(inp["L0_attn_bv"].reshape(1, 1024))
    d["l0_wo"] = C(inp["L0_attn_wo"])
    d["l0_bo"] = C(inp["L0_attn_bo"].reshape(1, 1024))
    d["l0f_wfcT"] = C(inp["L0_ffn_W_fc"].T)
    d["l0f_w2T"] = C(inp["L0_ffn_W_proj"].T)
    d["l0f_lnw"] = C(inp["L0_ffn_ln_w"].reshape(32, 128).T)
    d["l0f_lnb"] = C(inp["L0_ffn_ln_b"].reshape(32, 128).T)
    wqkv = inp["L1_attn_w_qkv"]
    d["l1_wq"] = C(wqkv[:, 0:1024])
    d["l1_wk"] = C(wqkv[:, 1024:2048])
    d["l1_wv"] = C(wqkv[:, 2048:3072])
    d["l1_wo"] = C(inp["L1_attn_w_out"])
    d["l1_bo"] = C(inp["L1_attn_b_out"].reshape(1, 1024))
    d["l1_mask"] = C(inp["L1_attn_mask_logits"].reshape(33, 1))
    for t in ("q", "k", "v"):
        d[f"l1f_{t}_w1"] = C(inp[f"L1_ffn_{t}_w1"].transpose(1, 0, 2).reshape(1024, 1024))
        d[f"l1f_{t}_w2"] = C(inp[f"L1_ffn_{t}_w2"].reshape(1024, 1024))
        d[f"l1f_{t}_nw"] = C(inp[f"L1_ffn_{t}_nw" if False else f"L1_ffn_{t}_node_w"].T)
        d[f"l1f_{t}_nb"] = C(np.broadcast_to(
            inp[f"L1_ffn_{t}_node_b"].reshape(1, 15), (128, 15)))
        d[f"l1f_{t}_b1"] = C(inp[f"L1_ffn_{t}_b1"].reshape(1024).reshape(8, 128).T)
        d[f"l1f_{t}_b2"] = C(inp[f"L1_ffn_{t}_b2"])
    return d


def host_consts():
    C = lambda x: np.ascontiguousarray(x, dtype=np.float32)
    c = {}
    c["c_ident"] = np.eye(P, dtype=np.float32)
    DR = 128
    inv_freq = 1.0 / 10000.0 ** (np.arange(0, DR, 2, dtype=np.float64) / DR)
    t = np.arange(S, dtype=np.float64)
    freqs = t[:, None] * inv_freq[None, :]
    emb = np.concatenate([freqs, freqs], axis=-1)
    c["c_cos"] = C(np.cos(emb).reshape(NT, P, DR).transpose(1, 0, 2))
    c["c_sin"] = C(np.sin(emb).reshape(NT, P, DR).transpose(1, 0, 2))
    m = np.zeros((P, P), dtype=np.float32)
    for i in range(P):
        m[i, i:] = 1.0
    c["c_diag"] = m
    E = np.zeros((8, P, 16), dtype=np.float32)
    for cc in range(8):
        E[cc, 0:64, 2 * cc] = 1.0
        E[cc, 64:128, 2 * cc + 1] = 1.0
    c["c_ecol"] = C(E.transpose(1, 0, 2).reshape(P, 128))
    kk = np.arange(33, dtype=np.float64)[:, None]
    pp = np.arange(64, dtype=np.float64)[None, :]
    ang = 2.0 * np.pi * kk * pp / 64.0
    c["c_dftc"] = C(np.cos(ang))
    c["c_dfts"] = C(np.sin(ang))
    ck = np.full((33, 1), 2.0 / 64.0)
    ck[0, 0] = 1.0 / 64.0
    ck[32, 0] = 1.0 / 64.0
    c["c_dftk"] = C(ck)
    EL = np.zeros((16, 1024), dtype=np.float32)
    for l in range(16):
        EL[l, 64 * l:64 * (l + 1)] = 1.0
    c["c_eleaf"] = EL
    c["c_ones_row"] = np.ones((1, P), dtype=np.float32)
    return c



def declare_inputs(nc, blocks):
    d = {}

    def di(name, shape):
        d[name] = nc.dram_tensor(name, list(shape), F32, kind="ExternalInput").ap()

    di("x_in", (NSEQ, S, D))
    di("c_ident", (P, P))
    di("c_ones_row", (1, P))
    if 0 in blocks:
        di("l0_wq", (1024, 256)); di("l0_bq", (P, 2))
        di("l0_wqc", (256, 896)); di("l0_bqc", (P, 7))
        di("l0_wqr", (1024, 128)); di("l0_bqr", (1, 128))
        di("l0_wkv", (1024, 256)); di("l0_bkv", (P, 2))
        di("l0_wkr", (1024, 128)); di("l0_bkr", (1, 128))
        di("l0_wkc", (256, 896)); di("l0_bkc", (P, 7))
        di("l0_wv", (256, 1024)); di("l0_bv", (1, 1024))
        di("l0_wo", (1024, 1024)); di("l0_bo", (1, 1024))
        di("c_cos", (P, NT, 128)); di("c_sin", (P, NT, 128))
    if 0 in blocks or 2 in blocks:
        di("c_diag", (P, P)); di("c_ecol", (P, 128))
    if 1 in blocks:
        di("l0f_wfcT", (1024, 4096)); di("l0f_w2T", (4096, 1024))
        di("l0f_lnw", (P, 32)); di("l0f_lnb", (P, 32))
        d["wfcq_dram"] = nc.dram_tensor("wfcq_dram", [1024, 4096], BF16,
                                        kind="Internal").ap()
        d["w2q_dram"] = nc.dram_tensor("w2q_dram", [4096, 1024], BF16,
                                       kind="Internal").ap()
        d["uv_dram"] = nc.dram_tensor("uv_dram", [1, 2048], BF16,
                                      kind="Internal").ap()
    if 2 in blocks:
        di("l1_wq", (1024, 1024)); di("l1_wk", (1024, 1024)); di("l1_wv", (1024, 1024))
        di("l1_wo", (1024, 1024)); di("l1_bo", (1, 1024))
        di("l1_mask", (33, 1))
        di("c_dftc", (33, 64)); di("c_dfts", (33, 64)); di("c_dftk", (33, 1))
    if 3 in blocks:
        for t in ("q", "k", "v"):
            di(f"l1f_{t}_w1", (1024, 1024)); di(f"l1f_{t}_w2", (1024, 1024))
            di(f"l1f_{t}_nw", (1024, 15)); di(f"l1f_{t}_nb", (P, 15))
            di(f"l1f_{t}_b1", (P, 8)); di(f"l1f_{t}_b2", (16, 1024))
        di("c_eleaf", (16, 1024))
    d["y_out"] = nc.dram_tensor("y_out", [NSEQ, S, D], F32, kind="ExternalOutput").ap()
    if DEBUG_B0:
        for nm in ("dbg_q", "dbg_k", "dbg_o", "dbg_xn"):
            d[nm] = nc.dram_tensor(nm, [P, 8, 2048], F32,
                                   kind="ExternalOutput").ap()
        d["dbg_v"] = nc.dram_tensor("dbg_v", [P, 16, 1040], F32,
                                    kind="ExternalOutput").ap()
        d["dbg_cq"] = nc.dram_tensor("dbg_cq", [P, 2, 2048], F32,
                                     kind="ExternalOutput").ap()
    return d


class H:
    pass


ARN_COLS = 68096


def build_handles(nc, ctx, blocks):
    h = H()
    h.nc = nc
    h.dram = declare_inputs(nc, blocks)
    sb = lambda name, shape, dt: ctx.enter_context(nc.sbuf_tensor(name, shape, dt))
    h.stream = sb("stream", [P, NT, D], F32)
    h.arn_t = sb("arena", [P, ARN_COLS], BF16)
    h.arn = Arena(h.arn_t[:, :], ARN_COLS)
    h.st32 = sb("st32", [P, 384], F32)
    h.pt = sb("pt", [P, 4, 1024], BF16)
    h.ident = sb("ident", [P, P], BF16)
    h.diag = sb("diag", [P, P], BF16)
    h.ecol = sb("ecol", [P, P], BF16)
    h.ones_row_bf = sb("ones_row_bf", [1, P], BF16)
    h.ones64 = sb("ones64", [P, 64], BF16)
    h.onesf32 = sb("onesf32", [P, 1], F32)
    h.idf32 = sb("idf32", [1, 1], F32)
    h.cbias = sb("cbias", [P, 16], F32)
    h.pall = ctx.enter_context(nc.psum_tensor("pall", [P, 4096], F32))
    return h


def alloc_elt(h, off):
    buf = h.arn.alloc_at("eltbuf", off, 5120)
    h.f32a = buf[:, 0:2048].bitcast(F32)
    h.f32b = buf[:, 2048:4096].bitcast(F32)
    h.junk = buf[:, 4096:5120]


def free_elt(h):
    h.arn.free_("eltbuf")
    h.f32a = h.f32b = h.junk = None


def pbank(h, b0, nb):
    return h.pall[:, b0 * 512:(b0 + nb) * 512]


def pbank_bf(h, b0, nb):
    return h.pall[:, b0 * 512:(b0 + nb) * 512].bitcast(BF16)



def emit_load_consts(p, h):
    d = h.dram
    p.dma("gpsimd", h.ident[:, :], d["c_ident"])
    p.dma("gpsimd", h.ones_row_bf[:, :], d["c_ones_row"])
    p.memset("gpsimd", h.ones64[:, :], 1.0)
    if "c_diag" in d:
        p.dma("gpsimd", h.diag[:, :], d["c_diag"])
        p.dma("gpsimd", h.ecol[:, :], d["c_ecol"])
    p.memset("gpsimd", h.onesf32[:, :], 1.0)
    p.memset("gpsimd", h.idf32[:, :], 1.0)
    p.mark("consts_gp", "gp")
    p.mark("consts", "dgp")


def emit_load_stream(p, h, seq):
    src = h.dram["x_in"][seq].rearrange("(t p) d -> p t d", p=P)
    w = p.need("stored")
    p.dma("sync", h.stream[:, 0:NT // 2, :], src[:, 0:NT // 2, :], wait=w)
    p.dma("gpsimd", h.stream[:, NT // 2:NT, :], src[:, NT // 2:NT, :], wait=w)
    p.mark("stream_sp", "dsp")
    p.mark("stream_gp", "dgp")
    p.mark("stream", "dgp")
    for t in range(NT):
        p.mark(f"strm{t}", "dsp" if t < NT // 2 else "dgp")


def emit_store_stream(p, h, seq):
    dst = h.dram["y_out"][seq].rearrange("(t p) d -> p t d", p=P)
    w = stream_ready(p)
    p.dma("sync", dst[:, 0:NT // 2, :], h.stream[:, 0:NT // 2, :], wait=w)
    p.dma("sync", dst[:, NT // 2:NT, :], h.stream[:, NT // 2:NT, :], wait=w)
    p.mark("stored", "dsp")


def stream_ready(p):
    return p.need("stream_sp", "stream_gp", "stream")


def load_w_fm(p, h, tag, dram_name, kc, m, q="gpsimd", extra_wait=(), off=None):
    buf = h.arn.alloc(tag, kc * m) if off is None else \
        h.arn.alloc_at(tag, off, kc * m)
    v = a3(buf, kc, m)
    p.dma(q, v, h.dram[dram_name].rearrange("(c p) m -> p c m", p=P), wait=extra_wait)
    p.mark(tag, "dsp" if q == "sync" else "dgp")
    return v


def load_row_bf(p, h, tag, dram_name, n, q="gpsimd", extra_wait=(), off=None):
    buf = h.arn.alloc(tag, max(n, 64)) if off is None else \
        h.arn.alloc_at(tag, off, max(n, 64))
    v = buf[0:1, 0:n]
    p.dma(q, v, h.dram[dram_name], wait=extra_wait)
    p.mark(tag, "dsp" if q == "sync" else "dgp")
    return v


def emit_prenorm_stats(p, h, tag, junk_bf):
    st = h.st32
    sums = st[:, 0:16]; ssq = st[:, 16:32]; mu = st[:, 32:48]
    nmu2 = st[:, 48:64]; veps = st[:, 64:80]; rstd = st[:, 80:96]
    negmur = st[:, 96:112]

    def cb(t):
        wt = p.req(f"strm{t}")
        p.red(sums[:, t:t + 1], h.stream[:, t, :], ALU.add, wait=wt,
              inc="dve")
        p.act(junk_bf[:, :], h.stream[:, t, :], AF.Square,
              accum_out=ssq[:, t:t + 1], wait=wt, inc="act")
        if t % 4 != 3:
            return
        q4 = t // 4
        cs = slice(4 * q4, 4 * q4 + 4)
        p.mark(f"{tag}_red{q4}", "dve"); p.mark(f"{tag}_sq{q4}", "act")
        wv = p.req(f"{tag}_red{q4}", f"{tag}_sq{q4}")
        p.ts("vector", mu[:, cs], sums[:, cs], 1.0 / D, None, ALU.mult,
             wait=wv, inc="dve")
        p.tt("vector", nmu2[:, cs], mu[:, cs], mu[:, cs], ALU.mult, inc="dve")
        p.ts("vector", veps[:, cs], ssq[:, cs], 1.0 / D, LN_EPS,
             ALU.mult, ALU.add, inc="dve")
        p.tt("vector", veps[:, cs], veps[:, cs], nmu2[:, cs], ALU.subtract,
             inc="dve")
        p.mark(f"{tag}_v{q4}", "dve")
        p.act(rstd[:, cs], veps[:, cs], AF.Ln, wait=p.req(f"{tag}_v{q4}"),
              inc="act")
        p.act(rstd[:, cs], rstd[:, cs], AF.Exp, scale=-0.5, inc="act")
        p.mark(f"{tag}_rstd{q4}", "act")
        p.stt("vector", negmur[:, cs], mu[:, cs], -1.0, rstd[:, cs],
              ALU.mult, ALU.mult, wait=p.req(f"{tag}_rstd{q4}"), inc="dve")
        p.mark(f"{tag}_nm{q4}", "dve")
    return cb


def emit_prenorm_fm(p, h, xnfm, tag, lo_cb=None, junk_ap=None,
                    prelude=False):
    st = h.st32
    sums = st[:, 0:16]; ssq = st[:, 16:32]; mu = st[:, 32:48]
    nmu2 = st[:, 48:64]; veps = st[:, 64:80]; rstd = st[:, 80:96]
    negmur = st[:, 96:112]
    cb = None
    if not prelude:
        jnk = h.junk if junk_ap is None else junk_ap
        cb = emit_prenorm_stats(p, h, tag, jnk)
    for t in range(NT):
        if cb is not None and t % 4 == 0:
            for tq in range(t, t + 4):
                cb(tq)
        par = t % 2
        f32t = h.f32a if par == 0 else h.f32b
        hi_tm = h.pt[:, 2 * par, :]
        pb = pbank_bf(h, 6 + par, 1)
        if lo_cb is None:
            p.act(hi_tm, h.stream[:, t, :], AF.Identity,
                  bias=negmur[:, t:t + 1], scale=rstd[:, t:t + 1],
                  wait=p.need(f"{tag}_rstd{t // 4}", f"{tag}_nm{t // 4}",
                              f"{tag}_cast{par}", f"{tag}_tp{par}"),
                  inc="act")
            p.mark(f"{tag}_cast{par}", "act")
        else:
            p.act(f32t[:, :], h.stream[:, t, :], AF.Identity,
                  bias=negmur[:, t:t + 1], scale=rstd[:, t:t + 1],
                  wait=p.need(f"{tag}_rstd{t // 4}", f"{tag}_nm{t // 4}",
                              f"{tag}_cast{par}"), inc="act")
            p.mark(f"{tag}_f32{par}", "act")
            p.copy("vector", hi_tm, f32t[:, :],
                   wait=p.need(f"{tag}_f32{par}", f"{tag}_tp{par}"),
                   inc="dve")
            lo_tm = h.pt[:, 2 * par + 1, :]
            p.stt("vector", lo_tm, hi_tm, -1.0, f32t[:, :], ALU.mult, ALU.add,
                  inc="dve")
            p.mark(f"{tag}_cast{par}", "dve")
        for c in range(8):
            p.transp(pb[:, c * 128:(c + 1) * 128], hi_tm[:, c * 128:(c + 1) * 128],
                     h.ident[:, :],
                     wait=(p.need(f"{tag}_cast{par}", "consts") + p.bkw(6 + par, 1)
                           if c == 0 else ()),
                     inc="pe" if c == 7 else None)
        p.mark(f"{tag}_tp{par}", "pe")
        p.copy("vector", xnfm[:, :, t * 128:(t + 1) * 128],
               pb.rearrange("p (c n) -> p c n", c=8),
               wait=p.req(f"{tag}_tp{par}"), inc="dve")
        p.bkr(6 + par, 1, "dve")
        p.mark(f"{tag}_pbf{par}", "dve")
        if lo_cb is not None:
            lo_cb(p, h, t, par)
        if t % 4 == 3:
            p.mark(f"{tag}q{t // 4}", "dve")
    p.mark(tag, "dve")


def emit_proj_fm(p, h, tag, dst_fm, lhsT_tiles, rhs_fm, n_kc, out_chunks,
                 act_func, bias_cols=None, rhs_ready=(), w_ready=(),
                 banks=(0, 2), scale=1.0, oc_base=0, rhs_ready_nq=None):
    it = 0
    for oc in range(oc_base, oc_base + out_chunks):
        for nq in range(4):
            par = it % 2
            ps = pbank(h, banks[par], 1)
            rr = rhs_ready_nq[nq] if rhs_ready_nq is not None else rhs_ready
            wl = list(rr) + list(w_ready) + p.bkw(banks[par], 1)
            for kc in range(n_kc):
                p.mm(ps, lhsT_tiles(kc, oc), rhs_fm[:, kc, nq * 512:(nq + 1) * 512],
                     start=(kc == 0), stop=(kc == n_kc - 1),
                     wait=wl if kc == 0 else (),
                     inc="pe" if kc == n_kc - 1 else None)
            p.mark(f"{tag}_mm{par}", "pe")
            b = bias_cols[:, oc:oc + 1] if bias_cols is not None else 0.0
            p.act(dst_fm[:, oc, nq * 512:(nq + 1) * 512], ps, act_func,
                  bias=b, scale=scale, wait=p.req(f"{tag}_mm{par}"), inc="act")
            p.bkr(banks[par], 1, "act")
            p.mark(f"{tag}_evac{par}", "act")
            it += 1
    p.mark(tag, "act")


def emit_l0_attn(p, h, seq, tail_cb=None, prelude=False):
    arn = h.arn
    d = h.dram
    t0 = f"b0_{seq}"

    prev = p.need("blkend")
    alloc_elt(h, 49408)
    cosb = a3(arn.alloc_at("ropec", 16384, 16 * 128), 16, 128)
    sinb = a3(arn.alloc_at("ropes", 18432, 16 * 128), 16, 128)
    p.dma("gpsimd", cosb, d["c_cos"], wait=prev)
    p.dma("gpsimd", sinb, d["c_sin"], wait=prev)
    p.mark("rope_tab", "dgp")

    xn = a3(arn.alloc_at("xn", 0, 8 * 2048), 8, 2048)
    emit_prenorm_fm(p, h, xn, t0 + "ln", prelude=prelude)
    wxn = p.req(t0 + "ln")
    wxn_nq = [p.req(f"{t0}lnq{q_}") for q_ in range(4)]
    if STOP_AFTER == "ln":
        p.mark("stream", "dve"); p.mark("blkend", "dve"); arn.reset(); return
    if DEBUG_B0 and seq == 0:
        p.dma("gpsimd", h.dram["dbg_xn"], xn, wait=wxn)

    wq = load_w_fm(p, h, "wq", "l0_wq", 8, 256, extra_wait=prev, off=20480)
    wkv = load_w_fm(p, h, "wkv", "l0_wkv", 8, 256, extra_wait=prev, off=22528)
    wqr = load_w_fm(p, h, "wqr", "l0_wqr", 8, 128, extra_wait=prev, off=24576)
    wkr = load_w_fm(p, h, "wkr", "l0_wkr", 8, 128, extra_wait=prev, off=25600)
    bqr = load_row_bf(p, h, "bqr", "l0_bqr", 128, extra_wait=prev, off=26624)
    bkr = load_row_bf(p, h, "bkr", "l0_bkr", 128, extra_wait=prev, off=26752)
    p.dma("sync", h.st32[:, 128:130], d["l0_bq"])
    p.dma("sync", h.st32[:, 130:132], d["l0_bkv"])
    p.mark("bq_bkv", "dsp")
    cq = a3(arn.alloc_at("cq", 54528, 2 * 2048), 2, 2048)
    ckv = a3(arn.alloc_at("ckv", 58624, 2 * 2048), 2, 2048)
    emit_proj_fm(p, h, f"cq{seq}", cq,
                 lambda kc, oc: wq[:, kc, oc * 128:(oc + 1) * 128],
                 xn, 8, 2, AF.Gelu, bias_cols=h.st32[:, 128:130],
                 rhs_ready_nq=wxn_nq, w_ready=p.req("wq", "bq_bkv"))
    emit_proj_fm(p, h, f"ckv{seq}", ckv,
                 lambda kc, oc: wkv[:, kc, oc * 128:(oc + 1) * 128],
                 xn, 8, 2, AF.Gelu, bias_cols=h.st32[:, 130:132],
                 rhs_ready_nq=wxn_nq, w_ready=p.req("wkv", "bq_bkv"))

    if DEBUG_B0 and seq == 0:
        p.dma("gpsimd", h.dram["dbg_cq"], cq, wait=p.req(f"cq{seq}"))
    qr = a3(arn.alloc_at("qr", 62720, 16 * 128), 16, 128)
    kr = a3(arn.alloc_at("kr", 64768, 16 * 128), 16, 128)
    for (nm, wtag, btag, wr, br, dst) in (("qr", "wqr", "bqr", wqr, bqr, qr),
                                          ("kr", "wkr", "bkr", wkr, bkr, kr)):
        for t in range(NT):
            par = t % 2
            ps = pbank(h, 4 + par, 1)[:, 0:128]
            wl = wxn_nq[t // 4] + p.req(wtag, btag) + p.bkw(4 + par, 1)
            for kc in range(8):
                p.mm(ps, xn[:, kc, t * 128:(t + 1) * 128], wr[:, kc, :],
                     start=(kc == 0), stop=False,
                     wait=wl if kc == 0 else ())
            p.mm(ps, h.ones_row_bf[:, :], br, start=False, stop=True, inc="pe")
            p.mark(f"rope_mm{par}", "pe")
            f32t = h.f32a if par == 0 else h.f32b
            wmm = p.req(f"rope_mm{par}") + p.need("rope_tab", f"rope_f32{par}")
            p.tt("vector", f32t[:, 0:128], ps, cosb[:, t, :], ALU.mult,
                 wait=wmm, inc="dve")
            p.ts("vector", f32t[:, 128:192], ps[:, 64:128], -1.0, None, ALU.mult,
                 inc="dve")
            p.copy("vector", f32t[:, 192:256], ps[:, 0:64], inc="dve")
            p.bkr(4 + par, 1, "dve")
            p.tt("vector", f32t[:, 128:256], f32t[:, 128:256], sinb[:, t, :],
                 ALU.mult, inc="dve")
            p.tt("vector", dst[:, t, :], f32t[:, 0:128], f32t[:, 128:256], ALU.add,
                 inc="dve")
            p.mark(f"rope_ps{par}", "dve")
            p.mark(f"rope_f32{par}", "dve")
    p.mark("rope_done", "dve")
    if STOP_AFTER == "stageA":
        p.mark("stream", "dve"); p.mark("blkend", "dve"); arn.reset(); return
    p.mark("stageA_pe", "pe")
    arn.free_("wq", "wkv", "wqr", "wkr", "bqr", "bkr", "xn", "ropec", "ropes")
    stageA_done = p.req("rope_done", "stageA_pe")

    wv = load_w_fm(p, h, "wv", "l0_wv", 2, 1024, extra_wait=stageA_done, off=20480)
    bv = load_row_bf(p, h, "bv", "l0_bv", 1024, extra_wait=stageA_done, off=22528)
    vaug = a3(arn.alloc_at("vaug", 0, 16 * 1040), 16, 1040)
    vones = vaug.rearrange("p t (hd c) -> p t hd c", hd=16)[:, :, :, 64:65]
    p.memset("gpsimd", vones, 1.0, wait=stageA_done, inc="gp")
    p.mark("vones", "gp")
    for t in range(NT):
        par = t % 2
        ps = pbank(h, 2 * par, 2)[:, 0:1024]
        wl = p.req(f"ckv{seq}", "wv", "bv") + p.bkw(2 * par, 2)
        for kc in range(2):
            for nn in range(2):
                p.mm(ps[:, nn * 512:(nn + 1) * 512],
                     ckv[:, kc, t * 128:(t + 1) * 128],
                     wv[:, kc, nn * 512:(nn + 1) * 512],
                     start=(kc == 0), stop=False,
                     wait=wl if (kc == 0 and nn == 0) else ())
        for nn in range(2):
            p.mm(ps[:, nn * 512:(nn + 1) * 512], h.ones_row_bf[:, :],
                 bv[:, nn * 512:(nn + 1) * 512], start=False, stop=True,
                 inc="pe" if nn == 1 else None)
        p.mark(f"vmm{par}", "pe")
        p.copy("vector",
               vaug[:, t, :].rearrange("p (hd c) -> p hd c", hd=16)[:, :, 0:64],
               ps.rearrange("p (hd c) -> p hd c", hd=16),
               wait=p.req(f"vmm{par}"), inc="dve")
        p.bkr(2 * par, 2, "dve")
        p.mark(f"vevac{par}", "dve")
    p.mark("vaug_done", "dve")
    arn.free_("wv", "bv")
    if STOP_AFTER == "V":
        p.mark("stream", "dve"); p.mark("blkend", "dve"); arn.reset(); return

    wqc = load_w_fm(p, h, "wqc", "l0_wqc", 2, 896, extra_wait=stageA_done, off=33024)
    p.dma("sync", h.st32[:, 132:139], d["l0_bqc"])
    p.mark("bqc", "dsp")
    Q = a3(arn.alloc_at("Q", 16640, 8 * 2048), 8, 2048)
    emit_proj_fm(p, h, f"qc{seq}", Q,
                 lambda kc, oc: wqc[:, kc, oc * 128:(oc + 1) * 128],
                 cq, 2, 7, AF.Gelu, bias_cols=h.st32[:, 132:139],
                 rhs_ready=p.req(f"cq{seq}") + p.need("vaug_done"),
                 w_ready=p.req("wqc", "bqc"))
    emit_rtile_gelu(p, h, f"qrt{seq}", qr, Q, 7, src_ready=p.req("rope_done"))
    arn.free_("wqc", "cq", "qr")
    if STOP_AFTER == "Q":
        p.mark("stream", "dve"); p.mark("blkend", "dve"); arn.reset(); return

    wkc = load_w_fm(p, h, "wkc", "l0_wkc", 2, 896,
                    extra_wait=p.req(f"qc{seq}", f"qrt{seq}"), off=54528)
    p.dma("sync", h.st32[:, 139:146], d["l0_bkc"])
    p.mark("bkc", "dsp")
    K = a3(arn.alloc_at("K", 33024, 8 * 2048), 8, 2048)
    emit_proj_fm(p, h, f"kc{seq}", K,
                 lambda kc, oc: wkc[:, kc, oc * 128:(oc + 1) * 128],
                 ckv, 2, 7, AF.Gelu, bias_cols=h.st32[:, 139:146],
                 rhs_ready=p.req(f"ckv{seq}", f"qc{seq}"),
                 w_ready=p.req("wkc", "bkc"))
    emit_rtile_gelu(p, h, f"krt{seq}", kr, K, 7, src_ready=p.req("rope_done"))
    arn.free_("wkc", "ckv", "kr")
    if STOP_AFTER == "K":
        p.mark("stream", "dve"); p.mark("blkend", "dve"); arn.reset(); return

    qk_ready = p.req(f"qc{seq}", f"qrt{seq}", f"kc{seq}", f"krt{seq}")
    if DEBUG_B0 and seq == 0:
        p.dma("gpsimd", h.dram["dbg_q"], Q, wait=qk_ready)
        p.dma("gpsimd", h.dram["dbg_k"], K, wait=qk_ready)
        p.dma("gpsimd", h.dram["dbg_v"], vaug, wait=qk_ready)
    free_elt(h)
    O = a3(arn.alloc_at("O", 49408, 8 * 2048), 8, 2048)
    emit_attention(p, h, f"at0_{seq}", Q, K, vaug, O, causal=True, scale=0.125,
                   qk_ready=qk_ready)
    arn.free_("Q", "K", "vaug")
    if STOP_AFTER == "att":
        p.mark("stream", "dve"); p.mark("blkend", "dve"); arn.reset(); return

    att_done = p.req(f"at0_{seq}_odone")
    if STOP_AFTER == "att2":
        p.mark("stream", "dve"); p.mark("blkend", "dve"); arn.reset(); return
    if DEBUG_B0 and seq == 0:
        p.dma("gpsimd", h.dram["dbg_o"], O,
              wait=att_done + p.req(f"at0_{seq}_oshifts"))
    alloc_elt(h, 9216)
    wo = load_w_fm(p, h, "wo", "l0_wo", 8, 1024, extra_wait=att_done, off=0)
    bo = load_row_bf(p, h, "bo", "l0_bo", 1024, extra_wait=att_done, off=8192)
    if STOP_AFTER == "wo":
        p.mark("stream", "dve"); p.mark("blkend", "dve"); arn.reset(); return
    emit_outproj_postnorm(p, h, f"b0o{seq}", O, wo, bo, tail_cb=tail_cb,
                          o_ready=p.req(f"at0_{seq}_odone", f"at0_{seq}_oshifts"),
                          w_ready=p.req("wo", "bo"),
                          ntiles=(1 if STOP_AFTER == "op1" else NT))
    arn.free_("O", "wo", "bo")
    free_elt(h)
    p.mark("blkend", "dve")


def emit_rtile_gelu(p, h, tag, r_tm, dst_fm, oc, src_ready):
    for t in range(NT):
        par = t % 2
        pb = pbank_bf(h, 4 + par, 1)[:, 0:128]
        p.transp(pb, r_tm[:, t, :], h.ident[:, :],
                 wait=list(src_ready) + p.need("consts") + p.bkw(4 + par, 1),
                 inc="pe")
        p.mark(f"{tag}_tp{par}", "pe")
        p.act(dst_fm[:, oc, t * 128:(t + 1) * 128], pb, AF.Gelu,
              wait=p.req(f"{tag}_tp{par}"), inc="act")
        p.bkr(4 + par, 1, "act")
        p.mark(f"{tag}_ev{par}", "act")
    p.mark(tag, "act")


def emit_attention(p, h, tag, Q, K, vaug, O, causal, scale, qk_ready):
    otmp = h.arn.alloc(tag + "_otmp", 2048, top=False)
    sqs = [h.pt[:, 0:2, :].rearrange("p a b -> p (a b)"),
           h.pt[:, 2:4, :].rearrange("p a b -> p (a b)")]
    qn = h.st32[:, 146:148]
    for (idx, X, xt) in ((0, Q, "q"), (1, K, "k")):
        ps = pbank(h, 0, 4)[0:16, 0:2048]

        def nc_mms(c):
            par = c % 2
            for nq in range(4):
                p.mm(ps[:, nq * 512:(nq + 1) * 512],
                     h.ecol[:, c * 16:(c + 1) * 16],
                     sqs[par][:, nq * 512:(nq + 1) * 512],
                     start=(c == 0), stop=(c == 7),
                     wait=(p.req(f"{tag}_sqc{par}") + p.need("consts") +
                           (p.bkw(0, 4) if c == 0 else []))
                     if nq == 0 else (),
                     inc="pe" if nq == 3 else None)
            p.mark(f"{tag}_nc{par}", "pe")

        for c in range(8):
            par = c % 2
            p.tt("gpsimd", sqs[par], X[:, c, :], X[:, c, :], ALU.mult,
                 wait=list(qk_ready) + p.need(f"{tag}_nc{par}"), inc="gp")
            p.mark(f"{tag}_sqc{par}", "gp")
            if c > 0:
                nc_mms(c - 1)
        nc_mms(7)
        p.mark(f"{tag}_norm{xt}", "pe")
        p.red(qn[0:16, idx:idx + 1], ps, ALU.max,
              wait=p.req(f"{tag}_norm{xt}"), inc="dve")
        p.bkr(0, 4, "dve")
        p.mark(f"{tag}_nmax{xt}", "dve")
    cb = h.st32[0:16, 148:149]
    cb2 = h.st32[0:16, 149:150]
    p.act(cb, qn[0:16, 0:1], AF.Ln, wait=p.req(f"{tag}_nmaxq", f"{tag}_nmaxk"),
          inc="act")
    p.act(cb2, qn[0:16, 1:2], AF.Ln, inc="act")
    p.mark(f"{tag}_lncb", "act")
    p.tt("vector", cb, cb, cb2, ALU.add, wait=p.req(f"{tag}_lncb"), inc="dve")
    p.mark(f"{tag}_cb1", "dve")
    p.act(cb, cb, AF.Exp, scale=0.5, wait=p.req(f"{tag}_cb1"), inc="act")
    p.mark(f"{tag}_cb2", "act")
    p.ts("vector", cb, cb, -scale, None, ALU.mult, wait=p.req(f"{tag}_cb2"),
         inc="dve")
    p.copy("vector", h.pt[0:16, 0, 0:1], cb, inc="dve")
    p.mark(f"{tag}_cbb", "dve")
    pb4 = pbank_bf(h, 4, 1)
    p.transp(pb4[0:1, 0:16], h.pt[0:16, 0, 0:1], h.ident[0:16, 0:16],
             wait=p.req(f"{tag}_cbb") + p.bkw(4, 1), inc="pe")
    p.mark(f"{tag}_cbT", "pe")
    p.copy("vector", h.pt[0:1, 0, 16:32], pb4[0:1, 0:16],
           wait=p.req(f"{tag}_cbT"), inc="dve")
    p.bkr(4, 1, "dve")
    p.mark(f"{tag}_cbrow", "dve")
    ps5 = pbank(h, 5, 1)
    for hd in range(16):
        p.mm(ps5[:, hd:hd + 1], h.ones_row_bf[:, :], h.pt[0:1, 0, 16 + hd:17 + hd],
             start=True, stop=True,
             wait=(p.req(f"{tag}_cbrow") + p.bkw(5, 1)) if hd == 0 else (),
             inc="pe" if hd == 15 else None)
    p.mark(f"{tag}_cbcast", "pe")
    p.copy("vector", h.cbias[:, 0:16], ps5[:, 0:16], wait=p.req(f"{tag}_cbcast"),
           inc="dve")
    p.bkr(5, 1, "dve")
    p.mark(f"{tag}_cbias", "dve")

    strips = []
    for kt in range(NT):
        for qh in range(2):
            q0, q1 = qh * 1024, (qh + 1) * 1024
            if causal and q1 <= kt * 128:
                continue
            qa = max(q0, kt * 128) if causal else q0
            strips.append((kt, qh, qa, q1 - qa))
    last_pv = {}
    for si, (kt, qh, qa, n) in enumerate(strips):
        off = 0
        while off < n:
            col = qa + off
            nn = min(512 - (col % 512), n - off)
            last_pv[col // 512] = (si, off)
            off += nn
    for hd in range(16):
        ch = hd // 2
        rb = 64 * (hd % 2)
        stb = 0
        first_pv = {}
        po = pbank(h, 4, 4)[0:65, 0:2048]
        def emit_strip(si):
            kt, qh, qa, n = strips[si]
            q0, q1 = qh * 1024, (qh + 1) * 1024
            par = si % 2
            ps = pbank(h, 2 * par, 2)[:, 0:1024]
            ptbuf = h.pt[:, par, :]
            wl = list(qk_ready) + p.bkw(2 * par, 2)
            off = 0
            while off < n:
                nn = min(512, n - off)
                last = off + nn >= n
                p.mm(ps[:, off:off + nn],
                     K[rb:rb + 64, ch, kt * 128:(kt + 1) * 128],
                     Q[rb:rb + 64, ch, qa + off:qa + off + nn],
                     start=True, stop=True,
                     wait=wl if off == 0 else (),
                     inc="pe" if last else None)
                off += nn
            p.mark(f"{tag}_s{par}", "pe")
            p.act(ptbuf[:, 0:n], ps[:, 0:n], AF.Exp,
                  bias=h.cbias[:, hd:hd + 1], scale=scale,
                  wait=p.req(f"{tag}_s{par}", f"{tag}_cbias") +
                  p.need(f"{tag}_pv{par}"), inc="act")
            p.bkr(2 * par, 2, "act")
            p.mark(f"{tag}_exp{par}", "act")
            if causal and q0 <= kt * 128 < q1:
                doff = kt * 128 - qa
                p.tt("vector", ptbuf[:, doff:doff + 128],
                     ptbuf[:, doff:doff + 128], h.diag[:, :], ALU.mult,
                     wait=p.req(f"{tag}_exp{par}") + p.need("consts"),
                     inc="dve")
                p.mark(f"{tag}_dm{par}", "dve")

        rrow = h.pt[64:65, 2:4, :].rearrange("p a b -> p (a b)")

        def emit_pv(si):
            kt, qh, qa, n = strips[si]
            q0, q1 = qh * 1024, (qh + 1) * 1024
            par = si % 2
            ptbuf = h.pt[:, par, :]
            expw = (p.req(f"{tag}_dm{par}")
                    if causal and q0 <= kt * 128 < q1
                    else p.req(f"{tag}_exp{par}"))
            off = 0
            while off < n:
                col = qa + off
                nn = min(512 - (col % 512), n - off)
                blk = col // 512
                st_ = first_pv.get(blk, True)
                first_pv[blk] = False
                last = off + nn >= n
                stop_ = last_pv[blk] == (si, off)
                p.mm(po[:, col:col + nn],
                     vaug[:, kt, hd * 65:hd * 65 + 65],
                     ptbuf[:, off:off + nn],
                     start=st_, stop=stop_,
                     wait=(expw + p.need("vaug_done", "vones",
                                         f"{tag}_cbias") + p.bkw(4, 4))
                     if off == 0 else (),
                     inc="pe" if last else None)
                off += nn
            p.mark(f"{tag}_pv{par}", "pe")

        for si in range(len(strips)):
            emit_strip(si)
            if si > 0:
                emit_pv(si - 1)
        emit_pv(len(strips) - 1)
        p.mark(f"{tag}_pvh", "pe")
        p.act(po[64:65, 0:2048], po[64:65, 0:2048], AF.Ln,
              wait=p.req(f"{tag}_pvh"), inc="act")
        p.mark(f"{tag}_lnr", "act")
        p.act(rrow, po[64:65, 0:2048], AF.Exp, scale=-1.0,
              wait=p.req(f"{tag}_lnr"), inc="act")
        p.mark(f"{tag}_rr", "act")
        pbA = pbank(h, 0, 2)[0:64, 0:1024]
        pbB = pbank(h, 2, 2)[0:64, 0:1024]
        for half, px in ((0, pbA), (1, pbB)):
            for nn in range(2):
                p.mm(px[:, nn * 512:(nn + 1) * 512],
                     h.ones64[64:65, 0:64],
                     rrow[0:1, half * 1024 + nn * 512:half * 1024 + (nn + 1) * 512],
                     start=True, stop=True,
                     wait=(p.req(f"{tag}_rr", "consts_gp") + p.bkw(0, 4))
                     if (half == 0 and nn == 0) else (),
                     inc="pe" if (half == 1 and nn == 1) else None)
        p.mark(f"{tag}_rbc", "pe")
        bcsA = h.pt[0:64, 2, :]
        bcsB = h.pt[0:64, 3, :]
        p.act(bcsA, pbA, AF.Identity, wait=p.req(f"{tag}_rbc"), inc="act")
        p.act(bcsB, pbB, AF.Identity, inc="act")
        p.bkr(0, 4, "act")
        p.mark(f"{tag}_bcs", "act")
        if rb == 0:
            dstA = O[0:64, ch, 0:1024]
            dstB = O[0:64, ch, 1024:2048]
        else:
            dstA = otmp[0:64, 0:1024]
            dstB = otmp[0:64, 1024:2048]
        p.tt("vector", dstA, po[0:64, 0:1024], bcsA, ALU.mult,
             wait=p.req(f"{tag}_bcs") + p.need(f"{tag}_oshift"), inc="dve")
        p.tt("vector", dstB, po[0:64, 1024:2048], bcsB, ALU.mult, inc="dve")
        p.bkr(4, 4, "dve")
        p.mark(f"{tag}_oevac", "dve")
        if rb != 0:
            p.dma("sync", O[64:128, ch, :], otmp[0:64, :],
                  wait=p.req(f"{tag}_oevac"))
            p.mark(f"{tag}_oshift", "dsp")
    h.arn.free_(tag + "_otmp")
    p.mark(f"{tag}_odone", "dve")
    p.mark(f"{tag}_oshifts", "dsp")


def emit_outproj_postnorm(p, h, tag, O, wo, bo, o_ready, w_ready, ntiles=NT,
                          tail_cb=None):
    st = h.st32
    ysum = st[:, 160:176]; yssq = st[:, 176:192]; mu = st[:, 192:208]
    veps = st[:, 208:224]; rstd = st[:, 224:240]; nmu2 = st[:, 240:256]
    negmur = st[:, 256:272]
    for t in range(ntiles):
        par = t % 2
        ps = pbank(h, 2 * par, 2)[:, 0:1024]
        wl = list(o_ready) + list(w_ready) + p.bkw(2 * par, 2)
        for oc in range(8):
            for nn in range(2):
                p.mm(ps[:, nn * 512:(nn + 1) * 512],
                     O[:, oc, t * 128:(t + 1) * 128],
                     wo[:, oc, nn * 512:(nn + 1) * 512],
                     start=(oc == 0), stop=False,
                     wait=wl if (oc == 0 and nn == 0) else ())
        for nn in range(2):
            p.mm(ps[:, nn * 512:(nn + 1) * 512], h.ones_row_bf[:, :],
                 bo[:, nn * 512:(nn + 1) * 512], start=False, stop=True,
                 inc="pe" if nn == 1 else None)
        p.mark(f"{tag}_mm{par}", "pe")
        if OP1_OPS < 2: continue
        p.red(ysum[:, t:t + 1], ps, ALU.add, wait=p.req(f"{tag}_mm{par}"),
              inc="dve")
        p.mark(f"{tag}_red{par}", "dve")
        if OP1_OPS < 3: continue
        p.act(h.junk[:, :], ps, AF.Square, accum_out=yssq[:, t:t + 1],
              wait=p.req(f"{tag}_mm{par}", f"{tag}_red{par}"), inc="act")
        p.mark(f"{tag}_sq{par}", "act")
        if OP1_OPS < 4: continue
        p.ts("vector", mu[:, t:t + 1], ysum[:, t:t + 1], 1.0 / D, None, ALU.mult,
             wait=p.req(f"{tag}_red{par}", f"{tag}_sq{par}"), inc="dve")
        p.tt("vector", nmu2[:, t:t + 1], mu[:, t:t + 1], mu[:, t:t + 1], ALU.mult,
             inc="dve")
        p.ts("vector", veps[:, t:t + 1], yssq[:, t:t + 1], 1.0 / D, LN_EPS,
             ALU.mult, ALU.add, inc="dve")
        p.tt("vector", veps[:, t:t + 1], veps[:, t:t + 1], nmu2[:, t:t + 1],
             ALU.subtract, inc="dve")
        p.mark(f"{tag}_veps{par}", "dve")
        if OP1_OPS < 5: continue
        p.act(rstd[:, t:t + 1], veps[:, t:t + 1], AF.Ln,
              wait=p.req(f"{tag}_veps{par}"), inc="act")
        p.act(rstd[:, t:t + 1], rstd[:, t:t + 1], AF.Exp, scale=-0.5, inc="act")
        p.mark(f"{tag}_rstd{par}", "act")
        if OP1_OPS < 6: continue
        p.ts("vector", negmur[:, t:t + 1], mu[:, t:t + 1], rstd[:, t:t + 1], -1.0,
             ALU.mult, ALU.mult, wait=p.req(f"{tag}_rstd{par}"), inc="dve")
        p.mark(f"{tag}_nm{par}", "dve")
        if OP1_OPS < 7: continue
        f32t = h.f32a if par == 0 else h.f32b
        p.act(f32t[:, :], ps, AF.Identity, bias=negmur[:, t:t + 1],
              scale=rstd[:, t:t + 1],
              wait=p.req(f"{tag}_rstd{par}", f"{tag}_nm{par}") +
              p.need(f"{tag}_resid{par}"), inc="act")
        p.bkr(2 * par, 2, "act")
        p.mark(f"{tag}_apl{par}", "act")
        if OP1_OPS < 8: continue
        p.tt("vector", h.stream[:, t, :], h.stream[:, t, :], f32t[:, :], ALU.add,
             wait=p.req(f"{tag}_apl{par}"), inc="dve")
        p.mark(f"{tag}_resid{par}", "dve")
        p.mark(f"strm{t}", "dve")
        if tail_cb is not None:
            tail_cb(t)
    p.mark("stream", "dve")
    p.mark(tag, "dve")




def emit_l1_attn(p, h, seq, tail_cb=None, prelude=False):
    arn = h.arn
    d = h.dram
    t0 = f"b2_{seq}"
    prev = p.need("blkend")
    alloc_elt(h, 16640)
    xn = a3(arn.alloc_at("xn", 0, 8 * 2048), 8, 2048)
    emit_prenorm_fm(p, h, xn, t0 + "ln", prelude=prelude,
                    junk_ap=arn.ap[:, 24832:26880].bitcast(F32))
    wxn = p.req(t0 + "ln")

    abd = arn.alloc_at("abd", 16384, 128)
    dftc = arn.alloc_at("dftc", 66048, 64)[0:33, 0:64]
    dfts = arn.alloc_at("dfts", 66112, 64)[0:33, 0:64]
    p.dma("gpsimd", dftc, d["c_dftc"], wait=prev)
    p.dma("gpsimd", dfts, d["c_dfts"], wait=prev)
    p.dma("sync", h.st32[0:33, 280:281], d["l1_mask"], wait=prev)
    p.dma("sync", h.st32[0:33, 281:282], d["c_dftk"], wait=prev)
    p.mark("flt_in", "dsp")
    p.mark("flt_in2", "dgp")
    m_ = h.st32[0:33, 282:283]
    p.act(m_, h.st32[0:33, 280:281], AF.Sigmoid, wait=p.req("flt_in"), inc="act")
    p.mark("flt_sig", "act")
    p.tt("vector", m_, m_, h.st32[0:33, 281:282], ALU.mult,
         wait=p.req("flt_sig"), inc="dve")
    p.mark("flt_mc", "dve")
    csc = h.pt[0:33, 0, 0:64]
    sns = h.pt[0:33, 0, 64:128]
    p.ts("vector", csc, dftc, m_, None, ALU.mult,
         wait=p.req("flt_mc", "flt_in2"), inc="dve")
    p.ts("vector", sns, dfts, m_, None, ALU.mult, inc="dve")
    p.mark("flt_tab", "dve")
    pA = pbank(h, 6, 1)
    for base in (0, 64):
        tp = (0, base)
        for (lt, rt) in ((csc, dftc), (sns, dfts)):
            def mmf(t, lt=lt, rt=rt, base=base):
                return t.matmul(pA[base:base + 64, 0:64], lt, rt,
                                start=(lt is csc), stop=(lt is sns),
                                tile_position=(0, base))
            p.do("tensor", mmf,
                 wait=(p.req("flt_tab") + p.bkw(6, 1)) if (base == 0 and lt is csc) else (),
                 inc="pe" if (base == 64 and lt is sns) else None)
    p.mark("flt_mm", "pe")
    p.memset("vector", abd, 0.0, wait=prev, inc="dve")
    p.mark("abd_z", "dve")
    p.act(abd[0:64, 0:64], pA[0:64, 0:64], AF.Identity,
          wait=p.req("flt_mm", "abd_z"), inc="act")
    p.act(abd[64:128, 64:128], pA[64:128, 0:64], AF.Identity, inc="act")
    p.bkr(6, 1, "act")
    p.mark("abd", "act")
    arn.free_("dftc", "dfts")

    wv1 = load_w_fm(p, h, "wv1", "l1_wv", 8, 1024, extra_wait=prev, off=33024)
    vaug = a3(arn.alloc_at("vaug", 49408, 16 * 1040), 16, 1040)
    vones = vaug.rearrange("p t (hd c) -> p t hd c", hd=16)[:, :, :, 64:65]
    p.memset("gpsimd", vones, 1.0, wait=prev, inc="gp")
    p.mark("vones", "gp")
    for t in range(NT):
        par = t % 2
        ps = pbank(h, 2 * par, 2)[:, 0:1024]
        wl = p.req(f"{t0}lnq{t // 4}", "wv1") + p.bkw(2 * par, 2)
        for kc in range(8):
            for nn in range(2):
                p.mm(ps[:, nn * 512:(nn + 1) * 512],
                     xn[:, kc, t * 128:(t + 1) * 128],
                     wv1[:, kc, nn * 512:(nn + 1) * 512],
                     start=(kc == 0), stop=(kc == 7),
                     wait=wl if (kc == 0 and nn == 0) else (),
                     inc="pe" if (kc == 7 and nn == 1) else None)
        p.mark(f"vmm{par}", "pe")
        p.copy("vector",
               vaug[:, t, :].rearrange("p (hd c) -> p hd c", hd=16)[:, :, 0:64],
               ps.rearrange("p (hd c) -> p hd c", hd=16),
               wait=p.req(f"vmm{par}"), inc="dve")
        p.bkr(2 * par, 2, "dve")
        p.mark(f"vevac{par}", "dve")
    p.mark("vaug_done", "dve")
    arn.free_("wv1")

    free_elt(h)
    for (nm, wname, woff, dst_off) in (("q2", "l1_wq", 33024, 16640),
                                       ("k2", "l1_wk", 66048, 33024)):
        if nm == "q2":
            w1 = load_w_fm(p, h, nm + "w", wname, 8, 1024,
                           extra_wait=p.req("vaug_done"), off=woff)
            X = a3(arn.alloc_at(nm, dst_off, 8 * 2048), 8, 2048)
            emit_proj_fm(p, h, f"{nm}p{seq}", X,
                         lambda kc, oc, w1=w1: w1[:, kc, oc * 128:(oc + 1) * 128],
                         xn, 8, 8, AF.Identity,
                         rhs_ready=wxn + p.req("vaug_done"),
                         w_ready=p.req(nm + "w"))
            arn.free_(nm + "w")
        else:
            X = a3(arn.alloc_at(nm, dst_off, 8 * 2048), 8, 2048)
            for quarter in range(4):
                wq_ = a3(arn.alloc_at(f"{nm}w{quarter}", 66048, 8 * 256), 8, 256)
                p.dma("gpsimd", wq_,
                      d[wname].rearrange("(c p) m -> p c m", p=P)
                      [:, :, quarter * 256:(quarter + 1) * 256],
                      wait=p.req(f"{nm}p{seq}_ev{quarter - 1}") if quarter else
                      p.req(f"q2p{seq}"))
                p.mark(f"{nm}wl{quarter}", "dgp")
                emit_proj_fm(p, h, f"{nm}p{seq}q{quarter}",
                             a3(X.rearrange("p a b -> p (a b)"), 8, 2048),
                             lambda kc, oc, wq_=wq_, quarter=quarter:
                             wq_[:, kc, (oc - 2 * quarter) * 128:
                                 (oc - 2 * quarter + 1) * 128],
                             xn, 8, 2, AF.Identity,
                             rhs_ready=wxn, w_ready=p.req(f"{nm}wl{quarter}"),
                             oc_base=2 * quarter)
                p.mark(f"{nm}p{seq}_ev{quarter}", "act")
                arn.free_(f"{nm}w{quarter}")
            p.mark(f"{nm}p{seq}", "act")
        for c in range(8):
            par = c % 2
            ps = pbank(h, 4 + par, 1)
            for nq in range(4):
                p.mm(ps[:, 0:512], abd, X[:, c, nq * 512:(nq + 1) * 512],
                     start=True, stop=True,
                     wait=(p.req("abd", f"{nm}p{seq}") + p.bkw(4 + par, 1) +
                           p.need(f"{nm}f_ev{par}")) if nq == 0 else
                     p.need(f"{nm}f_ev{par}"),
                     inc="pe")
                p.mark(f"{nm}f_mm{par}", "pe")
                p.act(X[:, c, nq * 512:(nq + 1) * 512], ps[:, 0:512], AF.Identity,
                      wait=p.req(f"{nm}f_mm{par}"), inc="act")
                p.bkr(4 + par, 1, "act")
                p.mark(f"{nm}f_ev{par}", "act")
        p.mark(f"{nm}flt", "act")

    Q = a3(arn.ap[:, 16640:16640 + 16384], 8, 2048)
    K = a3(arn.ap[:, 33024:33024 + 16384], 8, 2048)
    qk_ready = p.req(f"q2flt", f"k2flt")
    arn.free_("xn", "abd")
    O = a3(arn.alloc_at("O", 0, 8 * 2048), 8, 2048)
    emit_attention(p, h, f"at2_{seq}", Q, K, vaug, O, causal=False, scale=0.125,
                   qk_ready=qk_ready)
    arn.free_("q2", "k2", "vaug")

    att_done = p.req(f"at2_{seq}_odone")
    alloc_elt(h, 25856)
    wo = load_w_fm(p, h, "wo2", "l1_wo", 8, 1024, extra_wait=att_done, off=16640)
    bo = load_row_bf(p, h, "bo2", "l1_bo", 1024, extra_wait=att_done, off=24832)
    emit_outproj_postnorm(p, h, f"b2o{seq}", O, wo, bo, tail_cb=tail_cb,
                          o_ready=p.req(f"at2_{seq}_odone", f"at2_{seq}_oshifts"),
                          w_ready=p.req("wo2", "bo2"))
    arn.free_("O", "wo2", "bo2")
    free_elt(h)
    p.mark("blkend", "dve")




def emit_b3_routing(p, h, seq, xnfm, lofm_slot):
    st = h.st32

    def lo_cb(p, h, t, par):
        arn = h.arn
        lofm = lofm_slot
        pbL = pbank_bf(h, 4 + par, 1)
        lo_tm = h.pt[:, 2 * par + 1, :]
        tag = f"b3r{seq}"
        for c in range(8):
            p.transp(pbL[:, c * 128:(c + 1) * 128], lo_tm[:, c * 128:(c + 1) * 128],
                     h.ident[:, :],
                     wait=(p.need(f"b3_{seq}ln_cast{par}", "consts") +
                           p.bkw(4 + par, 1)) if c == 0 else (),
                     inc="pe" if c == 7 else None)
        p.mark(f"{tag}_lotp{par}", "pe")
        p.copy("vector", lofm, pbL.rearrange("p (c n) -> p c n", c=8),
               wait=p.req(f"{tag}_lotp{par}"), inc="dve")
        p.bkr(4 + par, 1, "dve")
        p.mark(f"{tag}_lofm", "dve")
        nwh_all, nwl_all = h.b3_nwa
        psb = pbank(h, 0, 1)[:, 0:48]
        it = 0
        for (lhs_src, rhs_w) in ((xnfm, nwh_all), (xnfm, nwl_all),
                                 (lofm, nwh_all)):
            for dc in range(8):
                if lhs_src is xnfm:
                    lhsT = xnfm[:, dc, t * 128:(t + 1) * 128]
                else:
                    lhsT = lofm[:, dc, :]
                p.mm(psb, lhsT, rhs_w[:, dc, :],
                     start=(it == 0), stop=(it == 23),
                     wait=(p.req(f"b3_{seq}ln_pbf{par}", f"{tag}_lofm",
                                 "b3nw", "b3nwz") + p.bkw(0, 1))
                     if it == 0 else (),
                     inc="pe" if it == 23 else None)
                it += 1
        p.mark(f"{tag}_scmm", "pe")
        for fi, f in enumerate(("q", "k", "v")):
            sc = st[:, 288:303]
            p.tt("vector", sc, psb[:, fi * 16:fi * 16 + 15], h.b3_nb[f],
                 ALU.add,
                 wait=p.req(f"{tag}_scmm", f"b3nbL{f}") + p.need(f"{tag}_g"),
                 inc="dve")
            if f == "v":
                p.bkr(0, 1, "dve")
            p.mark(f"{tag}_scv", "dve")
            gb = h.b3_gbase[f] + t * 15
            p.ts("gpsimd", h.b3_g1[:, gb:gb + 15],
                 sc, 0.0, None, ALU.is_ge,
                 wait=p.req(f"{tag}_scv"), inc="gp")
            p.mark(f"{tag}_g", "gp")
    return lo_cb


def emit_b3_sel(p, h, seq, f, fi, tiles):
    st = h.st32
    tag = f"b3sel{seq}{f}"
    scr = h.b3_selscr
    g30 = scr[:, 64:94]
    ua = scr[:, 32:48]
    ub = scr[:, 48:64]
    rs = ARN_COLS
    gt = g30.tensor
    for t in tiles:
        par = t % 2
        stm = h.b3_seltm[:, par * 16:par * 16 + 16]
        if f == "q":
            gsrc = (h.b3_g1qa[:, t * 15:t * 15 + 15] if t < 8 else
                    h.b3_g1qb[:, (t - 8) * 15:(t - 8) * 15 + 15])
        else:
            gb = h.b3_gbase[f] + t * 15
            gsrc = h.b3_g1[:, gb:gb + 15]
        p.copy("gpsimd", g30[:, 15:30], gsrc,
               wait=p.req(f"b3r{seq}_g"), inc="gp")
        p.ts("gpsimd", g30[:, 0:15], g30[:, 15:30], -1.0, 1.0,
             ALU.mult, ALU.add, inc="gp")
        o = g30.offset
        T0 = bass.AP(gt, o + 0, [[rs, 128], [15, 2], [0, 2], [0, 4]])
        T1 = bass.AP(gt, o + 1, [[rs, 128], [1, 2], [15, 2], [0, 4]])
        T2 = bass.AP(gt, o + 3, [[rs, 128], [1, 4], [15, 2], [0, 2]])
        T3 = bass.AP(gt, o + 7, [[rs, 128], [2, 4], [1, 2], [15, 2]])
        p.tt("gpsimd", ua.rearrange("p (a b c) -> p a b c", a=2, b=2),
             T0, T1, ALU.mult, inc="gp")
        p.tt("gpsimd", ub.rearrange("p (a b c) -> p a b c", a=4, b=2),
             T2, T3, ALU.mult, inc="gp")
        p.tt("gpsimd", stm, ua, ub, ALU.mult,
             wait=p.need(f"b3stm{par}"), inc="gp")
        p.mark(f"{tag}_stm", "gp")
        pbs = pbank_bf(h, 6 + par, 1)[0:16, 0:128]
        p.transp(pbs, stm, h.ident[:, :],
                 wait=p.req(f"{tag}_stm") + p.bkw(6 + par, 1), inc="pe")
        p.mark(f"{tag}_tpp{par}", "pe")
        p.mark(f"b3stm{par}", "pe")
        p.copy("vector", h.b3_selfm[0:16, t * 128:(t + 1) * 128], pbs,
               wait=p.req(f"{tag}_tpp{par}"), inc="dve")
        p.bkr(6 + par, 1, "dve")
        p.mark(f"{tag}_tp{par}", "dve")
        p.mark(f"{tag}_T{t}", "dve")
    p.mark(tag, "dve")




def emit_b3_fff(p, h, seq, f, fi, out_mode, groups, xn_src, out_buf, sel_off=0):
    d = h.dram
    tag = f"b3f{seq}{f}"
    st = h.st32
    w1, w2 = h.b3_w1, h.b3_w2
    ev_sem = "act" if out_mode == "fm" else "dve"
    for g in groups:
        gs = slice(g * 256, g * 256 + 256)
        ss = slice(sel_off + g * 256, sel_off + g * 256 + 256)
        tg = (sel_off + g * 256) // 128
        selw = p.req(f"b3sel{seq}{f}_T{tg}", f"b3sel{seq}{f}_T{tg + 1}")
        for half_m in range(2):
            pm = pbank(h, 6, 2).rearrange("p (c n) -> p c n", c=4)
            for hc4 in range(4):
                hc = half_m * 4 + hc4
                p.mm(pm[:, hc4, :], h.b3_eleaf[:, hc * 128:(hc + 1) * 128],
                     h.b3_selfm[0:16, ss], start=True, stop=True,
                     wait=(selw + p.bkw(6, 2) + p.req("b3el", f"{tag}_w"))
                     if hc4 == 0 else (),
                     inc="pe" if hc4 == 3 else None)
            p.mark(f"{tag}_mm", "pe")
            for hc4 in range(4):
                hc = half_m * 4 + hc4
                par = hc % 2
                ph = pbank(h, 4 + par, 1)[:, 0:256]
                wl = (p.req(f"{tag}_w", f"{tag}_b1") + p.bkw(4 + par, 1) +
                      list(xn_src[1]))
                for dc in range(8):
                    p.mm(ph, w1[:, dc, hc * 128:(hc + 1) * 128],
                         xn_src[0][:, dc, gs],
                         start=(dc == 0), stop=(dc == 7),
                         wait=wl if dc == 0 else (),
                         inc="pe" if dc == 7 else None)
                p.mark(f"{tag}_hmm{par}", "pe")
                hslot = h.pt[:, hc // 4, (hc % 4) * 256:(hc % 4) * 256 + 256]
                p.act(hslot, ph, AF.Relu, bias=st[:, 357 + hc:358 + hc],
                      wait=p.req(f"{tag}_hmm{par}") + p.need("b3H_free"),
                      inc="act")
                p.bkr(4 + par, 1, "act")
                p.mark(f"{tag}_hr{hc}", "act")
                p.tt("vector", hslot, hslot, pm[:, hc4, :], ALU.mult,
                     wait=p.req(f"{tag}_hr{hc}", f"{tag}_mm"), inc="dve")
                p.mark(f"{tag}_hm{hc}", "dve")
            p.bkr(6, 2, "dve")
        hw = p.req(*[f"{tag}_hm{hc}" for hc in range(8)])
        if out_mode == "fm":
            for ds in range(8):
                par = ds % 2
                ps = pbank(h, 2 + par, 1)[:, 0:256]
                for hc in range(8):
                    hslot = h.pt[:, hc // 4, (hc % 4) * 256:(hc % 4) * 256 + 256]
                    p.mm(ps, w2[:, hc, ds * 128:(ds + 1) * 128], hslot,
                         start=(hc == 0), stop=False,
                         wait=(hw + p.req(f"{tag}_w", f"{tag}_b2") +
                               p.bkw(2 + par, 1)) if hc == 0 else ())
                p.mm(ps, h.b3_b2[:, ds * 128:(ds + 1) * 128],
                     h.b3_selfm[0:16, ss], start=False, stop=True, inc="pe")
                p.mark(f"{tag}_pmm{par}", "pe")
                p.act(out_buf[:, ds, gs], ps, AF.Identity,
                      wait=p.req(f"{tag}_pmm{par}"), inc="act")
                p.bkr(2 + par, 1, "act")
                p.mark(f"{tag}_pev{par}", "act")
        else:
            for tt2 in range(2):
                t_ = g * 2 + tt2
                tsl = slice(sel_off + t_ * 128, sel_off + t_ * 128 + 128)
                par = tt2 % 2
                ps = pbank(h, 2 + 2 * par, 2)[:, 0:1024]
                for hc in range(8):
                    hslot = h.pt[:, hc // 4,
                                 (hc % 4) * 256 + tt2 * 128:
                                 (hc % 4) * 256 + tt2 * 128 + 128]
                    for nn in range(2):
                        p.mm(ps[:, nn * 512:(nn + 1) * 512], hslot,
                             w2[:, hc, nn * 512:(nn + 1) * 512],
                             start=(hc == 0), stop=False,
                             wait=(hw + p.req(f"{tag}_w", f"{tag}_b2") +
                                   p.bkw(2 + 2 * par, 2))
                             if (hc == 0 and nn == 0) else ())
                for nn in range(2):
                    p.mm(ps[:, nn * 512:(nn + 1) * 512],
                         h.b3_selfm[0:16, tsl],
                         h.b3_b2[:, nn * 512:(nn + 1) * 512],
                         start=False, stop=True,
                         inc="pe" if nn == 1 else None)
                p.mark(f"{tag}_pmm{par}", "pe")
                p.copy("vector", out_buf[:, t_, 0:1024], ps,
                       wait=p.req(f"{tag}_pmm{par}"), inc="dve")
                p.bkr(2 + 2 * par, 2, "dve")
                p.mark(f"{tag}_pev{par}", "dve")
        p.mark("b3H_free", ev_sem)
    p.mark(tag, ev_sem)


def emit_b3_fff_weights(p, h, seq, f, gate):
    d = h.dram
    tag = f"b3f{seq}{f}"
    p.dma("gpsimd", h.b3_w1, d[f"l1f_{f}_w1"].rearrange("(c p) m -> p c m", p=P),
          wait=gate)
    p.dma("gpsimd", h.b3_w2, d[f"l1f_{f}_w2"].rearrange("(c p) m -> p c m", p=P),
          wait=())
    p.dma("gpsimd", h.b3_b2, d[f"l1f_{f}_b2"], wait=())
    p.mark(f"{tag}_w", "dgp")
    p.mark(f"{tag}_b2", "dgp")
    p.dma("sync", h.st32[:, 357:365], d[f"l1f_{f}_b1"], wait=gate)
    p.mark(f"{tag}_b1", "dsp")


def emit_b3_attn_quarter(p, h, seq, quarter, qq, kf, vaug, qz_ready):
    st = h.st32
    tag = f"b3a{seq}q{quarter}"
    scale = 1.0 / 32.0
    sqb = h.pt[:, 0:2, :].rearrange("p a b -> p (a b)")
    for c in range(8):
        p.tt("gpsimd", sqb[:, 0:512], qq[:, c, :], qq[:, c, :], ALU.mult,
             wait=list(qz_ready) + p.need(f"{tag}_nc") + p.need("b3H_free"),
             inc="gp")
        p.mark(f"{tag}_sq", "gp")
        p.mm(pbank(h, 7, 1)[0:1, 0:512], h.ones64[:, 0:1], sqb[:, 0:512],
             start=(c == 0), stop=(c == 7),
             wait=p.req(f"{tag}_sq", "consts_gp") +
             (p.bkw(7, 1) if c == 0 else []),
             inc="pe")
        p.mark(f"{tag}_nc", "pe")
    p.mark("b3H_free", "gp")
    p.red(st[0:1, 278:279], pbank(h, 7, 1)[0:1, 0:512], ALU.max,
          wait=p.req(f"{tag}_nc"), inc="dve")
    p.bkr(7, 1, "dve")
    p.mark(f"{tag}_qmax", "dve")
    p.act(st[0:1, 277:278], st[0:1, 278:279], AF.Ln,
          wait=p.req(f"b3n{seq}_kmax", f"{tag}_qmax"), inc="act")
    p.act(st[0:1, 276:277], st[0:1, 279:280], AF.Ln, inc="act")
    p.mark(f"{tag}_lns", "act")
    p.tt("vector", st[0:1, 277:278], st[0:1, 277:278], st[0:1, 276:277],
         ALU.add, wait=p.req(f"{tag}_lns"), inc="dve")
    p.mark(f"{tag}_cbs", "dve")
    p.act(st[0:1, 277:278], st[0:1, 277:278], AF.Exp, scale=0.5,
          wait=p.req(f"{tag}_cbs"), inc="act")
    p.mark(f"{tag}_cbe", "act")
    p.ts("vector", st[0:1, 277:278], st[0:1, 277:278], -scale, None, ALU.mult,
         wait=p.req(f"{tag}_cbe"), inc="dve")
    p.copy("vector", h.pt[0:1, 2, 256:257], st[0:1, 277:278], inc="dve")
    p.mark(f"{tag}_cbb", "dve")
    p.mm(pbank(h, 7, 1)[:, 0:1], h.ones_row_bf[:, :], h.pt[0:1, 2, 256:257],
         start=True, stop=True, wait=p.req(f"{tag}_cbb") + p.bkw(7, 1),
         inc="pe")
    p.mark(f"{tag}_cbc", "pe")
    p.copy("vector", h.cbias[:, 0:1], pbank(h, 7, 1)[:, 0:1],
           wait=p.req(f"{tag}_cbc"), inc="dve")
    p.bkr(7, 1, "dve")
    p.mark(f"{tag}_cbias", "dve")

    kw = p.req(f"b3f{seq}k", "b3vones", f"b3f{seq}v")
    for qtb in range(2):
        qa = quarter * 512 + qtb * 256
        first = {}
        def emit_strip(kt):
            par = kt % 2
            ps = pbank(h, par, 1)[:, 0:256]
            ptb = h.pt[:, 0, par * 256:par * 256 + 256]
            wl = kw + list(qz_ready) + p.bkw(par, 1) + p.req(f"{tag}_cbias")
            for dc in range(8):
                p.mm(ps, kf[:, dc, kt * 128:(kt + 1) * 128],
                     qq[:, dc, qtb * 256:qtb * 256 + 256],
                     start=(dc == 0), stop=(dc == 7),
                     wait=wl if dc == 0 else (),
                     inc="pe" if dc == 7 else None)
            p.mark(f"{tag}_s{par}", "pe")
            p.act(ptb, ps, AF.Exp, bias=h.cbias[:, 0:1], scale=scale,
                  wait=p.req(f"{tag}_s{par}") + p.need(f"b3pv{seq}_{par}"),
                  inc="act")
            p.bkr(par, 1, "act")
            p.mark(f"{tag}_exp{par}", "act")

        def emit_pv(kt):
            par = kt % 2
            ptb = h.pt[:, 0, par * 256:par * 256 + 256]
            expw = p.req(f"{tag}_exp{par}")
            for qt in range(2):
                pts = ptb[:, qt * 128:qt * 128 + 128]
                for dhalf in range(2):
                    po = pbank(h, 2 + 2 * qt + dhalf, 1)
                    p.mm(po, pts, vaug[:, kt, dhalf * 512:(dhalf + 1) * 512],
                         start=(kt == 0), stop=(kt == NT - 1),
                         wait=(expw + p.bkw(2 + 2 * qt + dhalf, 1))
                         if kt == 0 else
                         (expw if (qt == 0 and dhalf == 0) else ()),
                         inc=None)
                p.mm(pbank(h, 6 + qt, 1)[:, 0:1], pts, vaug[:, kt, 1024:1025],
                     start=(kt == 0), stop=(kt == NT - 1),
                     wait=(expw + p.bkw(6 + qt, 1)) if kt == 0 else (),
                     inc="pe" if qt == 1 else None)
            p.mark(f"b3pv{seq}_{par}", "pe")

        for kt in range(NT):
            emit_strip(kt)
            if kt > 0:
                emit_pv(kt - 1)
        emit_pv(NT - 1)
        p.mark(f"{tag}_pvdone", "pe")
        p.mark("b3H_free", "pe")
        for qt in range(2):
            t = (qa + qt * 128) // 128
            par = qt
            rec = st[:, 275:276]
            p.recip(rec, pbank(h, 6 + qt, 1)[:, 0:1],
                    wait=p.req(f"{tag}_pvdone"), inc="dve")
            p.bkr(6 + qt, 1, "dve")
            p.mark(f"{tag}_rec", "dve")
            f32t = h.f32a if par == 0 else h.f32b
            for dhalf in range(2):
                p.ts("vector", f32t[:, dhalf * 512:(dhalf + 1) * 512],
                     pbank(h, 2 + 2 * qt + dhalf, 1), rec, None, ALU.mult,
                     wait=p.req(f"{tag}_rec") + p.need(f"{tag}_yfree{par}"),
                     inc="dve")
                p.bkr(2 + 2 * qt + dhalf, 1, "dve")
            p.mark(f"{tag}_y{par}", "dve")
            emit_postnorm_tile(p, h, f"{tag}e", t, par, f32t,
                               p.req(f"{tag}_y{par}"))
            p.mark(f"{tag}_yfree{par}", "dve")
    p.mark(tag, "dve")


def emit_postnorm_tile(p, h, tag, t, par, y_f32, y_ready, tail_cb=None):
    st = h.st32
    ysum = st[:, 160:176]; yssq = st[:, 176:192]; mu = st[:, 192:208]
    veps = st[:, 208:224]; rstd = st[:, 224:240]; nmu2 = st[:, 240:256]
    negmur = st[:, 256:272]
    p.red(ysum[:, t:t + 1], y_f32, ALU.add, wait=y_ready, inc="dve")
    p.mark(f"{tag}_red{par}", "dve")
    p.act(h.junk[:, :], y_f32, AF.Square, accum_out=yssq[:, t:t + 1],
          wait=y_ready + p.req(f"{tag}_red{par}"), inc="act")
    p.mark(f"{tag}_sq{par}", "act")
    p.ts("vector", mu[:, t:t + 1], ysum[:, t:t + 1], 1.0 / D, None, ALU.mult,
         wait=p.req(f"{tag}_red{par}", f"{tag}_sq{par}"), inc="dve")
    p.tt("vector", nmu2[:, t:t + 1], mu[:, t:t + 1], mu[:, t:t + 1], ALU.mult,
         inc="dve")
    p.ts("vector", veps[:, t:t + 1], yssq[:, t:t + 1], 1.0 / D, LN_EPS,
         ALU.mult, ALU.add, inc="dve")
    p.tt("vector", veps[:, t:t + 1], veps[:, t:t + 1], nmu2[:, t:t + 1],
         ALU.subtract, inc="dve")
    p.mark(f"{tag}_veps{par}", "dve")
    p.act(rstd[:, t:t + 1], veps[:, t:t + 1], AF.Ln,
          wait=p.req(f"{tag}_veps{par}"), inc="act")
    p.act(rstd[:, t:t + 1], rstd[:, t:t + 1], AF.Exp, scale=-0.5, inc="act")
    p.mark(f"{tag}_rstd{par}", "act")
    p.ts("vector", negmur[:, t:t + 1], mu[:, t:t + 1], rstd[:, t:t + 1], -1.0,
         ALU.mult, ALU.mult, wait=p.req(f"{tag}_rstd{par}"), inc="dve")
    p.mark(f"{tag}_nm{par}", "dve")
    p.ts("vector", y_f32, y_f32, rstd[:, t:t + 1], negmur[:, t:t + 1],
         ALU.mult, ALU.add, wait=p.req(f"{tag}_nm{par}"), inc="dve")
    p.mark(f"{tag}_apl{par}", "dve")
    p.tt("vector", h.stream[:, t, :], h.stream[:, t, :], y_f32, ALU.add,
         wait=p.req(f"{tag}_apl{par}"), inc="dve")
    p.mark(f"{tag}_resid{par}", "dve")
    p.mark("stream", "dve")
    p.mark(f"strm{t}", "dve")
    if tail_cb is not None:
        tail_cb(t)


def emit_l1_ffn(p, h, seq, prelude=False):
    arn = h.arn
    d = h.dram
    t0 = f"b3_{seq}"
    prev = p.need("blkend")
    st = h.st32

    h.b3_g1 = arn.alloc_at("b3g1", 33088, 720)
    h.b3_gbase = {"v": 0, "k": 240, "q": 480}
    h.b3_nwh = {}
    h.b3_nwl = {}
    h.b3_nb = {}
    off = 66048
    nwh_all = a3(arn.alloc_at("b3nwh", off, 384), 8, 48)
    nwl_all = a3(arn.alloc_at("b3nwl", off + 384, 384), 8, 48)
    for j in (15, 31, 47):
        p.memset("gpsimd", nwh_all[:, :, j:j + 1], 0.0, wait=prev, inc="gp")
        p.memset("gpsimd", nwl_all[:, :, j:j + 1], 0.0, inc="gp")
    p.mark("b3nwz", "gp")
    h.b3_nwa = (nwh_all, nwl_all)
    off += 768
    for fi, f in enumerate(("q", "k", "v")):
        nwf = arn.alloc_at(f"b3nw32{f}", off, 256).bitcast(F32)
        p.dma("sync", a3(nwf[:, 0:120], 8, 15),
              d[f"l1f_{f}_nw"].rearrange("(c p) m -> p c m", p=P), wait=prev)
        p.mark(f"b3nwf{f}", "dsp")
        nwh = nwh_all[:, :, fi * 16:fi * 16 + 15]
        nwl = nwl_all[:, :, fi * 16:fi * 16 + 15]
        nw32 = a3(nwf[:, 0:120], 8, 15)
        p.copy("vector", nwh, nw32, wait=p.req(f"b3nwf{f}"), inc="dve")
        p.stt("vector", nwl, nwh, -1.0, nw32, ALU.mult, ALU.add, inc="dve")
        nb = arn.alloc_at(f"b3nb{f}", off + 256, 32).bitcast(F32)[:, 0:15]
        p.dma("sync", nb, d[f"l1f_{f}_nb"], wait=prev)
        p.mark(f"b3nbL{f}", "dsp")
        h.b3_nb[f] = nb
        off += 320
    p.mark("b3nw", "dve")

    lofm_buf = a3(arn.alloc_at("b3lofm", 61760, 8 * 128), 8, 128)
    alloc_elt(h, 49472)
    xn = a3(arn.alloc_at("xn", 0, 8 * 2048), 8, 2048)
    lo_cb = emit_b3_routing(p, h, seq, xn, lofm_buf)
    emit_prenorm_fm(p, h, xn, t0 + "ln", lo_cb=lo_cb, prelude=prelude)
    wxn = p.req(t0 + "ln")
    free_elt(h)
    arn.free_("b3lofm")
    for f in ("q", "k", "v"):
        arn.free_(f"b3nw32{f}", f"b3nb{f}")
    arn.free_("b3nwh", "b3nwl")
    p.mark("b3wgate", "dve")
    h.b3_g1qa = arn.alloc_at("b3g1qa", 16512, 128)
    h.b3_g1qb = arn.alloc_at("b3g1qb", 65856, 192)
    p.copy("gpsimd", h.b3_g1qa[:, 0:120], h.b3_g1[:, 480:600],
           wait=p.req(f"b3r{seq}_g"), inc="gp")
    p.copy("gpsimd", h.b3_g1qb[:, 0:120], h.b3_g1[:, 600:720], inc="gp")
    p.mark("b3g1q", "gp")

    h.b3_w1 = a3(arn.alloc_at("b3w1", 49472, 8 * 1024), 8, 1024)
    h.b3_w2 = a3(arn.alloc_at("b3w2", 57664, 8 * 1024), 8, 1024)
    h.b3_b2 = arn.alloc_at("b3b2", 66048, 1024)[0:16, :]
    h.b3_eleaf = arn.alloc_at("b3el", 67072, 1024)[0:16, :]
    p.dma("gpsimd", h.b3_eleaf, d["c_eleaf"], wait=p.req("b3wgate"))
    p.mark("b3el", "dgp")
    h.b3_selfm = h.pt[0:16, 2:4, :].rearrange("p a b -> p (a b)")
    h.b3_selscr = arn.alloc_at("b3stm", 16384, 96)
    h.b3_seltm = h.b3_selscr[:, 0:32]
    p.mark("b3H_free", "dve")

    vaug = a3(arn.alloc_at("vaug", 16640, 16 * 1025), 16, 1025)
    p.memset("gpsimd", vaug[:, :, 1024:1025], 1.0, wait=p.req("b3wgate"),
             inc="gp")
    p.mark("b3vones", "gp")
    emit_b3_sel(p, h, seq, "v", 2, range(NT))
    emit_b3_fff_weights(p, h, seq, "v", p.req("b3wgate"))
    emit_b3_fff(p, h, seq, "v", 2, "vaug", list(range(8)), (xn, wxn), vaug)
    arn.free_("b3g1")
    kf = a3(arn.alloc_at("kf", 33088, 8 * 2048), 8, 2048)
    emit_b3_sel(p, h, seq, "k", 1, range(NT))
    emit_b3_fff_weights(p, h, seq, "k", p.req(f"b3f{seq}v"))
    emit_b3_fff(p, h, seq, "k", 1, "fm", list(range(8)), (xn, wxn), kf)

    tagk = f"b3n{seq}"
    kw = p.req(f"b3f{seq}k")
    sqb = h.pt[:, 0:2, :].rearrange("p a b -> p (a b)")
    for c in range(8):
        p.tt("gpsimd", sqb[:, 0:2048], kf[:, c, :], kf[:, c, :], ALU.mult,
             wait=kw + p.need(f"{tagk}_nc") + p.need("b3H_free"), inc="gp")
        p.mark(f"{tagk}_sq", "gp")
        for nq in range(4):
            p.mm(pbank(h, 2 + nq, 1)[0:1, 0:512], h.ones64[:, 0:1],
                 sqb[:, nq * 512:(nq + 1) * 512],
                 start=(c == 0), stop=(c == 7),
                 wait=(p.req(f"{tagk}_sq", "consts_gp") +
                       (p.bkw(2 + nq, 1) if c == 0 else [])),
                 inc="pe")
        p.mark(f"{tagk}_nc", "pe")
    p.mark("b3H_free", "gp")
    for nq in range(4):
        p.red(st[0:1, 344 + nq:345 + nq], pbank(h, 2 + nq, 1)[0:1, 0:512],
              ALU.max, wait=p.req(f"{tagk}_nc"), inc="dve")
        p.bkr(2 + nq, 1, "dve")
    p.tt("vector", st[0:1, 344:345], st[0:1, 344:345], st[0:1, 345:346],
         ALU.max, inc="dve")
    p.tt("vector", st[0:1, 346:347], st[0:1, 346:347], st[0:1, 347:348],
         ALU.max, inc="dve")
    p.tt("vector", st[0:1, 279:280], st[0:1, 344:345], st[0:1, 346:347],
         ALU.max, inc="dve")
    p.mark(f"{tagk}_kmax", "dve")

    arn.free_("xn")

    emit_b3_sel(p, h, seq, "q", 0, range(NT))
    emit_b3_fff_weights(p, h, seq, "q", p.req(f"b3f{seq}k"))
    qq = a3(arn.alloc_at("qq", 0, 8 * 512), 8, 512)
    xnq = a3(arn.alloc_at("xnq", 4096, 8 * 512), 8, 512)
    alloc_elt(h, 8192)
    regen = arn.alloc_at("b3regen", 13312, 1024)
    rstd = st[:, 80:96]
    negmur = st[:, 96:112]
    for quarter in range(4):
        tq0 = quarter * 4
        for ti in range(4):
            t = tq0 + ti
            par = ti % 2
            pb = pbank_bf(h, 6 + par, 1)
            p.act(h.f32a[:, :], h.stream[:, t, :], AF.Identity,
                  bias=negmur[:, t:t + 1], scale=rstd[:, t:t + 1],
                  wait=stream_ready(p) + p.need("b3rg_hi"), inc="act")
            p.mark("b3rg_f32", "act")
            p.copy("vector", regen, h.f32a[:, :],
                   wait=p.req("b3rg_f32") + p.need(f"b3rg_tpX{par}"), inc="dve")
            p.mark("b3rg_hi", "dve")
            for c in range(8):
                p.transp(pb[:, c * 128:(c + 1) * 128],
                         regen[:, c * 128:(c + 1) * 128], h.ident[:, :],
                         wait=(p.req("b3rg_hi") + p.bkw(6 + par, 1))
                         if c == 0 else (),
                         inc="pe" if c == 7 else None)
            p.mark(f"b3rg_tpp{par}", "pe")
            p.copy("vector", xnq[:, :, ti * 128:(ti + 1) * 128],
                   pb.rearrange("p (c n) -> p c n", c=8),
                   wait=p.req(f"b3rg_tpp{par}") + p.need(f"{t0}qused"),
                   inc="dve")
            p.bkr(6 + par, 1, "dve")
            p.mark(f"b3rg_tpX{par}", "dve")
        p.mark("b3rg", "dve")
        emit_b3_fff(p, h, seq, "q", 0, "fm", [0, 1], (xnq, p.req("b3rg")), qq,
                    sel_off=quarter * 512)
        emit_b3_attn_quarter(p, h, seq, quarter, qq, kf, vaug,
                             qz_ready=p.req(f"b3f{seq}q"))
        p.mark(f"{t0}qused", "pe")
    arn.free_("qq", "xnq", "b3regen", "b3w1", "b3w2", "b3b2", "b3stm",
              "b3g1qa", "b3g1qb",
              "b3el", "kf", "vaug")
    free_elt(h)
    p.mark("blkend", "dve")





def emit_b1_quantize(p, h):
    arn = h.arn
    d = h.dram
    st = h.st32
    wf32s = [arn.alloc_at("q_wf32", 16640, 8192).bitcast(F32),
             arn.alloc_at("q_wf32b", 38144, 8192).bitcast(F32)]
    abufs = [arn.alloc_at("q_a", 24832, 4096),
             arn.alloc_at("q_a2", 46336, 4096)]
    bbuf = arn.alloc_at("q_b", 28928, 4096)
    QE = ("sync", "gpsimd")
    QS = ("dsp", "dgp")
    onesrow32 = bbuf.bitcast(F32)[0:1, 0:128]
    lnw32 = st[:, 300:332]
    lnb32 = st[:, 332:364]
    lnwb = arn.ap[:, 16384:16416]
    lnbb = arn.ap[:, 16416:16448]
    p.dma("sync", lnw32, d["l0f_lnw"], wait=p.need("blkend"))
    p.dma("sync", lnb32, d["l0f_lnb"])
    p.mark("qln_w", "dsp")
    p.copy("vector", lnwb, lnw32, wait=p.req("qln_w"), inc="dve")
    p.copy("vector", lnbb, lnb32, inc="dve")
    p.mark("qln_bf", "dve")
    for (nm, dram_in, dram_out, nch, chw) in (
            ("fc", "l0f_wfcT", "wfcq_dram", 8, 4096),
            ("w2", "l0f_w2T", "w2q_dram", 32, 1024)):
        src3 = d[dram_in].rearrange("(c p) m -> p c m", p=P)
        dst3 = d[dram_out].rearrange("(c p) m -> p c m", p=P)
        asum = st[:, 365:366]
        acc = st[:, 366:367]
        p.memset("vector", onesrow32, 1.0, wait=p.need("blkend"), inc="dve")
        p.mark("q_ones", "dve")
        for c in range(nch):
            par = c % 2
            wb = wf32s[par]
            p.dma(QE[par], wb[:, 0:chw], src3[:, c, :],
                  wait=p.need(f"q{nm}_used{par}"))
            p.mark(f"q{nm}_ld{par}", QS[par])
            p.stt("vector", abufs[par][:, 0:chw], wb[:, 0:chw], -1.0,
                  wb[:, 0:chw], ALU.mult, ALU.max, accum_out=asum,
                  wait=p.req(f"q{nm}_ld{par}"), inc="dve")
            p.mark(f"q{nm}_used{par}", "dve")
            if c == 0:
                p.copy("vector", acc, asum, inc="dve")
            else:
                p.tt("vector", acc, acc, asum, ALU.add, inc="dve")
        p.mark(f"q{nm}_asum", "dve")
        p.mm(pbank(h, 0, 1)[0:1, 0:1], h.onesf32[:, :], acc, start=True, stop=True,
             wait=p.req(f"q{nm}_asum", "consts_gp") + p.bkw(0, 1), inc="pe")
        p.mark(f"q{nm}_tot", "pe")
        thr = st[0:1, 367:368]
        p.ts("vector", thr, pbank(h, 0, 1)[0:1, 0:1],
             0.5 / (4096.0 * 1024.0), 0.5e-6, ALU.mult, ALU.add,
             wait=p.req(f"q{nm}_tot"), inc="dve")
        p.bkr(0, 1, "dve")
        p.copy("vector", st[0:1, 368:369], thr, inc="dve")
        p.mark(f"q{nm}_thr0", "dve")
        p.mm(pbank(h, 0, 1)[:, 0:1], onesrow32, st[0:1, 368:369],
             start=True, stop=True,
             wait=p.req(f"q{nm}_thr0", "q_ones") + p.bkw(0, 1),
             inc="pe")
        p.mark(f"q{nm}_thrb", "pe")
        thrc = st[:, 369:370]
        nthrc = st[:, 370:371]
        p.copy("vector", thrc, pbank(h, 0, 1)[:, 0:1],
               wait=p.req(f"q{nm}_thrb"), inc="dve")
        p.bkr(0, 1, "dve")
        p.ts("vector", nthrc, thrc, -1.0, None, ALU.mult, inc="dve")
        p.mark(f"q{nm}_thr", "dve")
        for c in range(nch):
            par = c % 2
            wb = wf32s[par]
            ab_ = abufs[par]
            p.dma(QE[par], wb[:, 0:chw], src3[:, c, :],
                  wait=p.need(f"q{nm}_q{par}", f"q{nm}_used{par}"))
            p.mark(f"q{nm}_ld2{par}", QS[par])
            p.ts("vector", ab_[:, 0:chw], wb[:, 0:chw], thrc, None,
                 ALU.is_ge,
                 wait=p.req(f"q{nm}_ld2{par}", f"q{nm}_thr") +
                 p.need(f"q{nm}_st{par}", f"q{nm}_uv{par}"), inc="dve")
            p.ts("vector", bbuf[:, 0:chw], wb[:, 0:chw], nthrc, -1.0,
                 ALU.is_ge, ALU.add, inc="dve")
            p.mark(f"q{nm}_used{par}", "dve")
            p.tt("vector", ab_[:, 0:chw], ab_[:, 0:chw], bbuf[:, 0:chw],
                 ALU.add, inc="dve")
            if nm == "w2":
                p.mark(f"qw2_tern{par}", "dve")
                for bi, (bk, lcol) in enumerate(((2, lnbb), (4, lnwb))):
                    for nn in range(2):
                        p.mm(pbank(h, bk + nn, 1)[0:1, 0:512],
                             lcol[:, c:c + 1],
                             ab_[:, nn * 512:(nn + 1) * 512],
                             start=(c == 0), stop=(c == nch - 1),
                             wait=(p.req(f"qw2_tern{par}", "qln_bf") +
                                   (p.bkw(2, 4) if c == 0 else []))
                             if (bi == 0 and nn == 0) else (),
                             inc="pe" if (bi == 1 and nn == 1) else None)
                p.mark(f"qw2_uv{par}", "pe")
                p.ts("vector", ab_[:, 1024:2048], ab_[:, 0:1024],
                     lnw32[:, c:c + 1], None, ALU.mult, inc="dve")
                p.mark(f"q{nm}_q{par}", "dve")
                p.dma(QE[par], dst3[:, c, :], ab_[:, 1024:2048],
                      wait=p.req(f"q{nm}_q{par}"))
            else:
                p.mark(f"q{nm}_q{par}", "dve")
                p.dma(QE[par], dst3[:, c, :], ab_[:, 0:chw],
                      wait=p.req(f"q{nm}_q{par}"))
            p.mark(f"q{nm}_st{par}", QS[par])
    uvstage = arn.ap[0:1, 26880:28928]
    p.act(uvstage[0:1, 0:1024], pbank(h, 2, 2)[0:1, 0:1024], AF.Identity,
          wait=p.req("qw2_uv0", "qw2_uv1"), inc="act")
    p.act(uvstage[0:1, 1024:2048], pbank(h, 4, 2)[0:1, 0:1024], AF.Identity,
          inc="act")
    p.bkr(2, 4, "act")
    p.mark("b1uvkeep", "act")
    p.dma("sync", d["uv_dram"], uvstage, wait=p.req("b1uvkeep"))
    p.mark("b1uv_st", "dsp")
    arn.free_("q_wf32", "q_wf32b", "q_a", "q_a2", "q_b")
    p.mark("b1_quant", "dsp")
    p.mark("b1_quantg", "dgp")


def emit_l0_ffn(p, h, seq, tail_cb=None, prelude=False):
    arn = h.arn
    d = h.dram
    st = h.st32
    t0 = f"b1_{seq}"
    prev = p.need("blkend")
    alloc_elt(h, 33024)
    xn = a3(arn.alloc_at("xn", 0, 8 * 2048), 8, 2048)
    emit_prenorm_fm(p, h, xn, t0 + "ln", prelude=prelude)
    wxn = p.req(t0 + "ln")
    if seq == 0:
        emit_b1_quantize(p, h)

    wfq = d["wfcq_dram"].rearrange("(c p) m -> p c m", p=P)
    w2q3 = d["w2q_dram"].rearrange("(c p) m -> p c m", p=P)
    rows = arn.alloc_at("b1rows", 29952, 1024)
    rows2 = arn.alloc_at("b1rows2", 31232, 1536)
    vrow = rows2[0:1, 0:1024]
    urow = rows[0:1, 0:1024]
    nmub = rows2[0:1, 1024:1280]
    p.dma("gpsimd", vrow, d["uv_dram"][0:1, 0:1024], wait=p.req("b1uv_st"))
    p.dma("gpsimd", urow, d["uv_dram"][0:1, 1024:2048])
    p.mark("b1uvld", "dgp")
    vbc = arn.alloc_at("b1vbc", 28928, 1024)
    for nn in range(2):
        p.mm(pbank(h, 2 + nn, 1),
             h.ones_row_bf[:, :], vrow[0:1, nn * 512:(nn + 1) * 512],
             start=True, stop=True,
             wait=(p.req("b1uvld") + p.bkw(2, 2)) if nn == 0 else (), inc="pe")
    p.mark("b1vbc_mm", "pe")
    p.act(vbc[:, 0:512], pbank(h, 2, 1), AF.Identity, wait=p.req("b1vbc_mm"),
          inc="act")
    p.act(vbc[:, 512:1024], pbank(h, 3, 1), AF.Identity, inc="act")
    p.bkr(2, 2, "act")
    p.mark("b1vbc", "act")

    hgs = a3(arn.alloc_at("b1hg", 16640, 32 * 256), 32, 256)
    wst = arn.alloc_at("b1wfq", 38144, 8192)
    w2r = a3(arn.alloc_at("b1w2r", 46336, 16384), 16, 1024)
    w2s = arn.alloc_at("b1w2s", 62720, 4096)
    b0gate = p.need(f"b0o{seq}_mm0", f"b0o{seq}_mm1")
    for half in range(2):
        p.dma(("gpsimd", "sync")[half], w2r[:, half * 8:half * 8 + 8, :],
              w2q3[:, half * 8:half * 8 + 8, :],
              wait=b0gate + (p.req("qw2_st0", "qw2_st1") if seq == 0 else []))
    p.mark("b1w2r", "dgp")
    p.mark("b1w2rS", "dsp")
    sqslots = [h.pt[:, 0, 0:256], h.pt[:, 0, 256:512]]
    for g in range(8):
        ts0, ts1 = g * 256, g * 256 + 256
        w2v4s = [a3(a3(w2s, 2, 2048)[:, b_, :], 2, 1024) for b_ in range(2)]

        def hg_gelu(hc):
            hc4, hci = hc // 4, hc % 4
            bpar = hc4 % 2
            spar = hc % 2
            wsv4 = a3(a3(wst, 2, 4096)[:, bpar, :], 8, 512)
            wsv = a3(wsv4.rearrange("p a b -> p (a b)"), 32, 128)
            ph = pbank(h, spar, 1)[:, 0:256]
            for dc in range(8):
                p.mm(ph, wsv[:, dc * 4 + hci, :], xn[:, dc, ts0:ts1],
                     start=(dc == 0), stop=(dc == 7),
                     wait=(p.req(f"{t0}lnq{g // 2}", f"b1wfB{bpar}") +
                           p.bkw(spar, 1))
                     if dc == 0 else ())
            p.mark(f"b1hg_mm{spar}", "pe")
            p.act(hgs[:, hc, :], ph, AF.Gelu,
                  wait=p.req(f"b1hg_mm{spar}") + p.need(f"b1A_mm{spar}"),
                  inc="act")
            p.bkr(spar, 1, "act")
            p.mark(f"b1hg{hc}", "act")

        def sq_part(hc):
            par = hc % 2
            p.tt("gpsimd", sqslots[par], hgs[:, hc, :], hgs[:, hc, :], ALU.mult,
                 wait=p.req(f"b1hg{hc}") + p.need(f"b1sq_used{par}"), inc="gp")
            p.mark(f"b1sq{par}", "gp")

        def stats(hc):
            par = hc % 2
            hslice = hgs[:, hc, :]
            p.mm(pbank(h, 6, 1)[0:1, 0:256], h.ones64[:, 0:1], hslice,
                 start=(hc == 0), stop=(hc == 31),
                 wait=(p.req(f"b1hg{hc}", "consts_gp") +
                       (p.bkw(6, 1) if hc == 0 else [])), inc="pe")
            p.mm(pbank(h, 7, 1)[0:1, 0:256], h.ones64[:, 0:1], sqslots[par],
                 start=(hc == 0), stop=(hc == 31),
                 wait=(p.req(f"b1sq{par}") + (p.bkw(7, 1) if hc == 0 else [])),
                 inc="pe")
            p.mark(f"b1sq_used{par}", "pe")
            p.mark(f"b1stat{hc}", "pe")

        def amm(hc):
            spar = hc % 2
            if hc < 16:
                wsrc = w2r[:, hc, :]
                ww = p.req("b1w2r", "b1w2rS")
            else:
                b2_ = (hc - 16) // 2
                bpar = b2_ % 2
                wsrc = w2v4s[bpar][:, hc % 2, :]
                ww = p.req(f"b1w2B{bpar}")
            for t2 in range(2):
                for dh in range(2):
                    p.mm(pbank(h, 2 + 2 * t2 + dh, 1),
                         hgs[:, hc, t2 * 128:(t2 + 1) * 128],
                         wsrc[:, dh * 512:(dh + 1) * 512],
                         start=(hc == 0), stop=False,
                         wait=(p.req(f"b1hg{hc}") + ww +
                               (p.bkw(2 + 2 * t2 + dh, 1)
                                if hc == 0 else []))
                         if (t2 == 0 and dh == 0) or hc == 0 else ())
            p.mark(f"b1A_mm{spar}", "pe")
            if hc >= 16 and hc % 2 == 1:
                p.mark(f"b1A_mmB{((hc - 16) // 2) % 2}", "pe")

        for hc in range(32):
            if hc % 4 == 0:
                hc4 = hc // 4
                bpar = hc4 % 2
                wsv4 = a3(a3(wst, 2, 4096)[:, bpar, :], 8, 512)
                p.dma("sync" if bpar == 0 else "gpsimd", wsv4,
                      wfq[:, :, hc4 * 512:(hc4 + 1) * 512],
                      wait=p.need(f"b1hg_mmB{bpar}") +
                      (p.req("qfc_st0", "qfc_st1")
                       if (g == 0 and hc4 < 2) else []))
                p.mark(f"b1wfB{bpar}", "dsp" if bpar == 0 else "dgp")
            hg_gelu(hc)
            if hc % 4 == 3:
                p.mark(f"b1hg_mmB{(hc // 4) % 2}", "pe")
            if hc >= 1:
                sq_part(hc - 1)
            if hc >= 2:
                stats(hc - 2)
            if hc >= 3:
                amm(hc - 3)
            if hc >= 16 and hc % 2 == 0:
                b2_ = (hc - 16) // 2
                bpar = b2_ % 2
                p.dma("sync" if bpar == 0 else "gpsimd", w2v4s[bpar],
                      w2q3[:, hc:hc + 2, :],
                      wait=p.need(f"b1A_mmB{bpar}") +
                      (b0gate if (g == 0 and b2_ < 2) else []) +
                      (p.req("qw2_st0", "qw2_st1")
                       if (g == 0 and b2_ < 2 and seq == 0) else []))
                p.mark(f"b1w2B{bpar}", "dsp" if bpar == 0 else "dgp")
        sq_part(31)
        stats(30)
        stats(31)
        amm(29)
        amm(30)
        amm(31)
        mrow = h.f32b[0:1, 0:256]
        vrow_ = h.f32b[0:1, 256:512]
        rrow_ = h.f32b[0:1, 512:768]
        p.ts("vector", mrow, pbank(h, 6, 1)[0:1, 0:256], 1.0 / 4096.0, None,
             ALU.mult, wait=p.req("b1stat31") + p.need("b1f32b_free"),
             inc="dve")
        p.bkr(6, 1, "dve")
        p.ts("vector", vrow_, pbank(h, 7, 1)[0:1, 0:256], 1.0 / 4096.0, LN_EPS,
             ALU.mult, ALU.add, inc="dve")
        p.bkr(7, 1, "dve")
        p.tt("vector", rrow_, mrow, mrow, ALU.mult, inc="dve")
        p.tt("vector", vrow_, vrow_, rrow_, ALU.subtract, inc="dve")
        p.mark("b1veps", "dve")
        p.act(rrow_, vrow_, AF.Ln, wait=p.req("b1veps"), inc="act")
        p.act(rrow_, rrow_, AF.Exp, scale=-0.5, inc="act")
        p.mark("b1r", "act")
        p.ts("vector", nmub, mrow, -1.0, None, ALU.mult, wait=p.req("b1r"),
             inc="dve")
        p.mark("b1nmu", "dve")
        for t2 in range(2):
            p.dma("sync", st[:, 140 + t2:141 + t2],
                  rrow_[0:1, t2 * 128:(t2 + 1) * 128],
                  wait=p.req("b1r") + p.need("b1rcols_used"))
        p.mark("b1rcols", "dsp")
        for t2 in range(2):
            for dh in range(2):
                p.mm(pbank(h, 2 + 2 * t2 + dh, 1),
                     nmub[0:1, t2 * 128:(t2 + 1) * 128],
                     urow[0:1, dh * 512:(dh + 1) * 512],
                     start=False, stop=True,
                     wait=p.req("b1nmu", "b1uvld", f"b1A_mm0", f"b1A_mm1")
                     if (t2 == 0 and dh == 0) else (), inc="pe")
        p.mark("b1rank1", "pe")
        for t2 in range(2):
            t = g * 2 + t2
            par = t2
            f32t = h.f32a
            for dh in range(2):
                p.ts("vector", f32t[:, dh * 512:(dh + 1) * 512],
                     pbank(h, 2 + 2 * t2 + dh, 1), st[:, 140 + t2:141 + t2],
                     None, ALU.mult,
                     wait=p.req("b1rank1", "b1rcols") + p.need("b1yfree"),
                     inc="dve")
                p.bkr(2 + 2 * t2 + dh, 1, "dve")
            p.mark("b1rcols_used", "dve")
            p.tt("vector", f32t[:, :], f32t[:, :], vbc[:, :], ALU.add,
                 wait=p.req("b1vbc"), inc="dve")
            p.mark("b1y", "dve")
            emit_postnorm_tile(p, h, f"{t0}e", t, par, f32t, p.req("b1y"),
                               tail_cb=tail_cb)
            p.mark("b1yfree", "dve")
        p.mark("b1A_done", "dve")
        p.mark("b1f32b_free", "dve")
    arn.free_("xn", "b1hg", "b1wfq", "b1w2r", "b1w2s", "b1vbc", "b1rows",
              "b1rows2")
    free_elt(h)
    p.mark("blkend", "dve")




def emit_all(p, h, blocks):
    emit_load_consts(p, h)
    arn_ap = h.arn.ap
    for seq in range(NSEQ):
        emit_load_stream(p, h, seq)
        cb1 = cb2 = cb3 = None
        if 0 in blocks:
            if 1 in blocks:
                cb1 = emit_prenorm_stats(p, h, f"b1_{seq}ln",
                                         arn_ap[:, 24832:25856])
            emit_l0_attn(p, h, seq, tail_cb=cb1)
        if 1 in blocks:
            if 2 in blocks:
                cb2 = emit_prenorm_stats(p, h, f"b2_{seq}ln",
                                         arn_ap[:, 24832:25856])
            emit_l0_ffn(p, h, seq, tail_cb=cb2, prelude=cb1 is not None)
        if 2 in blocks:
            if 3 in blocks:
                cb3 = emit_prenorm_stats(p, h, f"b3_{seq}ln",
                                         arn_ap[:, 30976:32000])
            emit_l1_attn(p, h, seq, tail_cb=cb3, prelude=cb2 is not None)
        if 3 in blocks:
            emit_l1_ffn(p, h, seq, prelude=cb3 is not None)
        emit_store_stream(p, h, seq)
    p.do("sync", lambda e: e.wait_ge(p.sems["dsp"], p.c["dsp"]))
    p.do("gpsimd", lambda e: e.wait_ge(p.sems["dsp"], p.c["dsp"]))


def build_program(blocks):
    nc = bass.Bass("TRN2", target_bir_lowering=False)
    with ExitStack() as ctx:
        h = build_handles(nc, ctx, blocks)
        sems = {}
        for name in ("pe", "act", "dve", "gp", "dsp", "dgp"):
            sems[name] = ctx.enter_context(nc.semaphore(name))
        block = ctx.enter_context(nc.Block())
        counts = {}

        def run(engname):
            p = Prog(nc, engname, sems)
            h.arn.reset()
            emit_all(p, h, blocks)
            if counts:
                assert counts == p.c, f"sem divergence {engname}: {counts} vs {p.c}"
            else:
                counts.update(p.c)

        block.gpsimd(lambda g: run("gpsimd"))
        block.scalar(lambda s: run("scalar"))
        block.tensor(lambda t: run("tensor"))
        block.vector(lambda v: run("vector"))
        block.sync(lambda sy: run("sync"))
    return nc


def kernel(**inputs):
    prep = host_prep(inputs)
    consts = host_consts()
    x = np.ascontiguousarray(inputs["L0_attn_x"], dtype=np.float32)
    for group in FUSE:
        nc = build_program(group)
        declared = set()
        for alloc in nc.m.functions[0].allocations:
            if getattr(alloc, "kind", None) == "ExternalInput":
                if alloc.memorylocations:
                    declared.add(alloc.memorylocations[0].name)
        base = {}
        base.update({k: v for k, v in prep.items() if k in declared})
        base.update({k: v for k, v in consts.items() if k in declared})
        in_maps = []
        for c in range(NCORE):
            m = dict(base)
            m["x_in"] = np.ascontiguousarray(x[2 * c:2 * c + 2])
            in_maps.append(m)
        res = run_bass_kernel_spmd(nc, in_maps, core_ids=list(range(NCORE)))
        x = np.concatenate([r["y_out"] for r in res.results], axis=0)
    return x.reshape(16, S, D)


# version 20 (final)
# speedup vs baseline: 1.0472x; 1.0472x over previous
import math
import numpy as np
import ml_dtypes
from contextlib import ExitStack

import concourse.bass as bass
import concourse.mybir as mybir
from concourse.bass_utils import run_bass_kernel_spmd

F32 = mybir.dt.float32
BF16 = mybir.dt.bfloat16
AF = mybir.ActivationFunctionType
ALU = mybir.AluOpType
AX = mybir.AxisListType

STRICT = False

P = 128
S = 2048
D = 1024
NT = S // P
NSEQ = 2
NCORE = 8
LN_EPS = 1e-5

FUSE = [(0, 1, 2, 3)]
DEBUG_B0 = False
STOP_AFTER = None
OP1_OPS = 99

ENG_SEM = {"tensor": "pe", "scalar": "act", "vector": "dve", "gpsimd": "gp",
           "sync": None}


class Prog:
    def __init__(self, nc, cur, sems):
        self.nc = nc
        self.cur = cur
        self.sems = sems
        self.c = {k: 0 for k in sems}
        self.waited = {}
        self.ready = {}
        self.eng = {"tensor": nc.tensor, "scalar": nc.scalar, "vector": nc.vector,
                    "gpsimd": nc.gpsimd, "sync": nc.sync}

    def _wait(self, e, wl):
        for s, v in wl:
            if v <= 0:
                continue
            if s in ("dsp", "dgp"):
                v = self.c[s]
            if self.waited.get(s, -1) >= v:
                continue
            e.wait_ge(self.sems[s], v)
            self.waited[s] = v

    def need(self, *tags):
        out = []
        for t in tags:
            if t in self.ready:
                out.append(self.ready[t])
        return out

    def req(self, *tags):
        for t in tags:
            if t not in self.ready:
                raise KeyError(f"mark {t} not set")
        return [self.ready[t] for t in tags]

    def mark(self, tag, sem=None):
        sem = sem or ENG_SEM.get(self.cur)
        self.ready[tag] = (sem, self.c[sem])

    def do(self, engname, fn, wait=(), inc=None, n=1):
        own = ENG_SEM[engname]
        if engname in ("scalar", "vector", "gpsimd"):
            wait = list(wait) + [(own, self.c[own])]
            if inc is None:
                inc = own
        elif engname == "tensor":
            if inc is None:
                inc = own
            if STRICT:
                wait = list(wait) + [(own, self.c[own])]
        if engname == self.cur:
            e = self.eng[engname]
            self._wait(e, wait)
            ins = fn(e)
            if inc is not None and ins is not None:
                ins.then_inc(self.sems[inc], n)
        if inc is not None:
            self.c[inc] += n

    def bkr(self, b0, nb, sem):
        for b in range(b0, b0 + nb):
            self.mark(f"bkR{b}", sem)

    def bkw(self, b0, nb):
        out = []
        for b in range(b0, b0 + nb):
            out += self.need(f"bkR{b}")
        return out

    def mm(self, out, lhsT, rhs, start, stop, wait=(), inc=None, lowp=False):
        def f(t):
            if lowp:
                with self.nc.allow_low_precision(reason="exact small-int matmul"):
                    return t.matmul(out, lhsT, rhs, start=start, stop=stop)
            return t.matmul(out, lhsT, rhs, start=start, stop=stop)
        self.do("tensor", f, wait=wait, inc=inc)

    def transp(self, out, in_, ident, wait=(), inc=None):
        def f(t):
            with self.nc.allow_low_precision(reason="bf16 transpose"):
                return t.transpose(out, in_, ident)
        self.do("tensor", f, wait=wait, inc=inc)

    def act(self, out, in_, func, bias=0.0, scale=1.0, accum_out=None, wait=(), inc=None):
        self.do("scalar", lambda s: s.activation(out, in_, func, bias=bias, scale=scale,
                                                 accum_out=accum_out),
                wait=wait, inc=inc)

    def dma(self, q, out, in_, wait=(), inc=None):
        inc = inc or ("dsp" if q == "sync" else "dgp")
        chain = [(inc, self.c[inc])]
        self.do(q, lambda e: e.dma_start(out=out, in_=in_), wait=list(wait) + chain,
                inc=inc, n=16)

    def tt(self, eng, out, a, b, op, wait=(), inc=None):
        self.do(eng, lambda v: v.tensor_tensor(out, a, b, op), wait=wait, inc=inc)

    def ttb(self, eng, out, a, scol, op, wait=(), inc=None):
        s_b, _ = bass.broadcast_tensor_aps(scol, a)
        self.do(eng, lambda v: v.tensor_tensor(out, a, s_b, op),
                wait=wait, inc=inc)

    def ts(self, eng, out, in0, s1, s2, op0, op1=None, accum_out=None, wait=(), inc=None):
        if op1 is None:
            self.do(eng, lambda v: v.tensor_scalar(out, in0, s1, None, op0,
                                                   accum_out=accum_out), wait=wait, inc=inc)
        else:
            self.do(eng, lambda v: v.tensor_scalar(out, in0, s1, s2, op0, op1,
                                                   accum_out=accum_out), wait=wait, inc=inc)

    def stt(self, eng, out, in0, scalar, in1, op0, op1, accum_out=None, wait=(), inc=None):
        self.do(eng, lambda v: v.scalar_tensor_tensor(out, in0, scalar, in1, op0, op1,
                                                      accum_out=accum_out),
                wait=wait, inc=inc)

    def copy(self, eng, out, in_, wait=(), inc=None):
        if eng == "scalar":
            self.do(eng, lambda s: s.copy(out, in_), wait=wait, inc=inc)
        else:
            self.do(eng, lambda v: v.tensor_copy(out, in_), wait=wait, inc=inc)

    def red(self, out, in_, op, negate=False, wait=(), inc=None, eng="vector"):
        self.do(eng, lambda v: v.tensor_reduce(out, in_, AX.X, op, negate=negate),
                wait=wait, inc=inc)

    def recip(self, out, in_, wait=(), inc=None):
        def f(v):
            with self.nc.allow_low_precision(reason="recip"):
                return v.reciprocal(out, in_)
        self.do("vector", f, wait=wait, inc=inc)

    def memset(self, eng, ap, val, wait=(), inc=None):
        self.do(eng, lambda e: e.memset(ap, val), wait=wait, inc=inc)

    def nop(self, eng, wait=(), inc=None):
        self.do(eng, lambda e: e.nop(), wait=wait, inc=inc)


class Arena:

    def __init__(self, ap, cap):
        self.ap = ap
        self.cap = cap
        self.live = {}

    def _gaps(self):
        spans = sorted(self.live.values())
        gaps = []
        pos = 0
        for o, c in spans:
            if o > pos:
                gaps.append((pos, o - pos))
            pos = max(pos, o + c)
        if pos < self.cap:
            gaps.append((pos, self.cap - pos))
        return gaps

    def alloc(self, tag, cols, align=64, top=None):
        assert tag not in self.live, tag
        cols = (cols + align - 1) // align * align
        if top is None:
            top = cols >= 8192
        gaps = self._gaps()
        it = reversed(gaps) if top else iter(gaps)
        for off, ln in it:
            if top:
                o = (off + ln - cols) // align * align
                if o < off:
                    continue
            else:
                o = (off + align - 1) // align * align
                if o + cols > off + ln:
                    continue
            self.live[tag] = (o, cols)
            return self.ap[:, o:o + cols]
        raise MemoryError(f"arena OOM for {tag} ({cols}); gaps={gaps}")

    def alloc_at(self, tag, off, cols):
        assert tag not in self.live, tag
        assert off % 64 == 0 and off + cols <= self.cap, (tag, off, cols)
        for t, (o, c) in self.live.items():
            assert off + cols <= o or o + c <= off, \
                f"overlap {tag}@[{off},{off+cols}) with {t}@[{o},{o+c})"
        self.live[tag] = (off, cols)
        return self.ap[:, off:off + cols]

    def free_(self, *tags):
        for tag in tags:
            self.live.pop(tag)

    def reset(self):
        self.live = {}


def a3(ap, a, b):
    return ap.rearrange("p (a b) -> p a b", a=a)



def host_prep(inp):
    C = lambda x: np.ascontiguousarray(x, dtype=np.float32)
    d = {}
    d["l0_wq"] = C(inp["L0_attn_wq"])
    d["l0_bq"] = C(inp["L0_attn_bq"].reshape(2, 128).T)
    d["l0_wqc"] = C(inp["L0_attn_wqc"])
    d["l0_bqc"] = C(inp["L0_attn_bqc"].reshape(7, 128).T)
    d["l0_wqr"] = C(inp["L0_attn_wqr"])
    d["l0_bqr"] = C(inp["L0_attn_bqr"].reshape(1, 128))
    d["l0_wkv"] = C(inp["L0_attn_wkv"])
    d["l0_bkv"] = C(inp["L0_attn_bkv"].reshape(2, 128).T)
    d["l0_wkr"] = C(inp["L0_attn_wkr"])
    d["l0_bkr"] = C(inp["L0_attn_bkr"].reshape(1, 128))
    d["l0_wkc"] = C(inp["L0_attn_wkc"])
    d["l0_bkc"] = C(inp["L0_attn_bkc"].reshape(7, 128).T)
    d["l0_wv"] = C(inp["L0_attn_wv"])
    d["l0_bv"] = C(inp["L0_attn_bv"].reshape(1, 1024))
    d["l0_wo"] = C(inp["L0_attn_wo"])
    d["l0_bo"] = C(inp["L0_attn_bo"].reshape(1, 1024))
    d["l0f_wfcT"] = C(inp["L0_ffn_W_fc"].T)
    d["l0f_w2T"] = C(inp["L0_ffn_W_proj"].T)
    d["l0f_lnw"] = C(inp["L0_ffn_ln_w"].reshape(32, 128).T)
    d["l0f_lnb"] = C(inp["L0_ffn_ln_b"].reshape(32, 128).T)
    wqkv = inp["L1_attn_w_qkv"]
    d["l1_wq"] = C(wqkv[:, 0:1024])
    d["l1_wk"] = C(wqkv[:, 1024:2048])
    d["l1_wv"] = C(wqkv[:, 2048:3072])
    d["l1_wo"] = C(inp["L1_attn_w_out"])
    d["l1_bo"] = C(inp["L1_attn_b_out"].reshape(1, 1024))
    d["l1_mask"] = C(inp["L1_attn_mask_logits"].reshape(33, 1))
    for t in ("q", "k", "v"):
        d[f"l1f_{t}_w1"] = C(inp[f"L1_ffn_{t}_w1"].transpose(1, 0, 2).reshape(1024, 1024))
        d[f"l1f_{t}_w2"] = C(inp[f"L1_ffn_{t}_w2"].reshape(1024, 1024))
        d[f"l1f_{t}_nw"] = C(inp[f"L1_ffn_{t}_nw" if False else f"L1_ffn_{t}_node_w"].T)
        d[f"l1f_{t}_nb"] = C(np.broadcast_to(
            inp[f"L1_ffn_{t}_node_b"].reshape(1, 15), (128, 15)))
        d[f"l1f_{t}_b1"] = C(inp[f"L1_ffn_{t}_b1"].reshape(1024).reshape(8, 128).T)
        d[f"l1f_{t}_b2"] = C(inp[f"L1_ffn_{t}_b2"])
    return d


def host_consts():
    C = lambda x: np.ascontiguousarray(x, dtype=np.float32)
    c = {}
    c["c_ident"] = np.eye(P, dtype=np.float32)
    DR = 128
    inv_freq = 1.0 / 10000.0 ** (np.arange(0, DR, 2, dtype=np.float64) / DR)
    t = np.arange(S, dtype=np.float64)
    freqs = t[:, None] * inv_freq[None, :]
    emb = np.concatenate([freqs, freqs], axis=-1)
    c["c_cos"] = C(np.cos(emb).reshape(NT, P, DR).transpose(1, 0, 2))
    c["c_sin"] = C(np.sin(emb).reshape(NT, P, DR).transpose(1, 0, 2))
    m = np.zeros((P, P), dtype=np.float32)
    for i in range(P):
        m[i, i:] = 1.0
    c["c_diag"] = m
    E = np.zeros((8, P, 16), dtype=np.float32)
    for cc in range(8):
        E[cc, 0:64, 2 * cc] = 1.0
        E[cc, 64:128, 2 * cc + 1] = 1.0
    c["c_ecol"] = C(E.transpose(1, 0, 2).reshape(P, 128))
    kk = np.arange(33, dtype=np.float64)[:, None]
    pp = np.arange(64, dtype=np.float64)[None, :]
    ang = 2.0 * np.pi * kk * pp / 64.0
    c["c_dftc"] = C(np.cos(ang))
    c["c_dfts"] = C(np.sin(ang))
    ck = np.full((33, 1), 2.0 / 64.0)
    ck[0, 0] = 1.0 / 64.0
    ck[32, 0] = 1.0 / 64.0
    c["c_dftk"] = C(ck)
    EL = np.zeros((16, 1024), dtype=np.float32)
    for l in range(16):
        EL[l, 64 * l:64 * (l + 1)] = 1.0
    c["c_eleaf"] = EL
    c["c_ones_row"] = np.ones((1, P), dtype=np.float32)
    return c



def declare_inputs(nc, blocks):
    d = {}

    def di(name, shape):
        d[name] = nc.dram_tensor(name, list(shape), F32, kind="ExternalInput").ap()

    di("x_in", (NSEQ, S, D))
    di("c_ident", (P, P))
    di("c_ones_row", (1, P))
    if 0 in blocks:
        di("l0_wq", (1024, 256)); di("l0_bq", (P, 2))
        di("l0_wqc", (256, 896)); di("l0_bqc", (P, 7))
        di("l0_wqr", (1024, 128)); di("l0_bqr", (1, 128))
        di("l0_wkv", (1024, 256)); di("l0_bkv", (P, 2))
        di("l0_wkr", (1024, 128)); di("l0_bkr", (1, 128))
        di("l0_wkc", (256, 896)); di("l0_bkc", (P, 7))
        di("l0_wv", (256, 1024)); di("l0_bv", (1, 1024))
        di("l0_wo", (1024, 1024)); di("l0_bo", (1, 1024))
        di("c_cos", (P, NT, 128)); di("c_sin", (P, NT, 128))
    if 0 in blocks or 2 in blocks:
        di("c_diag", (P, P)); di("c_ecol", (P, 128))
    if 1 in blocks:
        di("l0f_wfcT", (1024, 4096)); di("l0f_w2T", (4096, 1024))
        di("l0f_lnw", (P, 32)); di("l0f_lnb", (P, 32))
        d["wfcq_dram"] = nc.dram_tensor("wfcq_dram", [1024, 4096], BF16,
                                        kind="Internal").ap()
        d["w2q_dram"] = nc.dram_tensor("w2q_dram", [4096, 1024], BF16,
                                       kind="Internal").ap()
        d["uv_dram"] = nc.dram_tensor("uv_dram", [1, 2048], BF16,
                                      kind="Internal").ap()
    if 2 in blocks:
        di("l1_wq", (1024, 1024)); di("l1_wk", (1024, 1024)); di("l1_wv", (1024, 1024))
        di("l1_wo", (1024, 1024)); di("l1_bo", (1, 1024))
        di("l1_mask", (33, 1))
        di("c_dftc", (33, 64)); di("c_dfts", (33, 64)); di("c_dftk", (33, 1))
    if 3 in blocks:
        for t in ("q", "k", "v"):
            di(f"l1f_{t}_w1", (1024, 1024)); di(f"l1f_{t}_w2", (1024, 1024))
            di(f"l1f_{t}_nw", (1024, 15)); di(f"l1f_{t}_nb", (P, 15))
            di(f"l1f_{t}_b1", (P, 8)); di(f"l1f_{t}_b2", (16, 1024))
        di("c_eleaf", (16, 1024))
    d["y_out"] = nc.dram_tensor("y_out", [NSEQ, S, D], F32, kind="ExternalOutput").ap()
    if DEBUG_B0:
        for nm in ("dbg_q", "dbg_k", "dbg_o", "dbg_xn"):
            d[nm] = nc.dram_tensor(nm, [P, 8, 2048], F32,
                                   kind="ExternalOutput").ap()
        d["dbg_v"] = nc.dram_tensor("dbg_v", [P, 16, 1040], F32,
                                    kind="ExternalOutput").ap()
        d["dbg_cq"] = nc.dram_tensor("dbg_cq", [P, 2, 2048], F32,
                                     kind="ExternalOutput").ap()
    return d


class H:
    pass


ARN_COLS = 68096


def build_handles(nc, ctx, blocks):
    h = H()
    h.nc = nc
    h.dram = declare_inputs(nc, blocks)
    sb = lambda name, shape, dt: ctx.enter_context(nc.sbuf_tensor(name, shape, dt))
    h.stream = sb("stream", [P, NT, D], F32)
    h.arn_t = sb("arena", [P, ARN_COLS], BF16)
    h.arn = Arena(h.arn_t[:, :], ARN_COLS)
    h.st32 = sb("st32", [P, 384], F32)
    h.pt = sb("pt", [P, 4, 1024], BF16)
    h.ident = sb("ident", [P, P], BF16)
    h.diag = sb("diag", [P, P], BF16)
    h.ecol = sb("ecol", [P, P], BF16)
    h.ones_row_bf = sb("ones_row_bf", [1, P], BF16)
    h.ones64 = sb("ones64", [P, 64], BF16)
    h.onesf32 = sb("onesf32", [P, 1], F32)
    h.idf32 = sb("idf32", [1, 1], F32)
    h.cbias = sb("cbias", [P, 16], F32)
    h.pall = ctx.enter_context(nc.psum_tensor("pall", [P, 4096], F32))
    return h


def alloc_elt(h, off):
    buf = h.arn.alloc_at("eltbuf", off, 5120)
    h.f32a = buf[:, 0:2048].bitcast(F32)
    h.f32b = buf[:, 2048:4096].bitcast(F32)
    h.junk = buf[:, 4096:5120]


def free_elt(h):
    h.arn.free_("eltbuf")
    h.f32a = h.f32b = h.junk = None


def pbank(h, b0, nb):
    return h.pall[:, b0 * 512:(b0 + nb) * 512]


def pbank_bf(h, b0, nb):
    return h.pall[:, b0 * 512:(b0 + nb) * 512].bitcast(BF16)



def emit_load_consts(p, h):
    d = h.dram
    p.dma("gpsimd", h.ident[:, :], d["c_ident"])
    p.dma("gpsimd", h.ones_row_bf[:, :], d["c_ones_row"])
    p.memset("gpsimd", h.ones64[:, :], 1.0)
    if "c_diag" in d:
        p.dma("gpsimd", h.diag[:, :], d["c_diag"])
        p.dma("gpsimd", h.ecol[:, :], d["c_ecol"])
    p.memset("gpsimd", h.onesf32[:, :], 1.0)
    p.memset("gpsimd", h.idf32[:, :], 1.0)
    p.mark("consts_gp", "gp")
    p.mark("consts", "dgp")


def emit_load_stream(p, h, seq):
    src = h.dram["x_in"][seq].rearrange("(t p) d -> p t d", p=P)
    for t in range(NT):
        q = "sync" if t < NT // 2 else "gpsimd"
        p.dma(q, h.stream[:, t, :], src[:, t, :], wait=p.need(f"strm_st{t}"))
        p.mark(f"strm{t}", "dsp" if q == "sync" else "dgp")
    p.mark("stream_sp", "dsp")
    p.mark("stream_gp", "dgp")
    p.mark("stream", "dgp")


def emit_store_stream(p, h, seq):
    dst = h.dram["y_out"][seq].rearrange("(t p) d -> p t d", p=P)
    for t in range(NT):
        q = "sync" if t < NT // 2 else "gpsimd"
        p.dma(q, dst[:, t, :], h.stream[:, t, :], wait=p.req(f"strm{t}"))
        p.mark(f"strm_st{t}", "dsp" if q == "sync" else "dgp")
    p.mark("stored", "dsp")


def stream_ready(p):
    return p.need("stream_sp", "stream_gp", "stream")


def load_w_fm(p, h, tag, dram_name, kc, m, q="gpsimd", extra_wait=(), off=None):
    buf = h.arn.alloc(tag, kc * m) if off is None else \
        h.arn.alloc_at(tag, off, kc * m)
    v = a3(buf, kc, m)
    p.dma(q, v, h.dram[dram_name].rearrange("(c p) m -> p c m", p=P), wait=extra_wait)
    p.mark(tag, "dsp" if q == "sync" else "dgp")
    return v


def load_row_bf(p, h, tag, dram_name, n, q="gpsimd", extra_wait=(), off=None):
    buf = h.arn.alloc(tag, max(n, 64)) if off is None else \
        h.arn.alloc_at(tag, off, max(n, 64))
    v = buf[0:1, 0:n]
    p.dma(q, v, h.dram[dram_name], wait=extra_wait)
    p.mark(tag, "dsp" if q == "sync" else "dgp")
    return v


def emit_prenorm_stats(p, h, tag, junk_bf):
    st = h.st32
    sums = st[:, 0:16]; ssq = st[:, 16:32]; mu = st[:, 32:48]
    nmu2 = st[:, 48:64]; veps = st[:, 64:80]; rstd = st[:, 80:96]
    negmur = st[:, 96:112]

    def cb(t):
        wt = p.req(f"strm{t}")
        p.red(sums[:, t:t + 1], h.stream[:, t, :], ALU.add, wait=wt,
              inc="dve")
        p.act(junk_bf[:, :], h.stream[:, t, :], AF.Square,
              accum_out=ssq[:, t:t + 1], wait=wt, inc="act")
        if t % 4 != 3:
            return
        q4 = t // 4
        cs = slice(4 * q4, 4 * q4 + 4)
        p.mark(f"{tag}_red{q4}", "dve"); p.mark(f"{tag}_sq{q4}", "act")
        wv = p.req(f"{tag}_red{q4}", f"{tag}_sq{q4}")
        p.ts("vector", mu[:, cs], sums[:, cs], 1.0 / D, None, ALU.mult,
             wait=wv, inc="dve")
        p.tt("vector", nmu2[:, cs], mu[:, cs], mu[:, cs], ALU.mult, inc="dve")
        p.ts("vector", veps[:, cs], ssq[:, cs], 1.0 / D, LN_EPS,
             ALU.mult, ALU.add, inc="dve")
        p.tt("vector", veps[:, cs], veps[:, cs], nmu2[:, cs], ALU.subtract,
             inc="dve")
        p.mark(f"{tag}_v{q4}", "dve")
        p.act(rstd[:, cs], veps[:, cs], AF.Ln, wait=p.req(f"{tag}_v{q4}"),
              inc="act")
        p.act(rstd[:, cs], rstd[:, cs], AF.Exp, scale=-0.5, inc="act")
        p.mark(f"{tag}_rstd{q4}", "act")
        p.stt("vector", negmur[:, cs], mu[:, cs], -1.0, rstd[:, cs],
              ALU.mult, ALU.mult, wait=p.req(f"{tag}_rstd{q4}"), inc="dve")
        p.mark(f"{tag}_nm{q4}", "dve")
    return cb


def emit_prenorm_fm(p, h, xnfm, tag, lo_cb=None, junk_ap=None,
                    prelude=False):
    st = h.st32
    sums = st[:, 0:16]; ssq = st[:, 16:32]; mu = st[:, 32:48]
    nmu2 = st[:, 48:64]; veps = st[:, 64:80]; rstd = st[:, 80:96]
    negmur = st[:, 96:112]
    cb = None
    if not prelude:
        jnk = h.junk if junk_ap is None else junk_ap
        cb = emit_prenorm_stats(p, h, tag, jnk)
    for t in range(NT):
        if cb is not None and t % 4 == 0:
            for tq in range(t, t + 4):
                cb(tq)
        par = t % 2
        f32t = h.f32a if par == 0 else h.f32b
        hi_tm = h.pt[:, 2 * par, :]
        pb = pbank_bf(h, 6 + par, 1)
        if lo_cb is None:
            p.act(hi_tm, h.stream[:, t, :], AF.Identity,
                  bias=negmur[:, t:t + 1], scale=rstd[:, t:t + 1],
                  wait=p.need(f"{tag}_rstd{t // 4}", f"{tag}_nm{t // 4}",
                              f"{tag}_cast{par}", f"{tag}_tp{par}"),
                  inc="act")
            p.mark(f"{tag}_cast{par}", "act")
        else:
            p.act(f32t[:, :], h.stream[:, t, :], AF.Identity,
                  bias=negmur[:, t:t + 1], scale=rstd[:, t:t + 1],
                  wait=p.need(f"{tag}_rstd{t // 4}", f"{tag}_nm{t // 4}",
                              f"{tag}_cast{par}"), inc="act")
            p.mark(f"{tag}_f32{par}", "act")
            p.copy("vector", hi_tm, f32t[:, :],
                   wait=p.need(f"{tag}_f32{par}", f"{tag}_tp{par}"),
                   inc="dve")
            lo_tm = h.pt[:, 2 * par + 1, :]
            p.stt("vector", lo_tm, hi_tm, -1.0, f32t[:, :], ALU.mult, ALU.add,
                  inc="dve")
            p.mark(f"{tag}_cast{par}", "dve")
        for c in range(8):
            p.transp(pb[:, c * 128:(c + 1) * 128], hi_tm[:, c * 128:(c + 1) * 128],
                     h.ident[:, :],
                     wait=(p.need(f"{tag}_cast{par}", "consts") + p.bkw(6 + par, 1)
                           if c == 0 else ()),
                     inc="pe" if c == 7 else None)
        p.mark(f"{tag}_tp{par}", "pe")
        p.copy("vector", xnfm[:, :, t * 128:(t + 1) * 128],
               pb.rearrange("p (c n) -> p c n", c=8),
               wait=p.req(f"{tag}_tp{par}"), inc="dve")
        p.bkr(6 + par, 1, "dve")
        p.mark(f"{tag}_pbf{par}", "dve")
        if lo_cb is not None:
            lo_cb(p, h, t, par)
        if t % 4 == 3:
            p.mark(f"{tag}q{t // 4}", "dve")
    p.mark(tag, "dve")


def emit_proj_fm(p, h, tag, dst_fm, lhsT_tiles, rhs_fm, n_kc, out_chunks,
                 act_func, bias_cols=None, rhs_ready=(), w_ready=(),
                 banks=(0, 2), scale=1.0, oc_base=0, rhs_ready_nq=None):
    it = 0
    for oc in range(oc_base, oc_base + out_chunks):
        for nq in range(4):
            par = it % 2
            ps = pbank(h, banks[par], 1)
            rr = rhs_ready_nq[nq] if rhs_ready_nq is not None else rhs_ready
            wl = list(rr) + list(w_ready) + p.bkw(banks[par], 1)
            for kc in range(n_kc):
                p.mm(ps, lhsT_tiles(kc, oc), rhs_fm[:, kc, nq * 512:(nq + 1) * 512],
                     start=(kc == 0), stop=(kc == n_kc - 1),
                     wait=wl if kc == 0 else (),
                     inc="pe" if kc == n_kc - 1 else None)
            p.mark(f"{tag}_mm{par}", "pe")
            b = bias_cols[:, oc:oc + 1] if bias_cols is not None else 0.0
            p.act(dst_fm[:, oc, nq * 512:(nq + 1) * 512], ps, act_func,
                  bias=b, scale=scale, wait=p.req(f"{tag}_mm{par}"), inc="act")
            p.bkr(banks[par], 1, "act")
            p.mark(f"{tag}_evac{par}", "act")
            it += 1
    p.mark(tag, "act")


def emit_l0_attn(p, h, seq, tail_cb=None, prelude=False):
    arn = h.arn
    d = h.dram
    t0 = f"b0_{seq}"

    prev = p.need("blkend")
    alloc_elt(h, 49408)
    cosb = a3(arn.alloc_at("ropec", 16384, 16 * 128), 16, 128)
    sinb = a3(arn.alloc_at("ropes", 18432, 16 * 128), 16, 128)
    p.dma("gpsimd", cosb, d["c_cos"], wait=prev)
    p.dma("gpsimd", sinb, d["c_sin"], wait=prev)
    p.mark("rope_tab", "dgp")

    xn = a3(arn.alloc_at("xn", 0, 8 * 2048), 8, 2048)
    emit_prenorm_fm(p, h, xn, t0 + "ln", prelude=prelude)
    wxn = p.req(t0 + "ln")
    wxn_nq = [p.req(f"{t0}lnq{q_}") for q_ in range(4)]
    if STOP_AFTER == "ln":
        p.mark("stream", "dve"); p.mark("blkend", "dve"); arn.reset(); return
    if DEBUG_B0 and seq == 0:
        p.dma("gpsimd", h.dram["dbg_xn"], xn, wait=wxn)

    wq = load_w_fm(p, h, "wq", "l0_wq", 8, 256, extra_wait=prev, off=20480)
    wkv = load_w_fm(p, h, "wkv", "l0_wkv", 8, 256, extra_wait=prev, off=22528)
    wqr = load_w_fm(p, h, "wqr", "l0_wqr", 8, 128, extra_wait=prev, off=24576)
    wkr = load_w_fm(p, h, "wkr", "l0_wkr", 8, 128, extra_wait=prev, off=25600)
    bqr = load_row_bf(p, h, "bqr", "l0_bqr", 128, extra_wait=prev, off=26624)
    bkr = load_row_bf(p, h, "bkr", "l0_bkr", 128, extra_wait=prev, off=26752)
    p.dma("sync", h.st32[:, 128:130], d["l0_bq"])
    p.dma("sync", h.st32[:, 130:132], d["l0_bkv"])
    p.mark("bq_bkv", "dsp")
    cq = a3(arn.alloc_at("cq", 54528, 2 * 2048), 2, 2048)
    ckv = a3(arn.alloc_at("ckv", 58624, 2 * 2048), 2, 2048)
    emit_proj_fm(p, h, f"cq{seq}", cq,
                 lambda kc, oc: wq[:, kc, oc * 128:(oc + 1) * 128],
                 xn, 8, 2, AF.Gelu, bias_cols=h.st32[:, 128:130],
                 rhs_ready_nq=wxn_nq, w_ready=p.req("wq", "bq_bkv"))
    emit_proj_fm(p, h, f"ckv{seq}", ckv,
                 lambda kc, oc: wkv[:, kc, oc * 128:(oc + 1) * 128],
                 xn, 8, 2, AF.Gelu, bias_cols=h.st32[:, 130:132],
                 rhs_ready_nq=wxn_nq, w_ready=p.req("wkv", "bq_bkv"))

    if DEBUG_B0 and seq == 0:
        p.dma("gpsimd", h.dram["dbg_cq"], cq, wait=p.req(f"cq{seq}"))
    qr = a3(arn.alloc_at("qr", 62720, 16 * 128), 16, 128)
    kr = a3(arn.alloc_at("kr", 64768, 16 * 128), 16, 128)
    for (nm, wtag, btag, wr, br, dst) in (("qr", "wqr", "bqr", wqr, bqr, qr),
                                          ("kr", "wkr", "bkr", wkr, bkr, kr)):
        for t in range(NT):
            par = t % 2
            ps = pbank(h, 4 + par, 1)[:, 0:128]
            wl = wxn_nq[t // 4] + p.req(wtag, btag) + p.bkw(4 + par, 1)
            for kc in range(8):
                p.mm(ps, xn[:, kc, t * 128:(t + 1) * 128], wr[:, kc, :],
                     start=(kc == 0), stop=False,
                     wait=wl if kc == 0 else ())
            p.mm(ps, h.ones_row_bf[:, :], br, start=False, stop=True, inc="pe")
            p.mark(f"rope_mm{par}", "pe")
            f32t = h.f32a if par == 0 else h.f32b
            wmm = p.req(f"rope_mm{par}") + p.need("rope_tab", f"rope_f32{par}")
            p.tt("vector", f32t[:, 0:128], ps, cosb[:, t, :], ALU.mult,
                 wait=wmm, inc="dve")
            p.ts("vector", f32t[:, 128:192], ps[:, 64:128], -1.0, None, ALU.mult,
                 inc="dve")
            p.copy("vector", f32t[:, 192:256], ps[:, 0:64], inc="dve")
            p.bkr(4 + par, 1, "dve")
            p.tt("vector", f32t[:, 128:256], f32t[:, 128:256], sinb[:, t, :],
                 ALU.mult, inc="dve")
            p.tt("vector", dst[:, t, :], f32t[:, 0:128], f32t[:, 128:256], ALU.add,
                 inc="dve")
            p.mark(f"rope_ps{par}", "dve")
            p.mark(f"rope_f32{par}", "dve")
    p.mark("rope_done", "dve")
    if STOP_AFTER == "stageA":
        p.mark("stream", "dve"); p.mark("blkend", "dve"); arn.reset(); return
    p.mark("stageA_pe", "pe")
    arn.free_("wq", "wkv", "wqr", "wkr", "bqr", "bkr", "xn", "ropec", "ropes")
    stageA_done = p.req("rope_done", "stageA_pe")

    wv = load_w_fm(p, h, "wv", "l0_wv", 2, 1024, extra_wait=stageA_done, off=20480)
    bv = load_row_bf(p, h, "bv", "l0_bv", 1024, extra_wait=stageA_done, off=22528)
    vaug = a3(arn.alloc_at("vaug", 0, 16 * 1040), 16, 1040)
    vones = vaug.rearrange("p t (hd c) -> p t hd c", hd=16)[:, :, :, 64:65]
    p.memset("gpsimd", vones, 1.0, wait=stageA_done, inc="gp")
    p.mark("vones", "gp")
    for t in range(NT):
        par = t % 2
        ps = pbank(h, 2 * par, 2)[:, 0:1024]
        wl = p.req(f"ckv{seq}", "wv", "bv") + p.bkw(2 * par, 2)
        for kc in range(2):
            for nn in range(2):
                p.mm(ps[:, nn * 512:(nn + 1) * 512],
                     ckv[:, kc, t * 128:(t + 1) * 128],
                     wv[:, kc, nn * 512:(nn + 1) * 512],
                     start=(kc == 0), stop=False,
                     wait=wl if (kc == 0 and nn == 0) else ())
        for nn in range(2):
            p.mm(ps[:, nn * 512:(nn + 1) * 512], h.ones_row_bf[:, :],
                 bv[:, nn * 512:(nn + 1) * 512], start=False, stop=True,
                 inc="pe" if nn == 1 else None)
        p.mark(f"vmm{par}", "pe")
        p.copy("vector",
               vaug[:, t, :].rearrange("p (hd c) -> p hd c", hd=16)[:, :, 0:64],
               ps.rearrange("p (hd c) -> p hd c", hd=16),
               wait=p.req(f"vmm{par}"), inc="dve")
        p.bkr(2 * par, 2, "dve")
        p.mark(f"vevac{par}", "dve")
    p.mark("vaug_done", "dve")
    arn.free_("wv", "bv")
    if STOP_AFTER == "V":
        p.mark("stream", "dve"); p.mark("blkend", "dve"); arn.reset(); return

    wqc = load_w_fm(p, h, "wqc", "l0_wqc", 2, 896, extra_wait=stageA_done, off=33024)
    p.dma("sync", h.st32[:, 132:139], d["l0_bqc"])
    p.mark("bqc", "dsp")
    Q = a3(arn.alloc_at("Q", 16640, 8 * 2048), 8, 2048)
    emit_proj_fm(p, h, f"qc{seq}", Q,
                 lambda kc, oc: wqc[:, kc, oc * 128:(oc + 1) * 128],
                 cq, 2, 7, AF.Gelu, bias_cols=h.st32[:, 132:139],
                 rhs_ready=p.req(f"cq{seq}") + p.need("vaug_done"),
                 w_ready=p.req("wqc", "bqc"))
    emit_rtile_gelu(p, h, f"qrt{seq}", qr, Q, 7, src_ready=p.req("rope_done"))
    arn.free_("wqc", "cq", "qr")
    if STOP_AFTER == "Q":
        p.mark("stream", "dve"); p.mark("blkend", "dve"); arn.reset(); return

    wkc = load_w_fm(p, h, "wkc", "l0_wkc", 2, 896,
                    extra_wait=p.req(f"qc{seq}", f"qrt{seq}"), off=54528)
    p.dma("sync", h.st32[:, 139:146], d["l0_bkc"])
    p.mark("bkc", "dsp")
    K = a3(arn.alloc_at("K", 33024, 8 * 2048), 8, 2048)
    emit_proj_fm(p, h, f"kc{seq}", K,
                 lambda kc, oc: wkc[:, kc, oc * 128:(oc + 1) * 128],
                 ckv, 2, 7, AF.Gelu, bias_cols=h.st32[:, 139:146],
                 rhs_ready=p.req(f"ckv{seq}", f"qc{seq}"),
                 w_ready=p.req("wkc", "bkc"))
    emit_rtile_gelu(p, h, f"krt{seq}", kr, K, 7, src_ready=p.req("rope_done"))
    arn.free_("wkc", "ckv", "kr")
    if STOP_AFTER == "K":
        p.mark("stream", "dve"); p.mark("blkend", "dve"); arn.reset(); return

    qk_ready = p.req(f"qc{seq}", f"qrt{seq}", f"kc{seq}", f"krt{seq}")
    if DEBUG_B0 and seq == 0:
        p.dma("gpsimd", h.dram["dbg_q"], Q, wait=qk_ready)
        p.dma("gpsimd", h.dram["dbg_k"], K, wait=qk_ready)
        p.dma("gpsimd", h.dram["dbg_v"], vaug, wait=qk_ready)
    free_elt(h)
    O = a3(arn.alloc_at("O", 49408, 8 * 2048), 8, 2048)
    emit_attention(p, h, f"at0_{seq}", Q, K, vaug, O, causal=True, scale=0.125,
                   qk_ready=qk_ready)
    arn.free_("Q", "K", "vaug")
    if STOP_AFTER == "att":
        p.mark("stream", "dve"); p.mark("blkend", "dve"); arn.reset(); return

    att_done = p.req(f"at0_{seq}_odone")
    if STOP_AFTER == "att2":
        p.mark("stream", "dve"); p.mark("blkend", "dve"); arn.reset(); return
    if DEBUG_B0 and seq == 0:
        p.dma("gpsimd", h.dram["dbg_o"], O,
              wait=att_done + p.req(f"at0_{seq}_oshifts"))
    alloc_elt(h, 9216)
    wo = load_w_fm(p, h, "wo", "l0_wo", 8, 1024, extra_wait=att_done, off=0)
    bo = load_row_bf(p, h, "bo", "l0_bo", 1024, extra_wait=att_done, off=8192)
    if STOP_AFTER == "wo":
        p.mark("stream", "dve"); p.mark("blkend", "dve"); arn.reset(); return
    emit_outproj_postnorm(p, h, f"b0o{seq}", O, wo, bo, tail_cb=tail_cb,
                          o_ready=p.req(f"at0_{seq}_odone", f"at0_{seq}_oshifts"),
                          w_ready=p.req("wo", "bo"),
                          ntiles=(1 if STOP_AFTER == "op1" else NT))
    arn.free_("O", "wo", "bo")
    free_elt(h)
    p.mark("blkend", "dve")


def emit_rtile_gelu(p, h, tag, r_tm, dst_fm, oc, src_ready):
    for t in range(NT):
        par = t % 2
        pb = pbank_bf(h, 4 + par, 1)[:, 0:128]
        p.transp(pb, r_tm[:, t, :], h.ident[:, :],
                 wait=list(src_ready) + p.need("consts") + p.bkw(4 + par, 1),
                 inc="pe")
        p.mark(f"{tag}_tp{par}", "pe")
        p.act(dst_fm[:, oc, t * 128:(t + 1) * 128], pb, AF.Gelu,
              wait=p.req(f"{tag}_tp{par}"), inc="act")
        p.bkr(4 + par, 1, "act")
        p.mark(f"{tag}_ev{par}", "act")
    p.mark(tag, "act")


def emit_attention(p, h, tag, Q, K, vaug, O, causal, scale, qk_ready):
    otmp = h.arn.alloc(tag + "_otmp", 2048, top=False)
    sqs = [h.pt[:, 0:2, :].rearrange("p a b -> p (a b)"),
           h.pt[:, 2:4, :].rearrange("p a b -> p (a b)")]
    qn = h.st32[:, 146:148]
    for (idx, X, xt) in ((0, Q, "q"), (1, K, "k")):
        ps = pbank(h, 0, 4)[0:16, 0:2048]

        def nc_mms(c):
            par = c % 2
            for nq in range(4):
                p.mm(ps[:, nq * 512:(nq + 1) * 512],
                     h.ecol[:, c * 16:(c + 1) * 16],
                     sqs[par][:, nq * 512:(nq + 1) * 512],
                     start=(c == 0), stop=(c == 7),
                     wait=(p.req(f"{tag}_sqc{par}") + p.need("consts") +
                           (p.bkw(0, 4) if c == 0 else []))
                     if nq == 0 else (),
                     inc="pe" if nq == 3 else None)
            p.mark(f"{tag}_nc{par}", "pe")

        for c in range(8):
            par = c % 2
            p.tt("gpsimd", sqs[par], X[:, c, :], X[:, c, :], ALU.mult,
                 wait=list(qk_ready) + p.need(f"{tag}_nc{par}"), inc="gp")
            p.mark(f"{tag}_sqc{par}", "gp")
            if c > 0:
                nc_mms(c - 1)
        nc_mms(7)
        p.mark(f"{tag}_norm{xt}", "pe")
        p.red(qn[0:16, idx:idx + 1], ps, ALU.max,
              wait=p.req(f"{tag}_norm{xt}"), inc="dve")
        p.bkr(0, 4, "dve")
        p.mark(f"{tag}_nmax{xt}", "dve")
    cb = h.st32[0:16, 148:149]
    cb2 = h.st32[0:16, 149:150]
    p.act(cb, qn[0:16, 0:1], AF.Ln, wait=p.req(f"{tag}_nmaxq", f"{tag}_nmaxk"),
          inc="act")
    p.act(cb2, qn[0:16, 1:2], AF.Ln, inc="act")
    p.mark(f"{tag}_lncb", "act")
    p.tt("vector", cb, cb, cb2, ALU.add, wait=p.req(f"{tag}_lncb"), inc="dve")
    p.mark(f"{tag}_cb1", "dve")
    p.act(cb, cb, AF.Exp, scale=0.5, wait=p.req(f"{tag}_cb1"), inc="act")
    p.mark(f"{tag}_cb2", "act")
    p.ts("vector", cb, cb, -scale, None, ALU.mult, wait=p.req(f"{tag}_cb2"),
         inc="dve")
    p.copy("vector", h.pt[0:16, 0, 0:1], cb, inc="dve")
    p.mark(f"{tag}_cbb", "dve")
    pb4 = pbank_bf(h, 4, 1)
    p.transp(pb4[0:1, 0:16], h.pt[0:16, 0, 0:1], h.ident[0:16, 0:16],
             wait=p.req(f"{tag}_cbb") + p.bkw(4, 1), inc="pe")
    p.mark(f"{tag}_cbT", "pe")
    p.copy("vector", h.pt[0:1, 0, 16:32], pb4[0:1, 0:16],
           wait=p.req(f"{tag}_cbT"), inc="dve")
    p.bkr(4, 1, "dve")
    p.mark(f"{tag}_cbrow", "dve")
    ps5 = pbank(h, 5, 1)
    for hd in range(16):
        p.mm(ps5[:, hd:hd + 1], h.ones_row_bf[:, :], h.pt[0:1, 0, 16 + hd:17 + hd],
             start=True, stop=True,
             wait=(p.req(f"{tag}_cbrow") + p.bkw(5, 1)) if hd == 0 else (),
             inc="pe" if hd == 15 else None)
    p.mark(f"{tag}_cbcast", "pe")
    p.copy("vector", h.cbias[:, 0:16], ps5[:, 0:16], wait=p.req(f"{tag}_cbcast"),
           inc="dve")
    p.bkr(5, 1, "dve")
    p.mark(f"{tag}_cbias", "dve")

    strips = []
    for kt in range(NT):
        for qh in range(2):
            q0, q1 = qh * 1024, (qh + 1) * 1024
            if causal and q1 <= kt * 128:
                continue
            qa = max(q0, kt * 128) if causal else q0
            strips.append((kt, qh, qa, q1 - qa))
    last_pv = {}
    for si, (kt, qh, qa, n) in enumerate(strips):
        off = 0
        while off < n:
            col = qa + off
            nn = min(512 - (col % 512), n - off)
            last_pv[col // 512] = (si, off)
            off += nn
    for hd in range(16):
        ch = hd // 2
        rb = 64 * (hd % 2)
        stb = 0
        first_pv = {}
        po = pbank(h, 4, 4)[0:65, 0:2048]
        def emit_strip(si):
            kt, qh, qa, n = strips[si]
            q0, q1 = qh * 1024, (qh + 1) * 1024
            par = si % 2
            ps = pbank(h, 2 * par, 2)[:, 0:1024]
            ptbuf = h.pt[:, par, :]
            wl = list(qk_ready) + p.bkw(2 * par, 2)
            off = 0
            while off < n:
                nn = min(512, n - off)
                last = off + nn >= n
                p.mm(ps[:, off:off + nn],
                     K[rb:rb + 64, ch, kt * 128:(kt + 1) * 128],
                     Q[rb:rb + 64, ch, qa + off:qa + off + nn],
                     start=True, stop=True,
                     wait=wl if off == 0 else (),
                     inc="pe" if last else None)
                off += nn
            p.mark(f"{tag}_s{par}", "pe")
            p.act(ptbuf[:, 0:n], ps[:, 0:n], AF.Exp,
                  bias=h.cbias[:, hd:hd + 1], scale=scale,
                  wait=p.req(f"{tag}_s{par}", f"{tag}_cbias") +
                  p.need(f"{tag}_pv{par}"), inc="act")
            p.bkr(2 * par, 2, "act")
            p.mark(f"{tag}_exp{par}", "act")
            if causal and q0 <= kt * 128 < q1:
                doff = kt * 128 - qa
                p.tt("vector", ptbuf[:, doff:doff + 128],
                     ptbuf[:, doff:doff + 128], h.diag[:, :], ALU.mult,
                     wait=p.req(f"{tag}_exp{par}") + p.need("consts"),
                     inc="dve")
                p.mark(f"{tag}_dm{par}", "dve")

        rrow = h.pt[64:65, 2:4, :].rearrange("p a b -> p (a b)")

        def emit_pv(si):
            kt, qh, qa, n = strips[si]
            q0, q1 = qh * 1024, (qh + 1) * 1024
            par = si % 2
            ptbuf = h.pt[:, par, :]
            expw = (p.req(f"{tag}_dm{par}")
                    if causal and q0 <= kt * 128 < q1
                    else p.req(f"{tag}_exp{par}"))
            off = 0
            while off < n:
                col = qa + off
                nn = min(512 - (col % 512), n - off)
                blk = col // 512
                st_ = first_pv.get(blk, True)
                first_pv[blk] = False
                last = off + nn >= n
                stop_ = last_pv[blk] == (si, off)
                p.mm(po[:, col:col + nn],
                     vaug[:, kt, hd * 65:hd * 65 + 65],
                     ptbuf[:, off:off + nn],
                     start=st_, stop=stop_,
                     wait=(expw + p.need("vaug_done", "vones",
                                         f"{tag}_cbias") + p.bkw(4, 4))
                     if off == 0 else (),
                     inc="pe" if last else None)
                off += nn
            p.mark(f"{tag}_pv{par}", "pe")

        for si in range(len(strips)):
            emit_strip(si)
            if si > 0:
                emit_pv(si - 1)
        emit_pv(len(strips) - 1)
        p.mark(f"{tag}_pvh", "pe")
        p.act(po[64:65, 0:2048], po[64:65, 0:2048], AF.Ln,
              wait=p.req(f"{tag}_pvh"), inc="act")
        p.mark(f"{tag}_lnr", "act")
        p.act(rrow, po[64:65, 0:2048], AF.Exp, scale=-1.0,
              wait=p.req(f"{tag}_lnr"), inc="act")
        p.mark(f"{tag}_rr", "act")
        pbA = pbank(h, 0, 2)[0:64, 0:1024]
        pbB = pbank(h, 2, 2)[0:64, 0:1024]
        for half, px in ((0, pbA), (1, pbB)):
            for nn in range(2):
                p.mm(px[:, nn * 512:(nn + 1) * 512],
                     h.ones64[64:65, 0:64],
                     rrow[0:1, half * 1024 + nn * 512:half * 1024 + (nn + 1) * 512],
                     start=True, stop=True,
                     wait=(p.req(f"{tag}_rr", "consts_gp") + p.bkw(0, 4))
                     if (half == 0 and nn == 0) else (),
                     inc="pe" if (half == 1 and nn == 1) else None)
        p.mark(f"{tag}_rbc", "pe")
        bcsA = h.pt[0:64, 2, :]
        bcsB = h.pt[0:64, 3, :]
        p.act(bcsA, pbA, AF.Identity, wait=p.req(f"{tag}_rbc"), inc="act")
        p.act(bcsB, pbB, AF.Identity, inc="act")
        p.bkr(0, 4, "act")
        p.mark(f"{tag}_bcs", "act")
        if rb == 0:
            dstA = O[0:64, ch, 0:1024]
            dstB = O[0:64, ch, 1024:2048]
        else:
            dstA = otmp[0:64, 0:1024]
            dstB = otmp[0:64, 1024:2048]
        p.tt("vector", dstA, po[0:64, 0:1024], bcsA, ALU.mult,
             wait=p.req(f"{tag}_bcs") + p.need(f"{tag}_oshift"), inc="dve")
        p.tt("vector", dstB, po[0:64, 1024:2048], bcsB, ALU.mult, inc="dve")
        p.bkr(4, 4, "dve")
        p.mark(f"{tag}_oevac", "dve")
        if rb != 0:
            p.dma("sync", O[64:128, ch, :], otmp[0:64, :],
                  wait=p.req(f"{tag}_oevac"))
            p.mark(f"{tag}_oshift", "dsp")
    h.arn.free_(tag + "_otmp")
    p.mark(f"{tag}_odone", "dve")
    p.mark(f"{tag}_oshifts", "dsp")


def emit_outproj_postnorm(p, h, tag, O, wo, bo, o_ready, w_ready, ntiles=NT,
                          tail_cb=None):
    st = h.st32
    ysum = st[:, 160:176]; yssq = st[:, 176:192]; mu = st[:, 192:208]
    veps = st[:, 208:224]; rstd = st[:, 224:240]; nmu2 = st[:, 240:256]
    negmur = st[:, 256:272]
    for t in range(ntiles):
        par = t % 2
        ps = pbank(h, 2 * par, 2)[:, 0:1024]
        wl = list(o_ready) + list(w_ready) + p.bkw(2 * par, 2)
        for oc in range(8):
            for nn in range(2):
                p.mm(ps[:, nn * 512:(nn + 1) * 512],
                     O[:, oc, t * 128:(t + 1) * 128],
                     wo[:, oc, nn * 512:(nn + 1) * 512],
                     start=(oc == 0), stop=False,
                     wait=wl if (oc == 0 and nn == 0) else ())
        for nn in range(2):
            p.mm(ps[:, nn * 512:(nn + 1) * 512], h.ones_row_bf[:, :],
                 bo[:, nn * 512:(nn + 1) * 512], start=False, stop=True,
                 inc="pe" if nn == 1 else None)
        p.mark(f"{tag}_mm{par}", "pe")
        if OP1_OPS < 2: continue
        p.red(ysum[:, t:t + 1], ps, ALU.add, wait=p.req(f"{tag}_mm{par}"),
              inc="dve")
        p.mark(f"{tag}_red{par}", "dve")
        if OP1_OPS < 3: continue
        p.act(h.junk[:, :], ps, AF.Square, accum_out=yssq[:, t:t + 1],
              wait=p.req(f"{tag}_mm{par}", f"{tag}_red{par}"), inc="act")
        p.mark(f"{tag}_sq{par}", "act")
        if OP1_OPS < 4: continue
        p.ts("vector", mu[:, t:t + 1], ysum[:, t:t + 1], 1.0 / D, None, ALU.mult,
             wait=p.req(f"{tag}_red{par}", f"{tag}_sq{par}"), inc="dve")
        p.tt("vector", nmu2[:, t:t + 1], mu[:, t:t + 1], mu[:, t:t + 1], ALU.mult,
             inc="dve")
        p.ts("vector", veps[:, t:t + 1], yssq[:, t:t + 1], 1.0 / D, LN_EPS,
             ALU.mult, ALU.add, inc="dve")
        p.tt("vector", veps[:, t:t + 1], veps[:, t:t + 1], nmu2[:, t:t + 1],
             ALU.subtract, inc="dve")
        p.mark(f"{tag}_veps{par}", "dve")
        if OP1_OPS < 5: continue
        p.act(rstd[:, t:t + 1], veps[:, t:t + 1], AF.Ln,
              wait=p.req(f"{tag}_veps{par}"), inc="act")
        p.act(rstd[:, t:t + 1], rstd[:, t:t + 1], AF.Exp, scale=-0.5, inc="act")
        p.mark(f"{tag}_rstd{par}", "act")
        if OP1_OPS < 6: continue
        p.ts("vector", negmur[:, t:t + 1], mu[:, t:t + 1], rstd[:, t:t + 1], -1.0,
             ALU.mult, ALU.mult, wait=p.req(f"{tag}_rstd{par}"), inc="dve")
        p.mark(f"{tag}_nm{par}", "dve")
        if OP1_OPS < 7: continue
        f32t = h.f32a if par == 0 else h.f32b
        p.act(f32t[:, :], ps, AF.Identity, bias=negmur[:, t:t + 1],
              scale=rstd[:, t:t + 1],
              wait=p.req(f"{tag}_rstd{par}", f"{tag}_nm{par}") +
              p.need(f"{tag}_resid{par}"), inc="act")
        p.bkr(2 * par, 2, "act")
        p.mark(f"{tag}_apl{par}", "act")
        if OP1_OPS < 8: continue
        p.tt("vector", h.stream[:, t, :], h.stream[:, t, :], f32t[:, :], ALU.add,
             wait=p.req(f"{tag}_apl{par}"), inc="dve")
        p.mark(f"{tag}_resid{par}", "dve")
        p.mark(f"strm{t}", "dve")
        if tail_cb is not None:
            tail_cb(t)
    p.mark("stream", "dve")
    p.mark(tag, "dve")




def emit_l1_attn(p, h, seq, tail_cb=None, prelude=False):
    arn = h.arn
    d = h.dram
    t0 = f"b2_{seq}"
    prev = p.need("blkend")
    alloc_elt(h, 16640)
    xn = a3(arn.alloc_at("xn", 0, 8 * 2048), 8, 2048)
    emit_prenorm_fm(p, h, xn, t0 + "ln", prelude=prelude,
                    junk_ap=arn.ap[:, 24832:26880].bitcast(F32))
    wxn = p.req(t0 + "ln")

    abd = arn.alloc_at("abd", 16384, 128)
    dftc = arn.alloc_at("dftc", 66048, 64)[0:33, 0:64]
    dfts = arn.alloc_at("dfts", 66112, 64)[0:33, 0:64]
    p.dma("gpsimd", dftc, d["c_dftc"], wait=prev)
    p.dma("gpsimd", dfts, d["c_dfts"], wait=prev)
    p.dma("sync", h.st32[0:33, 280:281], d["l1_mask"], wait=prev)
    p.dma("sync", h.st32[0:33, 281:282], d["c_dftk"], wait=prev)
    p.mark("flt_in", "dsp")
    p.mark("flt_in2", "dgp")
    m_ = h.st32[0:33, 282:283]
    p.act(m_, h.st32[0:33, 280:281], AF.Sigmoid, wait=p.req("flt_in"), inc="act")
    p.mark("flt_sig", "act")
    p.tt("vector", m_, m_, h.st32[0:33, 281:282], ALU.mult,
         wait=p.req("flt_sig"), inc="dve")
    p.mark("flt_mc", "dve")
    csc = h.pt[0:33, 0, 0:64]
    sns = h.pt[0:33, 0, 64:128]
    p.ts("vector", csc, dftc, m_, None, ALU.mult,
         wait=p.req("flt_mc", "flt_in2"), inc="dve")
    p.ts("vector", sns, dfts, m_, None, ALU.mult, inc="dve")
    p.mark("flt_tab", "dve")
    pA = pbank(h, 6, 1)
    for base in (0, 64):
        tp = (0, base)
        for (lt, rt) in ((csc, dftc), (sns, dfts)):
            def mmf(t, lt=lt, rt=rt, base=base):
                return t.matmul(pA[base:base + 64, 0:64], lt, rt,
                                start=(lt is csc), stop=(lt is sns),
                                tile_position=(0, base))
            p.do("tensor", mmf,
                 wait=(p.req("flt_tab") + p.bkw(6, 1)) if (base == 0 and lt is csc) else (),
                 inc="pe" if (base == 64 and lt is sns) else None)
    p.mark("flt_mm", "pe")
    p.memset("vector", abd, 0.0, wait=prev, inc="dve")
    p.mark("abd_z", "dve")
    p.act(abd[0:64, 0:64], pA[0:64, 0:64], AF.Identity,
          wait=p.req("flt_mm", "abd_z"), inc="act")
    p.act(abd[64:128, 64:128], pA[64:128, 0:64], AF.Identity, inc="act")
    p.bkr(6, 1, "act")
    p.mark("abd", "act")
    arn.free_("dftc", "dfts")

    wv1 = load_w_fm(p, h, "wv1", "l1_wv", 8, 1024, extra_wait=prev, off=33024)
    vaug = a3(arn.alloc_at("vaug", 49408, 16 * 1040), 16, 1040)
    vones = vaug.rearrange("p t (hd c) -> p t hd c", hd=16)[:, :, :, 64:65]
    p.memset("gpsimd", vones, 1.0, wait=prev, inc="gp")
    p.mark("vones", "gp")
    for t in range(NT):
        par = t % 2
        ps = pbank(h, 2 * par, 2)[:, 0:1024]
        wl = p.req(f"{t0}lnq{t // 4}", "wv1") + p.bkw(2 * par, 2)
        for kc in range(8):
            for nn in range(2):
                p.mm(ps[:, nn * 512:(nn + 1) * 512],
                     xn[:, kc, t * 128:(t + 1) * 128],
                     wv1[:, kc, nn * 512:(nn + 1) * 512],
                     start=(kc == 0), stop=(kc == 7),
                     wait=wl if (kc == 0 and nn == 0) else (),
                     inc="pe" if (kc == 7 and nn == 1) else None)
        p.mark(f"vmm{par}", "pe")
        p.copy("vector",
               vaug[:, t, :].rearrange("p (hd c) -> p hd c", hd=16)[:, :, 0:64],
               ps.rearrange("p (hd c) -> p hd c", hd=16),
               wait=p.req(f"vmm{par}"), inc="dve")
        p.bkr(2 * par, 2, "dve")
        p.mark(f"vevac{par}", "dve")
    p.mark("vaug_done", "dve")
    arn.free_("wv1")

    free_elt(h)
    for (nm, wname, woff, dst_off) in (("q2", "l1_wq", 33024, 16640),
                                       ("k2", "l1_wk", 66048, 33024)):
        if nm == "q2":
            w1 = load_w_fm(p, h, nm + "w", wname, 8, 1024,
                           extra_wait=p.req("vaug_done"), off=woff)
            X = a3(arn.alloc_at(nm, dst_off, 8 * 2048), 8, 2048)
            emit_proj_fm(p, h, f"{nm}p{seq}", X,
                         lambda kc, oc, w1=w1: w1[:, kc, oc * 128:(oc + 1) * 128],
                         xn, 8, 8, AF.Identity,
                         rhs_ready=wxn + p.req("vaug_done"),
                         w_ready=p.req(nm + "w"))
            arn.free_(nm + "w")
        else:
            X = a3(arn.alloc_at(nm, dst_off, 8 * 2048), 8, 2048)
            for quarter in range(4):
                wq_ = a3(arn.alloc_at(f"{nm}w{quarter}", 66048, 8 * 256), 8, 256)
                p.dma("gpsimd", wq_,
                      d[wname].rearrange("(c p) m -> p c m", p=P)
                      [:, :, quarter * 256:(quarter + 1) * 256],
                      wait=p.req(f"{nm}p{seq}_ev{quarter - 1}") if quarter else
                      p.req(f"q2p{seq}"))
                p.mark(f"{nm}wl{quarter}", "dgp")
                emit_proj_fm(p, h, f"{nm}p{seq}q{quarter}",
                             a3(X.rearrange("p a b -> p (a b)"), 8, 2048),
                             lambda kc, oc, wq_=wq_, quarter=quarter:
                             wq_[:, kc, (oc - 2 * quarter) * 128:
                                 (oc - 2 * quarter + 1) * 128],
                             xn, 8, 2, AF.Identity,
                             rhs_ready=wxn, w_ready=p.req(f"{nm}wl{quarter}"),
                             oc_base=2 * quarter)
                p.mark(f"{nm}p{seq}_ev{quarter}", "act")
                arn.free_(f"{nm}w{quarter}")
            p.mark(f"{nm}p{seq}", "act")
        for c in range(8):
            par = c % 2
            ps = pbank(h, 4 + par, 1)
            for nq in range(4):
                p.mm(ps[:, 0:512], abd, X[:, c, nq * 512:(nq + 1) * 512],
                     start=True, stop=True,
                     wait=(p.req("abd", f"{nm}p{seq}") + p.bkw(4 + par, 1) +
                           p.need(f"{nm}f_ev{par}")) if nq == 0 else
                     p.need(f"{nm}f_ev{par}"),
                     inc="pe")
                p.mark(f"{nm}f_mm{par}", "pe")
                p.act(X[:, c, nq * 512:(nq + 1) * 512], ps[:, 0:512], AF.Identity,
                      wait=p.req(f"{nm}f_mm{par}"), inc="act")
                p.bkr(4 + par, 1, "act")
                p.mark(f"{nm}f_ev{par}", "act")
        p.mark(f"{nm}flt", "act")

    Q = a3(arn.ap[:, 16640:16640 + 16384], 8, 2048)
    K = a3(arn.ap[:, 33024:33024 + 16384], 8, 2048)
    qk_ready = p.req(f"q2flt", f"k2flt")
    arn.free_("xn", "abd")
    O = a3(arn.alloc_at("O", 0, 8 * 2048), 8, 2048)
    emit_attention(p, h, f"at2_{seq}", Q, K, vaug, O, causal=False, scale=0.125,
                   qk_ready=qk_ready)
    arn.free_("q2", "k2", "vaug")

    att_done = p.req(f"at2_{seq}_odone")
    alloc_elt(h, 25856)
    wo = load_w_fm(p, h, "wo2", "l1_wo", 8, 1024, extra_wait=att_done, off=16640)
    bo = load_row_bf(p, h, "bo2", "l1_bo", 1024, extra_wait=att_done, off=24832)
    emit_outproj_postnorm(p, h, f"b2o{seq}", O, wo, bo, tail_cb=tail_cb,
                          o_ready=p.req(f"at2_{seq}_odone", f"at2_{seq}_oshifts"),
                          w_ready=p.req("wo2", "bo2"))
    arn.free_("O", "wo2", "bo2")
    free_elt(h)
    p.mark("blkend", "dve")




def emit_b3_routing(p, h, seq, xnfm, lofm_slot):
    st = h.st32

    def lo_cb(p, h, t, par):
        arn = h.arn
        lofm = lofm_slot
        pbL = pbank_bf(h, 4 + par, 1)
        lo_tm = h.pt[:, 2 * par + 1, :]
        tag = f"b3r{seq}"
        for c in range(8):
            p.transp(pbL[:, c * 128:(c + 1) * 128], lo_tm[:, c * 128:(c + 1) * 128],
                     h.ident[:, :],
                     wait=(p.need(f"b3_{seq}ln_cast{par}", "consts") +
                           p.bkw(4 + par, 1)) if c == 0 else (),
                     inc="pe" if c == 7 else None)
        p.mark(f"{tag}_lotp{par}", "pe")
        p.copy("vector", lofm, pbL.rearrange("p (c n) -> p c n", c=8),
               wait=p.req(f"{tag}_lotp{par}"), inc="dve")
        p.bkr(4 + par, 1, "dve")
        p.mark(f"{tag}_lofm", "dve")
        nwh_all, nwl_all = h.b3_nwa
        psb = pbank(h, 0, 1)[:, 0:48]
        it = 0
        for (lhs_src, rhs_w) in ((xnfm, nwh_all), (xnfm, nwl_all),
                                 (lofm, nwh_all)):
            for dc in range(8):
                if lhs_src is xnfm:
                    lhsT = xnfm[:, dc, t * 128:(t + 1) * 128]
                else:
                    lhsT = lofm[:, dc, :]
                p.mm(psb, lhsT, rhs_w[:, dc, :],
                     start=(it == 0), stop=(it == 23),
                     wait=(p.req(f"b3_{seq}ln_pbf{par}", f"{tag}_lofm",
                                 "b3nw", "b3nwz") + p.bkw(0, 1))
                     if it == 0 else (),
                     inc="pe" if it == 23 else None)
                it += 1
        p.mark(f"{tag}_scmm", "pe")
        for fi, f in enumerate(("q", "k", "v")):
            sc = st[:, 288:303]
            p.tt("vector", sc, psb[:, fi * 16:fi * 16 + 15], h.b3_nb[f],
                 ALU.add,
                 wait=p.req(f"{tag}_scmm", f"b3nbL{f}") + p.need(f"{tag}_g"),
                 inc="dve")
            if f == "v":
                p.bkr(0, 1, "dve")
            p.mark(f"{tag}_scv", "dve")
            gb = h.b3_gbase[f] + t * 15
            p.ts("gpsimd", h.b3_g1[:, gb:gb + 15],
                 sc, 0.0, None, ALU.is_ge,
                 wait=p.req(f"{tag}_scv"), inc="gp")
            p.mark(f"{tag}_g", "gp")
    return lo_cb


def emit_b3_sel(p, h, seq, f, fi, tiles):
    st = h.st32
    tag = f"b3sel{seq}{f}"
    scr = h.b3_selscr
    g30 = scr[:, 64:94]
    ua = scr[:, 32:48]
    ub = scr[:, 48:64]
    rs = ARN_COLS
    gt = g30.tensor
    for t in tiles:
        par = t % 2
        stm = h.b3_seltm[:, par * 16:par * 16 + 16]
        if f == "q":
            gsrc = (h.b3_g1qa[:, t * 15:t * 15 + 15] if t < 8 else
                    h.b3_g1qb[:, (t - 8) * 15:(t - 8) * 15 + 15])
        else:
            gb = h.b3_gbase[f] + t * 15
            gsrc = h.b3_g1[:, gb:gb + 15]
        p.copy("gpsimd", g30[:, 15:30], gsrc,
               wait=p.req(f"b3r{seq}_g"), inc="gp")
        p.ts("gpsimd", g30[:, 0:15], g30[:, 15:30], -1.0, 1.0,
             ALU.mult, ALU.add, inc="gp")
        o = g30.offset
        T0 = bass.AP(gt, o + 0, [[rs, 128], [15, 2], [0, 2], [0, 4]])
        T1 = bass.AP(gt, o + 1, [[rs, 128], [1, 2], [15, 2], [0, 4]])
        T2 = bass.AP(gt, o + 3, [[rs, 128], [1, 4], [15, 2], [0, 2]])
        T3 = bass.AP(gt, o + 7, [[rs, 128], [2, 4], [1, 2], [15, 2]])
        p.tt("gpsimd", ua.rearrange("p (a b c) -> p a b c", a=2, b=2),
             T0, T1, ALU.mult, inc="gp")
        p.tt("gpsimd", ub.rearrange("p (a b c) -> p a b c", a=4, b=2),
             T2, T3, ALU.mult, inc="gp")
        p.tt("gpsimd", stm, ua, ub, ALU.mult,
             wait=p.need(f"b3stm{par}"), inc="gp")
        p.mark(f"{tag}_stm", "gp")
        pbs = pbank_bf(h, 6 + par, 1)[0:16, 0:128]
        p.transp(pbs, stm, h.ident[:, :],
                 wait=p.req(f"{tag}_stm") + p.bkw(6 + par, 1), inc="pe")
        p.mark(f"{tag}_tpp{par}", "pe")
        p.mark(f"b3stm{par}", "pe")
        p.copy("vector", h.b3_selfm[0:16, t * 128:(t + 1) * 128], pbs,
               wait=p.req(f"{tag}_tpp{par}"), inc="dve")
        p.bkr(6 + par, 1, "dve")
        p.mark(f"{tag}_tp{par}", "dve")
        p.mark(f"{tag}_T{t}", "dve")
    p.mark(tag, "dve")




def emit_b3_fff(p, h, seq, f, fi, out_mode, groups, xn_src, out_buf, sel_off=0):
    d = h.dram
    tag = f"b3f{seq}{f}"
    st = h.st32
    w1, w2 = h.b3_w1, h.b3_w2
    ev_sem = "act" if out_mode == "fm" else "dve"
    for g in groups:
        gs = slice(g * 256, g * 256 + 256)
        ss = slice(sel_off + g * 256, sel_off + g * 256 + 256)
        tg = (sel_off + g * 256) // 128
        selw = p.req(f"b3sel{seq}{f}_T{tg}", f"b3sel{seq}{f}_T{tg + 1}")
        for half_m in range(2):
            pm = pbank(h, 6, 2).rearrange("p (c n) -> p c n", c=4)
            for hc4 in range(4):
                hc = half_m * 4 + hc4
                p.mm(pm[:, hc4, :], h.b3_eleaf[:, hc * 128:(hc + 1) * 128],
                     h.b3_selfm[0:16, ss], start=True, stop=True,
                     wait=(selw + p.bkw(6, 2) + p.req("b3el", f"{tag}_w"))
                     if hc4 == 0 else (),
                     inc="pe" if hc4 == 3 else None)
            p.mark(f"{tag}_mm", "pe")
            for hc4 in range(4):
                hc = half_m * 4 + hc4
                par = hc % 2
                ph = pbank(h, 4 + par, 1)[:, 0:256]
                wl = (p.req(f"{tag}_w", f"{tag}_b1") + p.bkw(4 + par, 1) +
                      list(xn_src[1]))
                for dc in range(8):
                    p.mm(ph, w1[:, dc, hc * 128:(hc + 1) * 128],
                         xn_src[0][:, dc, gs],
                         start=(dc == 0), stop=(dc == 7),
                         wait=wl if dc == 0 else (),
                         inc="pe" if dc == 7 else None)
                p.mark(f"{tag}_hmm{par}", "pe")
                hslot = h.pt[:, hc // 4, (hc % 4) * 256:(hc % 4) * 256 + 256]
                p.act(hslot, ph, AF.Relu, bias=st[:, 357 + hc:358 + hc],
                      wait=p.req(f"{tag}_hmm{par}") + p.need("b3H_free"),
                      inc="act")
                p.bkr(4 + par, 1, "act")
                p.mark(f"{tag}_hr{hc}", "act")
                p.tt("vector", hslot, hslot, pm[:, hc4, :], ALU.mult,
                     wait=p.req(f"{tag}_hr{hc}", f"{tag}_mm"), inc="dve")
                p.mark(f"{tag}_hm{hc}", "dve")
            p.bkr(6, 2, "dve")
        hw = p.req(*[f"{tag}_hm{hc}" for hc in range(8)])
        if out_mode == "fm":
            for ds in range(8):
                par = ds % 2
                ps = pbank(h, 2 + par, 1)[:, 0:256]
                for hc in range(8):
                    hslot = h.pt[:, hc // 4, (hc % 4) * 256:(hc % 4) * 256 + 256]
                    p.mm(ps, w2[:, hc, ds * 128:(ds + 1) * 128], hslot,
                         start=(hc == 0), stop=False,
                         wait=(hw + p.req(f"{tag}_w", f"{tag}_b2") +
                               p.bkw(2 + par, 1)) if hc == 0 else ())
                p.mm(ps, h.b3_b2[:, ds * 128:(ds + 1) * 128],
                     h.b3_selfm[0:16, ss], start=False, stop=True, inc="pe")
                p.mark(f"{tag}_pmm{par}", "pe")
                p.act(out_buf[:, ds, gs], ps, AF.Identity,
                      wait=p.req(f"{tag}_pmm{par}"), inc="act")
                p.bkr(2 + par, 1, "act")
                p.mark(f"{tag}_pev{par}", "act")
        else:
            for tt2 in range(2):
                t_ = g * 2 + tt2
                tsl = slice(sel_off + t_ * 128, sel_off + t_ * 128 + 128)
                par = tt2 % 2
                ps = pbank(h, 2 + 2 * par, 2)[:, 0:1024]
                for hc in range(8):
                    hslot = h.pt[:, hc // 4,
                                 (hc % 4) * 256 + tt2 * 128:
                                 (hc % 4) * 256 + tt2 * 128 + 128]
                    for nn in range(2):
                        p.mm(ps[:, nn * 512:(nn + 1) * 512], hslot,
                             w2[:, hc, nn * 512:(nn + 1) * 512],
                             start=(hc == 0), stop=False,
                             wait=(hw + p.req(f"{tag}_w", f"{tag}_b2") +
                                   p.bkw(2 + 2 * par, 2))
                             if (hc == 0 and nn == 0) else ())
                for nn in range(2):
                    p.mm(ps[:, nn * 512:(nn + 1) * 512],
                         h.b3_selfm[0:16, tsl],
                         h.b3_b2[:, nn * 512:(nn + 1) * 512],
                         start=False, stop=True,
                         inc="pe" if nn == 1 else None)
                p.mark(f"{tag}_pmm{par}", "pe")
                p.copy("vector", out_buf[:, t_, 0:1024], ps,
                       wait=p.req(f"{tag}_pmm{par}"), inc="dve")
                p.bkr(2 + 2 * par, 2, "dve")
                p.mark(f"{tag}_pev{par}", "dve")
        p.mark("b3H_free", ev_sem)
    p.mark(tag, ev_sem)


def emit_b3_fff_weights(p, h, seq, f, gate):
    d = h.dram
    tag = f"b3f{seq}{f}"
    p.dma("gpsimd", h.b3_w1, d[f"l1f_{f}_w1"].rearrange("(c p) m -> p c m", p=P),
          wait=gate)
    p.dma("gpsimd", h.b3_w2, d[f"l1f_{f}_w2"].rearrange("(c p) m -> p c m", p=P),
          wait=())
    p.dma("gpsimd", h.b3_b2, d[f"l1f_{f}_b2"], wait=())
    p.mark(f"{tag}_w", "dgp")
    p.mark(f"{tag}_b2", "dgp")
    p.dma("sync", h.st32[:, 357:365], d[f"l1f_{f}_b1"], wait=gate)
    p.mark(f"{tag}_b1", "dsp")


def emit_b3_attn_quarter(p, h, seq, quarter, qq, kf, vaug, qz_ready):
    st = h.st32
    tag = f"b3a{seq}q{quarter}"
    scale = 1.0 / 32.0
    sqb = h.pt[:, 0:2, :].rearrange("p a b -> p (a b)")
    for c in range(8):
        p.tt("gpsimd", sqb[:, 0:512], qq[:, c, :], qq[:, c, :], ALU.mult,
             wait=list(qz_ready) + p.need(f"{tag}_nc") + p.need("b3H_free"),
             inc="gp")
        p.mark(f"{tag}_sq", "gp")
        p.mm(pbank(h, 7, 1)[0:1, 0:512], h.ones64[:, 0:1], sqb[:, 0:512],
             start=(c == 0), stop=(c == 7),
             wait=p.req(f"{tag}_sq", "consts_gp") +
             (p.bkw(7, 1) if c == 0 else []),
             inc="pe")
        p.mark(f"{tag}_nc", "pe")
    p.mark("b3H_free", "gp")
    p.red(st[0:1, 278:279], pbank(h, 7, 1)[0:1, 0:512], ALU.max,
          wait=p.req(f"{tag}_nc"), inc="dve")
    p.bkr(7, 1, "dve")
    p.mark(f"{tag}_qmax", "dve")
    p.act(st[0:1, 277:278], st[0:1, 278:279], AF.Ln,
          wait=p.req(f"b3n{seq}_kmax", f"{tag}_qmax"), inc="act")
    p.act(st[0:1, 276:277], st[0:1, 279:280], AF.Ln, inc="act")
    p.mark(f"{tag}_lns", "act")
    p.tt("vector", st[0:1, 277:278], st[0:1, 277:278], st[0:1, 276:277],
         ALU.add, wait=p.req(f"{tag}_lns"), inc="dve")
    p.mark(f"{tag}_cbs", "dve")
    p.act(st[0:1, 277:278], st[0:1, 277:278], AF.Exp, scale=0.5,
          wait=p.req(f"{tag}_cbs"), inc="act")
    p.mark(f"{tag}_cbe", "act")
    p.ts("vector", st[0:1, 277:278], st[0:1, 277:278], -scale, None, ALU.mult,
         wait=p.req(f"{tag}_cbe"), inc="dve")
    p.copy("vector", h.pt[0:1, 2, 256:257], st[0:1, 277:278], inc="dve")
    p.mark(f"{tag}_cbb", "dve")
    p.mm(pbank(h, 7, 1)[:, 0:1], h.ones_row_bf[:, :], h.pt[0:1, 2, 256:257],
         start=True, stop=True, wait=p.req(f"{tag}_cbb") + p.bkw(7, 1),
         inc="pe")
    p.mark(f"{tag}_cbc", "pe")
    p.copy("vector", h.cbias[:, 0:1], pbank(h, 7, 1)[:, 0:1],
           wait=p.req(f"{tag}_cbc"), inc="dve")
    p.bkr(7, 1, "dve")
    p.mark(f"{tag}_cbias", "dve")

    kw = p.req(f"b3f{seq}k", "b3vones", f"b3f{seq}v")
    for qtb in range(2):
        qa = quarter * 512 + qtb * 256
        first = {}
        def emit_strip(kt):
            par = kt % 2
            ps = pbank(h, par, 1)[:, 0:256]
            ptb = h.pt[:, 0, par * 256:par * 256 + 256]
            wl = kw + list(qz_ready) + p.bkw(par, 1) + p.req(f"{tag}_cbias")
            for dc in range(8):
                p.mm(ps, kf[:, dc, kt * 128:(kt + 1) * 128],
                     qq[:, dc, qtb * 256:qtb * 256 + 256],
                     start=(dc == 0), stop=(dc == 7),
                     wait=wl if dc == 0 else (),
                     inc="pe" if dc == 7 else None)
            p.mark(f"{tag}_s{par}", "pe")
            p.act(ptb, ps, AF.Exp, bias=h.cbias[:, 0:1], scale=scale,
                  wait=p.req(f"{tag}_s{par}") + p.need(f"b3pv{seq}_{par}"),
                  inc="act")
            p.bkr(par, 1, "act")
            p.mark(f"{tag}_exp{par}", "act")

        def emit_pv(kt):
            par = kt % 2
            ptb = h.pt[:, 0, par * 256:par * 256 + 256]
            expw = p.req(f"{tag}_exp{par}")
            for qt in range(2):
                pts = ptb[:, qt * 128:qt * 128 + 128]
                for dhalf in range(2):
                    po = pbank(h, 2 + 2 * qt + dhalf, 1)
                    p.mm(po, pts, vaug[:, kt, dhalf * 512:(dhalf + 1) * 512],
                         start=(kt == 0), stop=(kt == NT - 1),
                         wait=(expw + p.bkw(2 + 2 * qt + dhalf, 1))
                         if kt == 0 else
                         (expw if (qt == 0 and dhalf == 0) else ()),
                         inc=None)
                p.mm(pbank(h, 6 + qt, 1)[:, 0:1], pts, vaug[:, kt, 1024:1025],
                     start=(kt == 0), stop=(kt == NT - 1),
                     wait=(expw + p.bkw(6 + qt, 1)) if kt == 0 else (),
                     inc="pe" if qt == 1 else None)
            p.mark(f"b3pv{seq}_{par}", "pe")

        for kt in range(NT):
            emit_strip(kt)
            if kt > 0:
                emit_pv(kt - 1)
        emit_pv(NT - 1)
        p.mark(f"{tag}_pvdone", "pe")
        p.mark("b3H_free", "pe")
        for qt in range(2):
            t = (qa + qt * 128) // 128
            par = qt
            rec = st[:, 275:276]
            p.recip(rec, pbank(h, 6 + qt, 1)[:, 0:1],
                    wait=p.req(f"{tag}_pvdone"), inc="dve")
            p.bkr(6 + qt, 1, "dve")
            p.mark(f"{tag}_rec", "dve")
            f32t = h.f32a if par == 0 else h.f32b
            for dhalf in range(2):
                p.ts("vector", f32t[:, dhalf * 512:(dhalf + 1) * 512],
                     pbank(h, 2 + 2 * qt + dhalf, 1), rec, None, ALU.mult,
                     wait=p.req(f"{tag}_rec") + p.need(f"{tag}_yfree{par}"),
                     inc="dve")
                p.bkr(2 + 2 * qt + dhalf, 1, "dve")
            p.mark(f"{tag}_y{par}", "dve")
            emit_postnorm_tile(p, h, f"{tag}e", t, par, f32t,
                               p.req(f"{tag}_y{par}"))
            p.mark(f"{tag}_yfree{par}", "dve")
    p.mark(tag, "dve")


def emit_postnorm_tile(p, h, tag, t, par, y_f32, y_ready, tail_cb=None):
    st = h.st32
    ysum = st[:, 160:176]; yssq = st[:, 176:192]; mu = st[:, 192:208]
    veps = st[:, 208:224]; rstd = st[:, 224:240]; nmu2 = st[:, 240:256]
    negmur = st[:, 256:272]
    p.red(ysum[:, t:t + 1], y_f32, ALU.add, wait=y_ready, inc="dve")
    p.mark(f"{tag}_red{par}", "dve")
    p.act(h.junk[:, :], y_f32, AF.Square, accum_out=yssq[:, t:t + 1],
          wait=y_ready + p.req(f"{tag}_red{par}"), inc="act")
    p.mark(f"{tag}_sq{par}", "act")
    p.ts("vector", mu[:, t:t + 1], ysum[:, t:t + 1], 1.0 / D, None, ALU.mult,
         wait=p.req(f"{tag}_red{par}", f"{tag}_sq{par}"), inc="dve")
    p.tt("vector", nmu2[:, t:t + 1], mu[:, t:t + 1], mu[:, t:t + 1], ALU.mult,
         inc="dve")
    p.ts("vector", veps[:, t:t + 1], yssq[:, t:t + 1], 1.0 / D, LN_EPS,
         ALU.mult, ALU.add, inc="dve")
    p.tt("vector", veps[:, t:t + 1], veps[:, t:t + 1], nmu2[:, t:t + 1],
         ALU.subtract, inc="dve")
    p.mark(f"{tag}_veps{par}", "dve")
    p.act(rstd[:, t:t + 1], veps[:, t:t + 1], AF.Ln,
          wait=p.req(f"{tag}_veps{par}"), inc="act")
    p.act(rstd[:, t:t + 1], rstd[:, t:t + 1], AF.Exp, scale=-0.5, inc="act")
    p.mark(f"{tag}_rstd{par}", "act")
    p.ts("vector", negmur[:, t:t + 1], mu[:, t:t + 1], rstd[:, t:t + 1], -1.0,
         ALU.mult, ALU.mult, wait=p.req(f"{tag}_rstd{par}"), inc="dve")
    p.mark(f"{tag}_nm{par}", "dve")
    p.ts("vector", y_f32, y_f32, rstd[:, t:t + 1], negmur[:, t:t + 1],
         ALU.mult, ALU.add, wait=p.req(f"{tag}_nm{par}"), inc="dve")
    p.mark(f"{tag}_apl{par}", "dve")
    p.tt("vector", h.stream[:, t, :], h.stream[:, t, :], y_f32, ALU.add,
         wait=p.req(f"{tag}_apl{par}"), inc="dve")
    p.mark(f"{tag}_resid{par}", "dve")
    p.mark("stream", "dve")
    p.mark(f"strm{t}", "dve")
    if tail_cb is not None:
        tail_cb(t)


def emit_l1_ffn(p, h, seq, prelude=False):
    arn = h.arn
    d = h.dram
    t0 = f"b3_{seq}"
    prev = p.need("blkend")
    st = h.st32

    h.b3_g1 = arn.alloc_at("b3g1", 33088, 720)
    h.b3_gbase = {"v": 0, "k": 240, "q": 480}
    h.b3_nwh = {}
    h.b3_nwl = {}
    h.b3_nb = {}
    off = 66048
    nwh_all = a3(arn.alloc_at("b3nwh", off, 384), 8, 48)
    nwl_all = a3(arn.alloc_at("b3nwl", off + 384, 384), 8, 48)
    for j in (15, 31, 47):
        p.memset("gpsimd", nwh_all[:, :, j:j + 1], 0.0, wait=prev, inc="gp")
        p.memset("gpsimd", nwl_all[:, :, j:j + 1], 0.0, inc="gp")
    p.mark("b3nwz", "gp")
    h.b3_nwa = (nwh_all, nwl_all)
    off += 768
    for fi, f in enumerate(("q", "k", "v")):
        nwf = arn.alloc_at(f"b3nw32{f}", off, 256).bitcast(F32)
        p.dma("sync", a3(nwf[:, 0:120], 8, 15),
              d[f"l1f_{f}_nw"].rearrange("(c p) m -> p c m", p=P), wait=prev)
        p.mark(f"b3nwf{f}", "dsp")
        nwh = nwh_all[:, :, fi * 16:fi * 16 + 15]
        nwl = nwl_all[:, :, fi * 16:fi * 16 + 15]
        nw32 = a3(nwf[:, 0:120], 8, 15)
        p.copy("vector", nwh, nw32, wait=p.req(f"b3nwf{f}"), inc="dve")
        p.stt("vector", nwl, nwh, -1.0, nw32, ALU.mult, ALU.add, inc="dve")
        nb = arn.alloc_at(f"b3nb{f}", off + 256, 32).bitcast(F32)[:, 0:15]
        p.dma("sync", nb, d[f"l1f_{f}_nb"], wait=prev)
        p.mark(f"b3nbL{f}", "dsp")
        h.b3_nb[f] = nb
        off += 320
    p.mark("b3nw", "dve")

    lofm_buf = a3(arn.alloc_at("b3lofm", 61760, 8 * 128), 8, 128)
    alloc_elt(h, 49472)
    xn = a3(arn.alloc_at("xn", 0, 8 * 2048), 8, 2048)
    lo_cb = emit_b3_routing(p, h, seq, xn, lofm_buf)
    emit_prenorm_fm(p, h, xn, t0 + "ln", lo_cb=lo_cb, prelude=prelude)
    wxn = p.req(t0 + "ln")
    free_elt(h)
    arn.free_("b3lofm")
    for f in ("q", "k", "v"):
        arn.free_(f"b3nw32{f}", f"b3nb{f}")
    arn.free_("b3nwh", "b3nwl")
    p.mark("b3wgate", "dve")
    h.b3_g1qa = arn.alloc_at("b3g1qa", 16512, 128)
    h.b3_g1qb = arn.alloc_at("b3g1qb", 65856, 192)
    p.copy("gpsimd", h.b3_g1qa[:, 0:120], h.b3_g1[:, 480:600],
           wait=p.req(f"b3r{seq}_g"), inc="gp")
    p.copy("gpsimd", h.b3_g1qb[:, 0:120], h.b3_g1[:, 600:720], inc="gp")
    p.mark("b3g1q", "gp")

    h.b3_w1 = a3(arn.alloc_at("b3w1", 49472, 8 * 1024), 8, 1024)
    h.b3_w2 = a3(arn.alloc_at("b3w2", 57664, 8 * 1024), 8, 1024)
    h.b3_b2 = arn.alloc_at("b3b2", 66048, 1024)[0:16, :]
    h.b3_eleaf = arn.alloc_at("b3el", 67072, 1024)[0:16, :]
    p.dma("gpsimd", h.b3_eleaf, d["c_eleaf"], wait=p.req("b3wgate"))
    p.mark("b3el", "dgp")
    h.b3_selfm = h.pt[0:16, 2:4, :].rearrange("p a b -> p (a b)")
    h.b3_selscr = arn.alloc_at("b3stm", 16384, 96)
    h.b3_seltm = h.b3_selscr[:, 0:32]
    p.mark("b3H_free", "dve")

    vaug = a3(arn.alloc_at("vaug", 16640, 16 * 1025), 16, 1025)
    p.memset("gpsimd", vaug[:, :, 1024:1025], 1.0, wait=p.req("b3wgate"),
             inc="gp")
    p.mark("b3vones", "gp")
    emit_b3_sel(p, h, seq, "v", 2, range(NT))
    emit_b3_fff_weights(p, h, seq, "v", p.req("b3wgate"))
    emit_b3_fff(p, h, seq, "v", 2, "vaug", list(range(8)), (xn, wxn), vaug)
    arn.free_("b3g1")
    kf = a3(arn.alloc_at("kf", 33088, 8 * 2048), 8, 2048)
    emit_b3_sel(p, h, seq, "k", 1, range(NT))
    emit_b3_fff_weights(p, h, seq, "k", p.req(f"b3f{seq}v"))
    emit_b3_fff(p, h, seq, "k", 1, "fm", list(range(8)), (xn, wxn), kf)

    tagk = f"b3n{seq}"
    kw = p.req(f"b3f{seq}k")
    sqb = h.pt[:, 0:2, :].rearrange("p a b -> p (a b)")
    for c in range(8):
        p.tt("gpsimd", sqb[:, 0:2048], kf[:, c, :], kf[:, c, :], ALU.mult,
             wait=kw + p.need(f"{tagk}_nc") + p.need("b3H_free"), inc="gp")
        p.mark(f"{tagk}_sq", "gp")
        for nq in range(4):
            p.mm(pbank(h, 2 + nq, 1)[0:1, 0:512], h.ones64[:, 0:1],
                 sqb[:, nq * 512:(nq + 1) * 512],
                 start=(c == 0), stop=(c == 7),
                 wait=(p.req(f"{tagk}_sq", "consts_gp") +
                       (p.bkw(2 + nq, 1) if c == 0 else [])),
                 inc="pe")
        p.mark(f"{tagk}_nc", "pe")
    p.mark("b3H_free", "gp")
    for nq in range(4):
        p.red(st[0:1, 344 + nq:345 + nq], pbank(h, 2 + nq, 1)[0:1, 0:512],
              ALU.max, wait=p.req(f"{tagk}_nc"), inc="dve")
        p.bkr(2 + nq, 1, "dve")
    p.tt("vector", st[0:1, 344:345], st[0:1, 344:345], st[0:1, 345:346],
         ALU.max, inc="dve")
    p.tt("vector", st[0:1, 346:347], st[0:1, 346:347], st[0:1, 347:348],
         ALU.max, inc="dve")
    p.tt("vector", st[0:1, 279:280], st[0:1, 344:345], st[0:1, 346:347],
         ALU.max, inc="dve")
    p.mark(f"{tagk}_kmax", "dve")

    arn.free_("xn")

    emit_b3_sel(p, h, seq, "q", 0, range(NT))
    emit_b3_fff_weights(p, h, seq, "q", p.req(f"b3f{seq}k"))
    qq = a3(arn.alloc_at("qq", 0, 8 * 512), 8, 512)
    xnq = a3(arn.alloc_at("xnq", 4096, 8 * 512), 8, 512)
    alloc_elt(h, 8192)
    regen = arn.alloc_at("b3regen", 13312, 1024)
    rstd = st[:, 80:96]
    negmur = st[:, 96:112]
    for quarter in range(4):
        tq0 = quarter * 4
        for ti in range(4):
            t = tq0 + ti
            par = ti % 2
            pb = pbank_bf(h, 6 + par, 1)
            p.act(h.f32a[:, :], h.stream[:, t, :], AF.Identity,
                  bias=negmur[:, t:t + 1], scale=rstd[:, t:t + 1],
                  wait=stream_ready(p) + p.need("b3rg_hi"), inc="act")
            p.mark("b3rg_f32", "act")
            p.copy("vector", regen, h.f32a[:, :],
                   wait=p.req("b3rg_f32") + p.need(f"b3rg_tpX{par}"), inc="dve")
            p.mark("b3rg_hi", "dve")
            for c in range(8):
                p.transp(pb[:, c * 128:(c + 1) * 128],
                         regen[:, c * 128:(c + 1) * 128], h.ident[:, :],
                         wait=(p.req("b3rg_hi") + p.bkw(6 + par, 1))
                         if c == 0 else (),
                         inc="pe" if c == 7 else None)
            p.mark(f"b3rg_tpp{par}", "pe")
            p.copy("vector", xnq[:, :, ti * 128:(ti + 1) * 128],
                   pb.rearrange("p (c n) -> p c n", c=8),
                   wait=p.req(f"b3rg_tpp{par}") + p.need(f"{t0}qused"),
                   inc="dve")
            p.bkr(6 + par, 1, "dve")
            p.mark(f"b3rg_tpX{par}", "dve")
        p.mark("b3rg", "dve")
        emit_b3_fff(p, h, seq, "q", 0, "fm", [0, 1], (xnq, p.req("b3rg")), qq,
                    sel_off=quarter * 512)
        emit_b3_attn_quarter(p, h, seq, quarter, qq, kf, vaug,
                             qz_ready=p.req(f"b3f{seq}q"))
        p.mark(f"{t0}qused", "pe")
    arn.free_("qq", "xnq", "b3regen", "b3w1", "b3w2", "b3b2", "b3stm",
              "b3g1qa", "b3g1qb",
              "b3el", "kf", "vaug")
    free_elt(h)
    p.mark("blkend", "dve")





def emit_b1_quantize(p, h):
    arn = h.arn
    d = h.dram
    st = h.st32
    wf32s = [arn.alloc_at("q_wf32", 16640, 8192).bitcast(F32),
             arn.alloc_at("q_wf32b", 38144, 8192).bitcast(F32)]
    abufs = [arn.alloc_at("q_a", 24832, 4096),
             arn.alloc_at("q_a2", 46336, 4096)]
    bbuf = arn.alloc_at("q_b", 28928, 4096)
    QE = ("sync", "gpsimd")
    QS = ("dsp", "dgp")
    onesrow32 = bbuf.bitcast(F32)[0:1, 0:128]
    lnw32 = st[:, 300:332]
    lnb32 = st[:, 332:364]
    lnwb = arn.ap[:, 16384:16416]
    lnbb = arn.ap[:, 16416:16448]
    p.dma("sync", lnw32, d["l0f_lnw"], wait=p.need("blkend"))
    p.dma("sync", lnb32, d["l0f_lnb"])
    p.mark("qln_w", "dsp")
    p.copy("vector", lnwb, lnw32, wait=p.req("qln_w"), inc="dve")
    p.copy("vector", lnbb, lnb32, inc="dve")
    p.mark("qln_bf", "dve")
    for (nm, dram_in, dram_out, nch, chw) in (
            ("fc", "l0f_wfcT", "wfcq_dram", 8, 4096),
            ("w2", "l0f_w2T", "w2q_dram", 32, 1024)):
        src3 = d[dram_in].rearrange("(c p) m -> p c m", p=P)
        dst3 = d[dram_out].rearrange("(c p) m -> p c m", p=P)
        asum = st[:, 365:366]
        acc = st[:, 366:367]
        p.memset("vector", onesrow32, 1.0, wait=p.need("blkend"), inc="dve")
        p.mark("q_ones", "dve")
        for c in range(nch):
            par = c % 2
            wb = wf32s[par]
            p.dma(QE[par], wb[:, 0:chw], src3[:, c, :],
                  wait=p.need(f"q{nm}_used{par}"))
            p.mark(f"q{nm}_ld{par}", QS[par])
            p.stt("vector", abufs[par][:, 0:chw], wb[:, 0:chw], -1.0,
                  wb[:, 0:chw], ALU.mult, ALU.max, accum_out=asum,
                  wait=p.req(f"q{nm}_ld{par}"), inc="dve")
            p.mark(f"q{nm}_used{par}", "dve")
            if c == 0:
                p.copy("vector", acc, asum, inc="dve")
            else:
                p.tt("vector", acc, acc, asum, ALU.add, inc="dve")
        p.mark(f"q{nm}_asum", "dve")
        p.mm(pbank(h, 0, 1)[0:1, 0:1], h.onesf32[:, :], acc, start=True, stop=True,
             wait=p.req(f"q{nm}_asum", "consts_gp") + p.bkw(0, 1), inc="pe")
        p.mark(f"q{nm}_tot", "pe")
        thr = st[0:1, 367:368]
        p.ts("vector", thr, pbank(h, 0, 1)[0:1, 0:1],
             0.5 / (4096.0 * 1024.0), 0.5e-6, ALU.mult, ALU.add,
             wait=p.req(f"q{nm}_tot"), inc="dve")
        p.bkr(0, 1, "dve")
        p.copy("vector", st[0:1, 368:369], thr, inc="dve")
        p.mark(f"q{nm}_thr0", "dve")
        p.mm(pbank(h, 0, 1)[:, 0:1], onesrow32, st[0:1, 368:369],
             start=True, stop=True,
             wait=p.req(f"q{nm}_thr0", "q_ones") + p.bkw(0, 1),
             inc="pe")
        p.mark(f"q{nm}_thrb", "pe")
        thrc = st[:, 369:370]
        nthrc = st[:, 370:371]
        p.copy("vector", thrc, pbank(h, 0, 1)[:, 0:1],
               wait=p.req(f"q{nm}_thrb"), inc="dve")
        p.bkr(0, 1, "dve")
        p.ts("vector", nthrc, thrc, -1.0, None, ALU.mult, inc="dve")
        p.mark(f"q{nm}_thr", "dve")
        for c in range(nch):
            par = c % 2
            wb = wf32s[par]
            ab_ = abufs[par]
            p.dma(QE[par], wb[:, 0:chw], src3[:, c, :],
                  wait=p.need(f"q{nm}_q{par}", f"q{nm}_used{par}"))
            p.mark(f"q{nm}_ld2{par}", QS[par])
            p.ts("vector", ab_[:, 0:chw], wb[:, 0:chw], thrc, None,
                 ALU.is_ge,
                 wait=p.req(f"q{nm}_ld2{par}", f"q{nm}_thr") +
                 p.need(f"q{nm}_st{par}", f"q{nm}_uv{par}"), inc="dve")
            p.ts("vector", bbuf[:, 0:chw], wb[:, 0:chw], nthrc, -1.0,
                 ALU.is_ge, ALU.add, inc="dve")
            p.mark(f"q{nm}_used{par}", "dve")
            p.tt("vector", ab_[:, 0:chw], ab_[:, 0:chw], bbuf[:, 0:chw],
                 ALU.add, inc="dve")
            if nm == "w2":
                p.mark(f"qw2_tern{par}", "dve")
                for bi, (bk, lcol) in enumerate(((2, lnbb), (4, lnwb))):
                    for nn in range(2):
                        p.mm(pbank(h, bk + nn, 1)[0:1, 0:512],
                             lcol[:, c:c + 1],
                             ab_[:, nn * 512:(nn + 1) * 512],
                             start=(c == 0), stop=(c == nch - 1),
                             wait=(p.req(f"qw2_tern{par}", "qln_bf") +
                                   (p.bkw(2, 4) if c == 0 else []))
                             if (bi == 0 and nn == 0) else (),
                             inc="pe" if (bi == 1 and nn == 1) else None)
                p.mark(f"qw2_uv{par}", "pe")
                p.ts("vector", ab_[:, 1024:2048], ab_[:, 0:1024],
                     lnw32[:, c:c + 1], None, ALU.mult, inc="dve")
                p.mark(f"q{nm}_q{par}", "dve")
                p.dma(QE[par], dst3[:, c, :], ab_[:, 1024:2048],
                      wait=p.req(f"q{nm}_q{par}"))
            else:
                p.mark(f"q{nm}_q{par}", "dve")
                p.dma(QE[par], dst3[:, c, :], ab_[:, 0:chw],
                      wait=p.req(f"q{nm}_q{par}"))
            p.mark(f"q{nm}_st{par}", QS[par])
    uvstage = arn.ap[0:1, 26880:28928]
    p.act(uvstage[0:1, 0:1024], pbank(h, 2, 2)[0:1, 0:1024], AF.Identity,
          wait=p.req("qw2_uv0", "qw2_uv1"), inc="act")
    p.act(uvstage[0:1, 1024:2048], pbank(h, 4, 2)[0:1, 0:1024], AF.Identity,
          inc="act")
    p.bkr(2, 4, "act")
    p.mark("b1uvkeep", "act")
    p.dma("sync", d["uv_dram"], uvstage, wait=p.req("b1uvkeep"))
    p.mark("b1uv_st", "dsp")
    arn.free_("q_wf32", "q_wf32b", "q_a", "q_a2", "q_b")
    p.mark("b1_quant", "dsp")
    p.mark("b1_quantg", "dgp")


def emit_l0_ffn(p, h, seq, tail_cb=None, prelude=False):
    arn = h.arn
    d = h.dram
    st = h.st32
    t0 = f"b1_{seq}"
    prev = p.need("blkend")
    alloc_elt(h, 33024)
    xn = a3(arn.alloc_at("xn", 0, 8 * 2048), 8, 2048)
    emit_prenorm_fm(p, h, xn, t0 + "ln", prelude=prelude)
    wxn = p.req(t0 + "ln")
    if seq == 0:
        emit_b1_quantize(p, h)

    wfq = d["wfcq_dram"].rearrange("(c p) m -> p c m", p=P)
    w2q3 = d["w2q_dram"].rearrange("(c p) m -> p c m", p=P)
    rows = arn.alloc_at("b1rows", 29952, 1024)
    rows2 = arn.alloc_at("b1rows2", 31232, 1536)
    vrow = rows2[0:1, 0:1024]
    urow = rows[0:1, 0:1024]
    nmub = rows2[0:1, 1024:1280]
    p.dma("gpsimd", vrow, d["uv_dram"][0:1, 0:1024], wait=p.req("b1uv_st"))
    p.dma("gpsimd", urow, d["uv_dram"][0:1, 1024:2048])
    p.mark("b1uvld", "dgp")
    vbc = arn.alloc_at("b1vbc", 28928, 1024)
    for nn in range(2):
        p.mm(pbank(h, 2 + nn, 1),
             h.ones_row_bf[:, :], vrow[0:1, nn * 512:(nn + 1) * 512],
             start=True, stop=True,
             wait=(p.req("b1uvld") + p.bkw(2, 2)) if nn == 0 else (), inc="pe")
    p.mark("b1vbc_mm", "pe")
    p.act(vbc[:, 0:512], pbank(h, 2, 1), AF.Identity, wait=p.req("b1vbc_mm"),
          inc="act")
    p.act(vbc[:, 512:1024], pbank(h, 3, 1), AF.Identity, inc="act")
    p.bkr(2, 2, "act")
    p.mark("b1vbc", "act")

    hgs = a3(arn.alloc_at("b1hg", 16640, 32 * 256), 32, 256)
    wst = arn.alloc_at("b1wfq", 38144, 8192)
    w2r = a3(arn.alloc_at("b1w2r", 46336, 16384), 16, 1024)
    w2s = arn.alloc_at("b1w2s", 62720, 4096)
    b0gate = p.need(f"b0o{seq}_mm0", f"b0o{seq}_mm1")
    for half in range(2):
        p.dma(("gpsimd", "sync")[half], w2r[:, half * 8:half * 8 + 8, :],
              w2q3[:, half * 8:half * 8 + 8, :],
              wait=b0gate + (p.req("qw2_st0", "qw2_st1") if seq == 0 else []))
    p.mark("b1w2r", "dgp")
    p.mark("b1w2rS", "dsp")
    sqslots = [h.pt[:, 0, 0:256], h.pt[:, 0, 256:512]]
    for g in range(8):
        ts0, ts1 = g * 256, g * 256 + 256
        w2v4s = [a3(a3(w2s, 2, 2048)[:, b_, :], 2, 1024) for b_ in range(2)]

        def hg_gelu(hc):
            hc4, hci = hc // 4, hc % 4
            bpar = hc4 % 2
            spar = hc % 2
            wsv4 = a3(a3(wst, 2, 4096)[:, bpar, :], 8, 512)
            wsv = a3(wsv4.rearrange("p a b -> p (a b)"), 32, 128)
            ph = pbank(h, spar, 1)[:, 0:256]
            for dc in range(8):
                p.mm(ph, wsv[:, dc * 4 + hci, :], xn[:, dc, ts0:ts1],
                     start=(dc == 0), stop=(dc == 7),
                     wait=(p.req(f"{t0}lnq{g // 2}", f"b1wfB{bpar}") +
                           p.bkw(spar, 1))
                     if dc == 0 else ())
            p.mark(f"b1hg_mm{spar}", "pe")
            p.act(hgs[:, hc, :], ph, AF.Gelu,
                  wait=p.req(f"b1hg_mm{spar}") + p.need(f"b1A_mm{spar}"),
                  inc="act")
            p.bkr(spar, 1, "act")
            p.mark(f"b1hg{hc}", "act")

        def sq_part(hc):
            par = hc % 2
            p.tt("gpsimd", sqslots[par], hgs[:, hc, :], hgs[:, hc, :], ALU.mult,
                 wait=p.req(f"b1hg{hc}") + p.need(f"b1sq_used{par}"), inc="gp")
            p.mark(f"b1sq{par}", "gp")

        def stats(hc):
            par = hc % 2
            hslice = hgs[:, hc, :]
            p.mm(pbank(h, 6, 1)[0:1, 0:256], h.ones64[:, 0:1], hslice,
                 start=(hc == 0), stop=(hc == 31),
                 wait=(p.req(f"b1hg{hc}", "consts_gp") +
                       (p.bkw(6, 1) if hc == 0 else [])), inc="pe")
            p.mm(pbank(h, 7, 1)[0:1, 0:256], h.ones64[:, 0:1], sqslots[par],
                 start=(hc == 0), stop=(hc == 31),
                 wait=(p.req(f"b1sq{par}") + (p.bkw(7, 1) if hc == 0 else [])),
                 inc="pe")
            p.mark(f"b1sq_used{par}", "pe")
            p.mark(f"b1stat{hc}", "pe")

        def amm(hc):
            spar = hc % 2
            if hc < 16:
                wsrc = w2r[:, hc, :]
                ww = p.req("b1w2r", "b1w2rS")
            else:
                b2_ = (hc - 16) // 2
                bpar = b2_ % 2
                wsrc = w2v4s[bpar][:, hc % 2, :]
                ww = p.req(f"b1w2B{bpar}")
            for t2 in range(2):
                for dh in range(2):
                    p.mm(pbank(h, 2 + 2 * t2 + dh, 1),
                         hgs[:, hc, t2 * 128:(t2 + 1) * 128],
                         wsrc[:, dh * 512:(dh + 1) * 512],
                         start=(hc == 0), stop=False,
                         wait=(p.req(f"b1hg{hc}") + ww +
                               (p.bkw(2 + 2 * t2 + dh, 1)
                                if hc == 0 else []))
                         if (t2 == 0 and dh == 0) or hc == 0 else ())
            p.mark(f"b1A_mm{spar}", "pe")
            if hc >= 16 and hc % 2 == 1:
                p.mark(f"b1A_mmB{((hc - 16) // 2) % 2}", "pe")

        for hc in range(32):
            if hc % 4 == 0:
                hc4 = hc // 4
                bpar = hc4 % 2
                wsv4 = a3(a3(wst, 2, 4096)[:, bpar, :], 8, 512)
                p.dma("sync", wsv4, wfq[:, :, hc4 * 512:(hc4 + 1) * 512],
                      wait=p.need(f"b1hg_mmB{bpar}") +
                      (p.req("qfc_st0", "qfc_st1")
                       if (g == 0 and hc4 < 2) else []))
                p.mark(f"b1wfB{bpar}", "dsp")
            hg_gelu(hc)
            if hc % 4 == 3:
                p.mark(f"b1hg_mmB{(hc // 4) % 2}", "pe")
            if hc >= 1:
                sq_part(hc - 1)
            if hc >= 2:
                stats(hc - 2)
            if hc >= 3:
                amm(hc - 3)
            if hc >= 16 and hc % 2 == 0:
                b2_ = (hc - 16) // 2
                bpar = b2_ % 2
                p.dma("gpsimd", w2v4s[bpar],
                      w2q3[:, hc:hc + 2, :],
                      wait=p.need(f"b1A_mmB{bpar}") +
                      (b0gate if (g == 0 and b2_ < 2) else []) +
                      (p.req("qw2_st0", "qw2_st1")
                       if (g == 0 and b2_ < 2 and seq == 0) else []))
                p.mark(f"b1w2B{bpar}", "dgp")
        sq_part(31)
        stats(30)
        stats(31)
        amm(29)
        amm(30)
        amm(31)
        mrow = h.f32b[0:1, 0:256]
        vrow_ = h.f32b[0:1, 256:512]
        rrow_ = h.f32b[0:1, 512:768]
        p.ts("vector", mrow, pbank(h, 6, 1)[0:1, 0:256], 1.0 / 4096.0, None,
             ALU.mult, wait=p.req("b1stat31") + p.need("b1f32b_free"),
             inc="dve")
        p.bkr(6, 1, "dve")
        p.ts("vector", vrow_, pbank(h, 7, 1)[0:1, 0:256], 1.0 / 4096.0, LN_EPS,
             ALU.mult, ALU.add, inc="dve")
        p.bkr(7, 1, "dve")
        p.tt("vector", rrow_, mrow, mrow, ALU.mult, inc="dve")
        p.tt("vector", vrow_, vrow_, rrow_, ALU.subtract, inc="dve")
        p.mark("b1veps", "dve")
        p.act(rrow_, vrow_, AF.Ln, wait=p.req("b1veps"), inc="act")
        p.act(rrow_, rrow_, AF.Exp, scale=-0.5, inc="act")
        p.mark("b1r", "act")
        p.ts("vector", nmub, mrow, -1.0, None, ALU.mult, wait=p.req("b1r"),
             inc="dve")
        p.mark("b1nmu", "dve")
        for t2 in range(2):
            p.dma("sync", st[:, 140 + t2:141 + t2],
                  rrow_[0:1, t2 * 128:(t2 + 1) * 128],
                  wait=p.req("b1r") + p.need("b1rcols_used"))
        p.mark("b1rcols", "dsp")
        for t2 in range(2):
            for dh in range(2):
                p.mm(pbank(h, 2 + 2 * t2 + dh, 1),
                     nmub[0:1, t2 * 128:(t2 + 1) * 128],
                     urow[0:1, dh * 512:(dh + 1) * 512],
                     start=False, stop=True,
                     wait=p.req("b1nmu", "b1uvld", f"b1A_mm0", f"b1A_mm1")
                     if (t2 == 0 and dh == 0) else (), inc="pe")
        p.mark("b1rank1", "pe")
        for t2 in range(2):
            t = g * 2 + t2
            par = t2
            f32t = h.f32a
            for dh in range(2):
                p.ts("vector", f32t[:, dh * 512:(dh + 1) * 512],
                     pbank(h, 2 + 2 * t2 + dh, 1), st[:, 140 + t2:141 + t2],
                     None, ALU.mult,
                     wait=p.req("b1rank1", "b1rcols") + p.need("b1yfree"),
                     inc="dve")
                p.bkr(2 + 2 * t2 + dh, 1, "dve")
            p.mark("b1rcols_used", "dve")
            p.tt("vector", f32t[:, :], f32t[:, :], vbc[:, :], ALU.add,
                 wait=p.req("b1vbc"), inc="dve")
            p.mark("b1y", "dve")
            emit_postnorm_tile(p, h, f"{t0}e", t, par, f32t, p.req("b1y"),
                               tail_cb=tail_cb)
            p.mark("b1yfree", "dve")
        p.mark("b1A_done", "dve")
        p.mark("b1f32b_free", "dve")
    arn.free_("xn", "b1hg", "b1wfq", "b1w2r", "b1w2s", "b1vbc", "b1rows",
              "b1rows2")
    free_elt(h)
    p.mark("blkend", "dve")




def emit_all(p, h, blocks):
    emit_load_consts(p, h)
    arn_ap = h.arn.ap
    for seq in range(NSEQ):
        emit_load_stream(p, h, seq)
        cb1 = cb2 = cb3 = None
        if 0 in blocks:
            if 1 in blocks:
                cb1 = emit_prenorm_stats(p, h, f"b1_{seq}ln",
                                         arn_ap[:, 24832:25856])
            emit_l0_attn(p, h, seq, tail_cb=cb1)
        if 1 in blocks:
            if 2 in blocks:
                cb2 = emit_prenorm_stats(p, h, f"b2_{seq}ln",
                                         arn_ap[:, 24832:25856])
            emit_l0_ffn(p, h, seq, tail_cb=cb2, prelude=cb1 is not None)
        if 2 in blocks:
            if 3 in blocks:
                cb3 = emit_prenorm_stats(p, h, f"b3_{seq}ln",
                                         arn_ap[:, 30976:32000])
            emit_l1_attn(p, h, seq, tail_cb=cb3, prelude=cb2 is not None)
        if 3 in blocks:
            emit_l1_ffn(p, h, seq, prelude=cb3 is not None)
        emit_store_stream(p, h, seq)
    p.do("sync", lambda e: e.wait_ge(p.sems["dsp"], p.c["dsp"]))
    p.do("sync", lambda e: e.wait_ge(p.sems["dgp"], p.c["dgp"]))
    p.do("gpsimd", lambda e: e.wait_ge(p.sems["dsp"], p.c["dsp"]))
    p.do("gpsimd", lambda e: e.wait_ge(p.sems["dgp"], p.c["dgp"]))


def build_program(blocks):
    nc = bass.Bass("TRN2", target_bir_lowering=False)
    with ExitStack() as ctx:
        h = build_handles(nc, ctx, blocks)
        sems = {}
        for name in ("pe", "act", "dve", "gp", "dsp", "dgp"):
            sems[name] = ctx.enter_context(nc.semaphore(name))
        block = ctx.enter_context(nc.Block())
        counts = {}

        def run(engname):
            p = Prog(nc, engname, sems)
            h.arn.reset()
            emit_all(p, h, blocks)
            if counts:
                assert counts == p.c, f"sem divergence {engname}: {counts} vs {p.c}"
            else:
                counts.update(p.c)

        block.gpsimd(lambda g: run("gpsimd"))
        block.scalar(lambda s: run("scalar"))
        block.tensor(lambda t: run("tensor"))
        block.vector(lambda v: run("vector"))
        block.sync(lambda sy: run("sync"))
    return nc


def kernel(**inputs):
    prep = host_prep(inputs)
    consts = host_consts()
    x = np.ascontiguousarray(inputs["L0_attn_x"], dtype=np.float32)
    for group in FUSE:
        nc = build_program(group)
        declared = set()
        for alloc in nc.m.functions[0].allocations:
            if getattr(alloc, "kind", None) == "ExternalInput":
                if alloc.memorylocations:
                    declared.add(alloc.memorylocations[0].name)
        base = {}
        base.update({k: v for k, v in prep.items() if k in declared})
        base.update({k: v for k, v in consts.items() if k in declared})
        in_maps = []
        for c in range(NCORE):
            m = dict(base)
            m["x_in"] = np.ascontiguousarray(x[2 * c:2 * c + 2])
            in_maps.append(m)
        res = run_bass_kernel_spmd(nc, in_maps, core_ids=list(range(NCORE)))
        x = np.concatenate([r["y_out"] for r in res.results], axis=0)
    return x.reshape(16, S, D)
